# version 05
# baseline (speedup 1.0000x reference)
import math
import os
from contextlib import ExitStack

import numpy as np
import ml_dtypes

import concourse.bass as bass
import concourse.mybir as mybir
from concourse.alu_op_type import AluOpType as OP
from concourse.bass_utils import run_bass_kernel_spmd

F32 = mybir.dt.float32
BF16 = mybir.dt.bfloat16
I32 = mybir.dt.int32
AF = mybir.ActivationFunctionType
AX = mybir.AxisListType

B, S, E = 2, 8192, 1024
H, D = 16, 64
WL = 16
ROWS = 2048
NRT = ROWS // 128
KERR = 2.7e-18 * 1e6
EPS = 1e-5
FF = 4096
NT = S // 128
NQ = 256
QCN = ROWS // NQ
TWO_PI = 2.0 * math.pi
ENGS = ("sync", "vector", "tensor", "scalar", "gpsimd")


class Sem:
    def __init__(self, handle):
        self.h = handle
        self.n = 0


class Prog:
    def __init__(self, nc):
        self.nc = nc
        self.q = {k: [] for k in ENGS}

    def on(self, eng, fn):
        self.q[eng].append(fn)

    def wait(self, eng, sem, val):
        if val > 0:
            self.q[eng].append(lambda e, s=sem.h, v=val: e.wait_ge(s, v))

    def emit(self):
        q = self.q
        with self.nc.Block() as blk:
            @blk.sync
            def _(e):
                for fn in q["sync"]:
                    fn(e)

            @blk.vector
            def _(e):
                for fn in q["vector"]:
                    fn(e)

            @blk.tensor
            def _(e):
                for fn in q["tensor"]:
                    fn(e)

            @blk.scalar
            def _(e):
                for fn in q["scalar"]:
                    fn(e)

            @blk.gpsimd
            def _(e):
                for fn in q["gpsimd"]:
                    fn(e)


def build(nphase=6):
    nc = bass.Bass("TRN2", target_bir_lowering=False)

    dri = lambda nm, shp, dt: nc.dram_tensor(nm, shp, dt, kind="ExternalInput").ap()
    xb = dri("xb", [S, E], F32)
    wq_a, wk_a, wv_a, wo_a = (dri(n, [E, E], F32) for n in ("wq_a", "wk_a", "wv_a", "wo_a"))
    bq_a, bk_a, bv_a, bo_a = (dri(n, [E], F32) for n in ("bq_a", "bk_a", "bv_a", "bo_a"))
    pmt = dri("pmt", [H, WL], F32)
    iw = dri("iw", [WL], F32)
    f_g, f_b = dri("f_g", [E], F32), dri("f_b", [E], F32)
    w1q = dri("w1q", [E, FF], I32)
    s1 = dri("s1", [8, FF], F32)
    fb1 = dri("fb1", [FF], F32)
    w2q = dri("w2q", [FF, E], I32)
    s2 = dri("s2", [32, E], F32)
    fb2 = dri("fb2", [E], F32)
    wq_b, wk_b0, wk_b1, wv_b0, wv_b1, wo_b = (
        dri(n, [E, E], F32) for n in ("wq_b", "wk_b0", "wk_b1", "wv_b0", "wv_b1", "wo_b"))
    bq_b, bk_b0, bk_b1, bv_b0, bv_b1, bo_b = (
        dri(n, [E], F32) for n in ("bq_b", "bk_b0", "bk_b1", "bv_b0", "bv_b1", "bo_b"))
    lng, lnb = dri("lng", [E], F32), dri("lnb", [E], F32)
    idf = dri("idf", [128, 128], F32)
    out = nc.dram_tensor("out", [ROWS, E], F32, kind="ExternalOutput").ap()

    kts = nc.dram_tensor("kts", [H, D, S], BF16).ap()
    vts = nc.dram_tensor("vts", [H, D, S], BF16).ap()
    qts = nc.dram_tensor("qts", [H, D, ROWS], BF16).ap()
    accs = nc.dram_tensor("accs", [E, ROWS], BF16).ap()
    flatr = nc.dram_tensor("flatr", [WL * H], F32).ap()
    w1ds = nc.dram_tensor("w1ds", [32, 128, 8, 128], BF16).ap()
    w2ds = nc.dram_tensor("w2ds", [8, 128, 32, 128], BF16).ap()
    ksr = nc.dram_tensor("ksr", [2, ROWS, E], BF16).ap()
    vsr = nc.dram_tensor("vsr", [2, ROWS, E], BF16).ap()
    qtb = nc.dram_tensor("qtb", [H, D, ROWS], BF16).ap()
    cin_t = nc.dram_tensor("cin_st", [128, E], F32)
    cout_t = nc.dram_tensor("cout_st", [128, E], F32)

    outer = ExitStack()
    names = ["dS", "dG", "c0", "c2", "c3", "c4", "c5",
             "eST", "eEV", "eXP", "eAQ", "eFN", "rRQ", "rBC", "rTM",
             "pVT", "vSB", "cAC", "pPJ", "pTR",
             "dx0", "dx1", "dx2", "dx3", "ds0", "ds1", "ds2", "ds3",
             "dk0", "dk1", "dn0", "dn1", "dq0", "dq1", "dw0", "dw1", "drb",
             "bS", "bV", "bT", "bA", "bG"]
    sm = {n: Sem(outer.enter_context(nc.semaphore(n))) for n in names}
    dS, dG = sm["dS"], sm["dG"]
    DX = [sm[f"dx{i}"] for i in range(4)]
    DS4 = [sm[f"ds{i}"] for i in range(4)]
    DK = [sm[f"dk{i}"] for i in range(2)]
    DN = [sm[f"dn{i}"] for i in range(2)]
    DQ = [sm[f"dq{i}"] for i in range(2)]
    DW = [sm[f"dw{i}"] for i in range(2)]
    DRB = sm["drb"]
    SYNC_QS = [dS] + DX + DS4 + DK + DN + DQ + DW + [DRB]
    BARMAP = dict(zip(ENGS, [sm["bS"], sm["bV"], sm["bT"], sm["bA"], sm["bG"]]))
    PH = [0]

    def barrier(P):
        PH[0] += 1
        tgt = PH[0]
        for q in SYNC_QS:
            P.wait("sync", q, q.n)
        P.wait("gpsimd", dG, dG.n)
        for en in ENGS:
            b = BARMAP[en]
            b.n += 1
            P.on(en, lambda e, _b=b: e.sem_inc(_b.h, 1))
        for en in ENGS:
            for b in BARMAP.values():
                P.wait(en, b, tgt)

    def _dma(P, eng, dst, src, q=None, slow=False):
        q = q if q is not None else (dG if eng == "gpsimd" else dS)
        q.n += 16
        P.on(eng, lambda e, d=dst, s=src, h=q.h, sl=slow: e.dma_start(
            out=d, in_=s, allow_slow_non_contiguous=sl).then_inc(h, 16))
        return q.n

    ones_f = outer.enter_context(nc.sbuf_tensor("ones_f", [1, 128], F32))
    ones_b = outer.enter_context(nc.sbuf_tensor("ones_b", [64, 1], BF16))
    idf_sb = outer.enter_context(nc.sbuf_tensor("idf_sb", [128, 128], F32))
    idb_sb = outer.enter_context(nc.sbuf_tensor("idb_sb", [128, 128], BF16))
    epsc = outer.enter_context(nc.sbuf_tensor("epsc", [128, 1], F32))
    bcols = outer.enter_context(nc.sbuf_tensor("bcols", [128, 8 * 11], F32))
    BCN = ["bq_a", "bk_a", "bv_a", "bo_a", "f_g", "f_b", "fb2h", "bo_b", "lng", "lnb", "bq_b"]
    BC = {n: i * 8 for i, n in enumerate(BCN)}
    fb1c = outer.enter_context(nc.sbuf_tensor("fb1c", [128, 32], F32))
    fb2c = outer.enter_context(nc.sbuf_tensor("fb2c", [128, 8], F32))
    bkvc = outer.enter_context(nc.sbuf_tensor("bkvc", [128, 5 * 8], F32))
    stackA = ExitStack()
    sccol = stackA.enter_context(nc.sbuf_tensor("sccol", [128, 512], F32))
    iwd = stackA.enter_context(nc.sbuf_tensor("iwd", [1, WL], F32))
    misc = stackA.enter_context(nc.sbuf_tensor("misc", [1, 16], F32))
    arow = stackA.enter_context(nc.sbuf_tensor("arow", [1, 512], F32))
    ncur = stackA.enter_context(nc.sbuf_tensor("ncur", [1, 4], F32))

    P = Prog(nc)
    _dma(P, "sync", idf_sb[:], idf)
    for nm, src in [("bq_a", bq_a), ("bk_a", bk_a), ("bv_a", bv_a), ("bo_a", bo_a),
                    ("f_g", f_g), ("f_b", f_b), ("fb2h", fb2), ("bo_b", bo_b),
                    ("lng", lng), ("lnb", lnb), ("bq_b", bq_b)]:
        _dma(P, "sync", bcols[:, BC[nm]:BC[nm] + 8], src.rearrange("(c p) -> p c", p=128), slow=True)
    _dma(P, "sync", fb1c[:], fb1.rearrange("(c p) -> p c", p=128), slow=True)
    for i, src in enumerate([bq_b, bk_b0, bk_b1, bv_b0, bv_b1]):
        _dma(P, "sync", bkvc[:, i * 8:(i + 1) * 8], src.rearrange("(c p) -> p c", p=128), slow=True)
    _dma(P, "sync", iwd[:], iw.unsqueeze(0))
    P.wait("vector", dS, dS.n)
    P.on("vector", lambda v: v.memset(ones_f[:], 1.0))
    P.on("vector", lambda v: v.memset(ones_b[:], 1.0))
    P.on("vector", lambda v: v.memset(epsc[:], EPS))
    P.on("vector", lambda v: v.memset(ncur[:], 0.0))
    P.on("vector", lambda v: v.tensor_copy(out=idb_sb[:], in_=idf_sb[:]))
    P.on("vector", lambda v: v.tensor_scalar(out=iwd[:], in0=iwd[:], scalar1=1.0 / 16.0,
                                             scalar2=None, op0=OP.mult))
    P.on("vector", lambda v: v.tensor_scalar(out=fb2c[:], in0=bcols[:, BC["fb2h"]:BC["fb2h"] + 8],
                                             scalar1=0.5, scalar2=None, op0=OP.mult))
    barrier(P)
    P.emit()

    def emit_ln(P, mcol, src, center_dst, sqbuf, c2, c3, c4):
        c2.n += 1
        P.on("vector", lambda v: v.reduce_sum(out=mcol[:, 0:1], in_=src, axis=AX.X,
                                              negate=True).then_inc(c2.h, 1))
        P.wait("vector", c2, c2.n)
        c2.n += 1
        P.on("vector", lambda v: v.tensor_scalar(out=mcol[:, 0:1], in0=mcol[:, 0:1],
                                                 scalar1=1.0 / E, scalar2=None,
                                                 op0=OP.mult).then_inc(c2.h, 1))
        P.wait("vector", c2, c2.n)
        c2.n += 1
        P.on("vector", lambda v: v.tensor_scalar(out=center_dst, in0=src,
                                                 scalar1=mcol[:, 0:1], scalar2=None,
                                                 op0=OP.add).then_inc(c2.h, 1))
        P.wait("vector", c2, c2.n)
        c2.n += 1
        P.on("vector", lambda v: v.tensor_tensor(out=sqbuf, in0=center_dst, in1=center_dst,
                                                 op=OP.mult).then_inc(c2.h, 1))
        P.wait("vector", c2, c2.n)
        c3.n += 1
        P.on("vector", lambda v: v.reduce_sum(out=mcol[:, 1:2], in_=sqbuf,
                                              axis=AX.X).then_inc(c3.h, 1))
        P.wait("scalar", c3, c3.n)
        c4.n += 1
        P.on("scalar", lambda s: s.activation(out=mcol[:, 2:3], in_=mcol[:, 1:2], func=AF.Ln,
                                              scale=1.0 / E, bias=epsc[:]).then_inc(c4.h, 1))
        P.wait("scalar", c4, c4.n)
        c4.n += 1
        P.on("scalar", lambda s: s.activation(out=mcol[:, 3:4], in_=mcol[:, 2:3], func=AF.Exp,
                                              scale=-0.5).then_inc(c4.h, 1))
        P.wait("vector", c4, c4.n)

    with ExitStack() as st:
        en = st.enter_context
        wqs = en(nc.sbuf_tensor("wqs", [128, 8 * E], BF16))
        wks = en(nc.sbuf_tensor("wks", [128, 8 * E], BF16))
        wvs = en(nc.sbuf_tensor("wvs", [128, 8 * E], BF16))
        xt = [en(nc.sbuf_tensor(f"xt{i}", [128, E], F32)) for i in range(4)]
        sqb = en(nc.sbuf_tensor("sqb", [128, E], F32))
        xlb = [en(nc.sbuf_tensor(f"xlb{i}", [128, E], BF16)) for i in range(2)]
        xlnt = en(nc.sbuf_tensor("xlnt", [128, 8 * 512], BF16))
        stg = [en(nc.sbuf_tensor(f"stg{i}", [128, 512], BF16)) for i in range(4)]
        mcol = en(nc.sbuf_tensor("mcolA", [128, 4], F32))
        ptrp = [en(nc.psum_tensor(f"ptrp{i}", [128, 512], F32)) for i in range(2)]
        ppp = [en(nc.psum_tensor(f"pppA{i}", [128, 512], F32)) for i in range(2)]

        P = Prog(nc)
        c0, c2, c3, c4, c5 = (sm[x] for x in ("c0", "c2", "c3", "c4", "c5"))
        pTR, pPJ, eST, eEV = sm["pTR"], sm["pPJ"], sm["eST"], sm["eEV"]
        b0 = {x: sm[x].n for x in ("c0", "eST", "eEV", "pTR", "pPJ", "c5")}

        for wsb, wsrc in [(wqs, wq_a), (wks, wk_a), (wvs, wv_a)]:
            _dma(P, "gpsimd", wsb[:].rearrange("p (c n) -> p c n", n=E), wsrc.rearrange("(c p) n -> p c n", p=128))
        wload = dG.n

        stg_dma = {}
        chunk_est = {}
        nstg = 0
        for ch in range(16):
            own = ch < 4
            for r in range(4):
                i = ch * 4 + r
                if i >= 4:
                    P.wait("sync", c0, b0["c0"] + i - 3)
                dtot = _dma(P, "sync", xt[i % 4][:], xb[i * 128:(i + 1) * 128, :],
                            q=DX[i % 4])
                x = xt[i % 4][:]
                xl = xlb[i % 2]
                P.wait("vector", DX[i % 4], dtot)
                if i >= 2:
                    P.wait("vector", pTR, b0["pTR"] + (i - 1) * 8)
                emit_ln(P, mcol, x, x, sqb[:], c2, c3, c4)
                c0.n += 1
                P.on("vector", lambda v, x=x, xl=xl: v.tensor_scalar(
                    out=xl[:], in0=x, scalar1=mcol[:, 3:4], scalar2=None,
                    op0=OP.mult).then_inc(c0.h, 1))
                P.wait("tensor", c0, c0.n)
                if r == 0:
                    pass
                for fc in range(8):
                    gt = i * 8 + fc
                    pj = ptrp[gt % 2]
                    if gt >= 2:
                        P.wait("tensor", c5, b0["c5"] + gt - 1)
                    pTR.n += 1
                    P.on("tensor", lambda t, xl=xl, fc=fc, pj=pj: t.transpose(
                        out=pj.bitcast(BF16)[:, 0:128], in_=xl[:, fc * 128:(fc + 1) * 128],
                        identity=idb_sb[:]).then_inc(pTR.h, 1))
                    P.wait("vector", pTR, pTR.n)
                    if fc == 0 and r == 0 and ch > 0:
                        P.wait("vector", eST, chunk_est[ch - 1])
                    c5.n += 1
                    P.on("vector", lambda v, pj=pj, fc=fc, r=r: v.tensor_copy(
                        out=xlnt[:, fc * 512 + r * 128: fc * 512 + r * 128 + 128],
                        in_=pj.bitcast(BF16)[:, 0:128]).then_inc(c5.h, 1))
            xlnt_ready = c5.n
            P.wait("tensor", c5, xlnt_ready)
            if ch == 0:
                P.wait("tensor", dG, wload)
            groups = [("K", wks, BC["bk_a"], kts), ("V", wvs, BC["bv_a"], vts)]
            if own:
                groups.append(("Q", wqs, BC["bq_a"], qts))
            for gn, wsb, bcoff, dstt in groups:
                for dp in range(8):
                    jj = nstg
                    pj = ppp[jj % 2]
                    sgt = stg[jj % 4]
                    if jj >= 2:
                        P.wait("tensor", eEV, b0["eEV"] + jj - 1)
                    for fc in range(8):
                        if fc == 7:
                            eST.n += 1
                        P.on("tensor", lambda t, wsb=wsb, fc=fc, dp=dp, pj=pj: (
                            t.matmul(out=pj[:],
                                     lhsT=wsb[:, fc * E + dp * 128: fc * E + dp * 128 + 128],
                                     rhs=xlnt[:, fc * 512:(fc + 1) * 512],
                                     start=(fc == 0), stop=(fc == 7)).then_inc(eST.h, 1)
                            if fc == 7 else
                            t.matmul(out=pj[:],
                                     lhsT=wsb[:, fc * E + dp * 128: fc * E + dp * 128 + 128],
                                     rhs=xlnt[:, fc * 512:(fc + 1) * 512],
                                     start=(fc == 0), stop=(fc == 7))))
                    P.wait("scalar", eST, eST.n)
                    if jj >= 4:
                        P.wait("scalar", DS4[jj % 4], stg_dma[jj - 4])
                    eEV.n += 1
                    P.on("scalar", lambda s, pj=pj, sgt=sgt, bcoff=bcoff, dp=dp: s.activation(
                        out=sgt[:], in_=pj[:], func=AF.Identity,
                        bias=bcols[:, bcoff + dp: bcoff + dp + 1]).then_inc(eEV.h, 1))
                    P.wait("sync", eEV, eEV.n)
                    base = ch * 512
                    _dma(P, "sync", dstt[2 * dp, :, base:base + 512], sgt[0:64, :],
                         q=DS4[jj % 4])
                    stg_dma[jj] = _dma(P, "sync", dstt[2 * dp + 1, :, base:base + 512],
                                       sgt[64:128, :], q=DS4[jj % 4])
                    nstg += 1
            chunk_est[ch] = eST.n
        barrier(P)
        P.emit()

    with ExitStack() as st:
        en = st.enter_context
        nbuf = en(nc.sbuf_tensor("nbuf", [64, S], BF16))
        nps = en(nc.sbuf_tensor("nps", [1, 512], F32))
        pm_sb = en(nc.sbuf_tensor("pm_sb", [H, WL], F32))
        t16 = en(nc.sbuf_tensor("t16", [H, WL], F32))
        t16b = en(nc.sbuf_tensor("t16b", [H, WL], F32))
        pnp = [en(nc.psum_tensor(f"pnp{i}", [128, 512], F32)) for i in range(2)]
        psc = en(nc.psum_tensor("psc", [128, 512], F32))

        P = Prog(nc)
        c2, c3 = sm["c2"], sm["c3"]
        eST = sm["eST"]

        def vstep(fn):
            c2.n += 1
            P.on("vector", lambda v, f=fn: f(v).then_inc(c2.h, 1))
            P.wait("vector", c2, c2.n)

        def sstep(fn):
            P.wait("scalar", c2, c2.n)
            c3.n += 1
            P.on("scalar", lambda s, f=fn: f(s).then_inc(c3.h, 1))
            P.wait("vector", c3, c3.n)

        for src, ncol, ncols in [(kts, 1, S), (qts, 0, ROWS)]:
            for h in range(H):
                P.wait("sync", c2, c2.n)
                _dma(P, "sync", nbuf[:, 0:ncols], src[h])
                P.wait("vector", dS, dS.n)
                vstep(lambda v, n=ncols: v.tensor_tensor(
                    out=nbuf[:, 0:n], in0=nbuf[:, 0:n], in1=nbuf[:, 0:n], op=OP.mult))
                for j in range(ncols // 512):
                    pj = pnp[j % 2]
                    P.wait("tensor", c2, c2.n)
                    eST.n += 1
                    P.on("tensor", lambda t, pj=pj, j=j: t.matmul(
                        out=pj[0:1, :], lhsT=ones_b[:], rhs=nbuf[:, j * 512:(j + 1) * 512],
                        start=True, stop=True).then_inc(eST.h, 1))
                    P.wait("vector", eST, eST.n)
                    vstep(lambda v, pj=pj: v.tensor_copy(out=nps[:], in_=pj[0:1, :]))
                    vstep(lambda v: v.reduce_max(out=nps[0:1, 0:1], in_=nps[:], axis=AX.X))
                    vstep(lambda v, c=ncol: v.tensor_tensor(
                        out=ncur[0:1, c:c + 1], in0=ncur[0:1, c:c + 1],
                        in1=nps[0:1, 0:1], op=OP.max))

        P.wait("sync", c2, c2.n)
        _dma(P, "sync", pm_sb[:], pmt)
        P.wait("vector", dS, dS.n)
        MAGIC = 12582912.0
        vstep(lambda v: v.tensor_scalar(out=t16[:], in0=pm_sb[:], scalar1=1.0 / TWO_PI,
                                        scalar2=MAGIC, op0=OP.mult, op1=OP.add))
        vstep(lambda v: v.tensor_scalar(out=t16[:], in0=t16[:], scalar1=-MAGIC,
                                        scalar2=None, op0=OP.add))
        vstep(lambda v: v.scalar_tensor_tensor(out=t16[:], in0=t16[:], scalar=-TWO_PI,
                                               in1=pm_sb[:], op0=OP.mult, op1=OP.add))
        sstep(lambda s: s.activation(out=t16b[:], in_=t16[:], func=AF.Sin, scale=0.5))
        vstep(lambda v: v.tensor_tensor(out=t16[:], in0=t16b[:], in1=t16b[:], op=OP.mult))
        vstep(lambda v: v.tensor_scalar(out=t16[:], in0=t16[:], scalar1=-2.0, scalar2=1.0,
                                        op0=OP.mult, op1=OP.add))
        vstep(lambda v: v.tensor_tensor(out=t16b[:], in0=t16[:], in1=t16[:], op=OP.mult))
        vstep(lambda v: v.tensor_scalar(out=t16b[:], in0=t16b[:], scalar1=0.125,
                                        scalar2=None, op0=OP.mult))
        P.wait("sync", c2, c2.n)
        _dma(P, "sync", flatr.rearrange("(h c) -> h c", c=WL), t16b[:])
        P.wait("sync", dS, dS.n)
        _dma(P, "sync", arow[0:1, 0:256], flatr.unsqueeze(0))
        P.wait("vector", dS, dS.n)
        vstep(lambda v: v.reduce_max(out=misc[0:1, 4:5], in_=arow[0:1, 0:256], axis=AX.X))
        vstep(lambda v: v.tensor_tensor(out=misc[0:1, 5:6], in0=ncur[0:1, 0:1],
                                        in1=ncur[0:1, 1:2], op=OP.mult))
        sstep(lambda s: s.activation(out=misc[0:1, 6:7], in_=misc[0:1, 5:6], func=AF.Ln))
        P.wait("scalar", c3, c3.n)
        c3.n += 1
        P.on("scalar", lambda s: s.activation(out=misc[0:1, 2:3], in_=misc[0:1, 6:7],
                                              func=AF.Exp, scale=0.5).then_inc(c3.h, 1))
        P.wait("vector", c3, c3.n)
        vstep(lambda v: v.tensor_tensor(out=misc[0:1, 7:8], in0=misc[0:1, 4:5],
                                        in1=misc[0:1, 2:3], op=OP.mult))
        vstep(lambda v: v.tensor_tensor(out=misc[0:1, 8:9], in0=misc[0:1, 7:8],
                                        in1=misc[0:1, 7:8], op=OP.mult))
        sstep(lambda s: s.activation(out=misc[0:1, 9:10], in_=misc[0:1, 8:9], func=AF.Sin,
                                     scale=KERR / 2.0))
        vstep(lambda v: v.tensor_tensor(out=misc[0:1, 10:11], in0=misc[0:1, 9:10],
                                        in1=misc[0:1, 9:10], op=OP.mult))
        vstep(lambda v: v.tensor_scalar(out=misc[0:1, 3:4], in0=misc[0:1, 10:11],
                                        scalar1=-2.0, scalar2=1.0, op0=OP.mult, op1=OP.add))
        vstep(lambda v: v.tensor_scalar(out=arow[0:1, 0:256], in0=arow[0:1, 0:256],
                                        scalar1=misc[0:1, 3:4], scalar2=None, op0=OP.mult))
        vstep(lambda v: v.tensor_scalar(out=arow[0:1, 256:512], in0=arow[0:1, 0:256],
                                        scalar1=misc[0:1, 2:3], scalar2=-1.0,
                                        op0=OP.mult, op1=OP.mult))
        P.wait("tensor", c2, c2.n)
        eST.n += 1
        P.on("tensor", lambda t: t.matmul(out=psc[:], lhsT=ones_f[:], rhs=arow[:],
                                          start=True, stop=True).then_inc(eST.h, 1))
        P.wait("vector", eST, eST.n)
        vstep(lambda v: v.tensor_copy(out=sccol[:], in_=psc[:]))
        barrier(P)
        P.emit()

    if nphase < 2:
        NH_RUN = 0
    else:
        NH_RUN = int(os.environ.get("MK_NH", str(H)))
    A3CUT = int(os.environ.get("MK_A3CUT", "5"))
    with ExitStack() as st:
        en = st.enter_context
        spr = en(nc.sbuf_tensor("spr", [128, 3 * 8192], F32))
        pbuf = en(nc.sbuf_tensor("pbuf", [128, 2 * 32 * NQ], BF16))
        ktb = [en(nc.sbuf_tensor(f"ktb{i}", [64, S], BF16)) for i in range(2)]
        qtbu = [en(nc.sbuf_tensor(f"qtbu{i}", [64, ROWS], BF16)) for i in range(2)]
        vtb1 = en(nc.sbuf_tensor("vtb0", [128, S // 2], BF16))
        vtb = [vtb1, vtb1]
        vsb = [en(nc.sbuf_tensor(f"vsb{i}", [128, NT * 65], BF16)) for i in range(2)]
        accf = [en(nc.sbuf_tensor(f"accf{i}", [64, NQ], F32)) for i in range(2)]
        rpair = en(nc.sbuf_tensor("rpair", [1, 512], F32))
        osb = en(nc.sbuf_tensor("osb", [64, NQ], F32))
        tmpf = en(nc.sbuf_tensor("tmpf", [64, NQ], F32))
        stp = [en(nc.psum_tensor(f"stp{i}", [128, 512], F32)) for i in range(2)]
        avp = [en(nc.psum_tensor(f"avp{i}", [128, 512], F32)) for i in range(3)]
        rbcp = en(nc.psum_tensor("rbcp", [128, 512], F32))
        vtpp = [en(nc.psum_tensor(f"vtpp{i}", [128, 256], BF16)) for i in range(2)]

        P = Prog(nc)
        eST, eEV, eXP, eAQ, eFN = (sm[x] for x in ("eST", "eEV", "eXP", "eAQ", "eFN"))
        rRQ, rBC, rTM = sm["rRQ"], sm["rBC"], sm["rTM"]
        pVT, vSB, cAC, c2 = sm["pVT"], sm["vSB"], sm["cAC"], sm["c2"]
        A = {x: sm[x].n for x in ("eST", "eEV", "eXP", "eAQ", "eFN", "rRQ", "rBC",
                                  "rTM", "pVT", "vSB", "c2", "cAC")}
        dS_kv = {}
        acc_dma = {}
        vsb_done = {}

        for b_ in range(2):
            P.on("vector", lambda v, b_=b_: v.memset(
                vsb[b_].rearrange("p (t c) -> p t c", c=65)[:, :, 64:65], 1.0))

        for h in range(NH_RUN):
            if h >= 2:
                P.wait("sync", eST, A["eST"] + (h - 1) * 256)
            if h >= 1 and A3CUT >= 3:
                P.wait("sync", pVT, A["pVT"] + h * (NT // 2))
            _dma(P, "sync", ktb[h % 2][:], kts[h], q=DK[h % 2])
            _dma(P, "sync", qtbu[h % 2][:], qts[h], q=DK[h % 2])
            vsrc = vts[h].rearrange("d (u two j) -> d two u j", two=2, j=128)
            _dma(P, "sync", vtb[h % 2][0:64, :].rearrange("d (u j) -> d u j", j=128),
                 vsrc[:, 0, :, :], q=DK[h % 2])
            dS_kv[h] = _dma(P, "sync",
                            vtb[h % 2][64:128, :].rearrange("d (u j) -> d u j", j=128),
                            vsrc[:, 1, :, :], q=DK[h % 2])

        def emit_fills(i):
            h, qc = divmod(i, QCN)
            P.wait("tensor", DK[h % 2], dS_kv[h])
            for f in range(32):
                gf = i * 32 + f
                pj = stp[gf % 2]
                if gf >= 2:
                    P.wait("tensor", eEV, A["eEV"] + gf - 1)
                for tl in range(2):
                    t_ = f * 2 + tl
                    if tl == 1:
                        eST.n += 1
                    P.on("tensor", lambda t, h=h, t_=t_, tl=tl, pj=pj, qc=qc: (
                        t.matmul(out=pj[:, (t_ % 2) * 256:(t_ % 2) * 256 + 256],
                                 lhsT=ktb[h % 2][:, t_ * 128:(t_ + 1) * 128],
                                 rhs=qtbu[h % 2][:, qc * NQ:(qc + 1) * NQ],
                                 start=True, stop=True).then_inc(eST.h, 1)
                        if tl == 1 else
                        t.matmul(out=pj[:, (t_ % 2) * 256:(t_ % 2) * 256 + 256],
                                 lhsT=ktb[h % 2][:, t_ * 128:(t_ + 1) * 128],
                                 rhs=qtbu[h % 2][:, qc * NQ:(qc + 1) * NQ],
                                 start=True, stop=True)))
                P.wait("vector", eST, eST.n)
                g = i * 2 + f // 16
                if g >= 3 and A3CUT >= 2:
                    P.wait("vector", eXP, A["eXP"] + 16 * (g - 2))
                eEV.n += 1
                sg = g % 3
                P.on("vector", lambda v, pj=pj, f=f, sg=sg: v.tensor_copy(
                    out=spr[:, sg * 8192 + (f % 16) * 512: sg * 8192 + (f % 16) * 512 + 512],
                    in_=pj[:]).then_inc(eEV.h, 1))

        def emit_vbuild(h):
            b_ = h % 2
            P.wait("tensor", DK[h % 2], dS_kv[h])
            if h >= 2:
                if A3CUT >= 4:
                    P.wait("tensor", eAQ, A["eAQ"] + (h - 1) * 256)
                if A3CUT >= 5:
                    P.wait("tensor", rTM, A["rTM"] + (h - 1) * 8 * 16)
            for u in range(NT // 2):
                vv = vtpp[u % 2][:, 0:128]
                P.wait("tensor", vSB, vSB.n - 2)
                pVT.n += 1
                P.on("tensor", lambda t, b_=b_, u=u, vv=vv: t.transpose(
                    out=vv, in_=vtb[b_][:, u * 128:(u + 1) * 128],
                    identity=idb_sb[:]).then_inc(pVT.h, 1))
                P.wait("vector", pVT, pVT.n)
                for par in range(2):
                    t_ = 2 * u + par
                    vSB.n += 1
                    P.on("vector", lambda v, b_=b_, t_=t_, vv=vv, par=par: v.tensor_copy(
                        out=vsb[b_][:, t_ * 65: t_ * 65 + 64],
                        in_=vv[:, par * 64:par * 64 + 64]).then_inc(vSB.h, 1))
            vsb_done[h] = vSB.n

        EFN_REQ = {0: -2, 1: -2, 2: 0, 3: 0, 4: -4, 5: -4, 6: 2, 7: 2,
                   8: 4, 9: 4, 10: 6, 11: 6, 12: 8, 13: 8, 14: 10, 15: 10}

        def emit_av(i, ch):
            h, qc = divmod(i, QCN)
            if A3CUT >= 5:
                P.wait("tensor", eFN, A["eFN"] + i * 16 + EFN_REQ[ch])
            if A3CUT >= 3:
                P.wait("tensor", vSB, vsb_done[h])
            for hh in range(2):
                idx = i * 32 + ch * 2 + hh
                slot = idx % 2
                P.wait("tensor", eXP, A["eXP"] + idx + 1)
                for tl in range(32):
                    t_ = hh * 32 + tl
                    if tl == 31:
                        eAQ.n += 1
                    P.on("tensor", lambda t, h=h, ch=ch, t_=t_, tl=tl, slot=slot: (
                        t.matmul(out=avp[(ch // 2) % 3][0:65, (ch % 2) * 256:(ch % 2) * 256 + 256],
                                 lhsT=vsb[h % 2][:, t_ * 65:(t_ + 1) * 65],
                                 rhs=pbuf[:, slot * 8192 + tl * 256: slot * 8192 + tl * 256 + 256],
                                 start=(t_ == 0), stop=(t_ == 63)).then_inc(eAQ.h, 1)
                        if tl == 31 else
                        t.matmul(out=avp[(ch // 2) % 3][0:65, (ch % 2) * 256:(ch % 2) * 256 + 256],
                                 lhsT=vsb[h % 2][:, t_ * 65:(t_ + 1) * 65],
                                 rhs=pbuf[:, slot * 8192 + tl * 256: slot * 8192 + tl * 256 + 256],
                                 start=(t_ == 0), stop=(t_ == 63))))
            if ch % 2 == 1 and ch <= 9:
                emit_rbc(i, ch // 2)

        def emit_rbc(i, p):
            g = i * 8 + p
            P.wait("tensor", rRQ, A["rRQ"] + g + 1)
            P.wait("tensor", rTM, A["rTM"] + 2 * g)
            rBC.n += 1
            P.on("tensor", lambda t: t.matmul(
                out=rbcp[0:64, 0:512], lhsT=ones_f[0:1, 0:64], rhs=rpair[:],
                start=True, stop=True).then_inc(rBC.h, 1))

        def emit_finish(i, p):
            che, cho = 2 * p, 2 * p + 1
            pt = avp[p % 3]
            g = i * 8 + p
            if p == 0 and i > 1:
                P.wait("vector", dG, acc_dma[i - 2])
            P.wait("vector", eAQ, A["eAQ"] + i * 32 + cho * 2 + 2)
            P.wait("vector", rBC, A["rBC"] + g)
            for chl, col in ((che, 0), (cho, 256)):
                c2.n += 1
                P.on("vector", lambda v, pt=pt, col=col: v.reciprocal(
                    out=rpair[0:1, col:col + 256],
                    in_=pt[64:65, col:col + 256]).then_inc(c2.h, 1))
                P.wait("vector", c2, c2.n)
                if chl == cho:
                    rRQ.n += 1
                P.on("vector", lambda v, chl=chl, col=col, last=(chl == cho): (
                    v.tensor_scalar(out=rpair[0:1, col:col + 256],
                                    in0=rpair[0:1, col:col + 256],
                                    scalar1=iwd[0:1, chl:chl + 1], scalar2=None,
                                    op0=OP.mult).then_inc(rRQ.h, 1)
                    if last else
                    v.tensor_scalar(out=rpair[0:1, col:col + 256],
                                    in0=rpair[0:1, col:col + 256],
                                    scalar1=iwd[0:1, chl:chl + 1], scalar2=None,
                                    op0=OP.mult)))
            P.wait("vector", rBC, A["rBC"] + g + 1)
            for chl, col in ((che, 0), (cho, 256)):
                c2.n += 1
                P.on("vector", lambda v, pt=pt, col=col: v.tensor_copy(
                    out=osb[:], in_=pt[0:64, col:col + 256]).then_inc(c2.h, 1))
                P.wait("vector", c2, c2.n)
                rTM.n += 1
                P.on("vector", lambda v, col=col: v.tensor_tensor(
                    out=tmpf[:], in0=osb[:], in1=rbcp[0:64, col:col + 256],
                    op=OP.mult).then_inc(rTM.h, 1))
                P.wait("vector", rTM, rTM.n)
                eFN.n += 1
                if chl == 0:
                    P.on("vector", lambda v, i=i: v.tensor_copy(
                        out=accf[i % 2][:], in_=tmpf[:]).then_inc(eFN.h, 1))
                else:
                    P.on("vector", lambda v, i=i: v.tensor_tensor(
                        out=accf[i % 2][:], in0=accf[i % 2][:], in1=tmpf[:],
                        op=OP.add).then_inc(eFN.h, 1))

        NITEM = NH_RUN * QCN
        if NITEM:
            if A3CUT >= 3:
                emit_vbuild(0)
            emit_fills(0)
        for i in range(NITEM):
            h, qc = divmod(i, QCN)
            for ch in range(16) if A3CUT >= 2 else []:
                for hh in range(2):
                    idx = i * 32 + ch * 2 + hh
                    slot = idx % 2
                    if ch == 0:
                        P.wait("scalar", eEV, A["eEV"] + i * 32 + (hh + 1) * 16)
                    if idx >= 2 and A3CUT >= 4:
                        P.wait("scalar", eAQ, A["eAQ"] + idx - 1)
                    eXP.n += 1
                    hc = h * 16 + ch
                    sg = (i * 2 + hh) % 3
                    P.on("scalar", lambda s, sg=sg, slot=slot, hc=hc: s.activation(
                        out=pbuf[:, slot * 8192:(slot + 1) * 8192],
                        in_=spr[:, sg * 8192:(sg + 1) * 8192], func=AF.Exp,
                        scale=sccol[:, hc:hc + 1],
                        bias=sccol[:, 256 + hc:256 + hc + 1]).then_inc(eXP.h, 1))
            if A3CUT >= 4:
                for ch in range(12):
                    emit_av(i, ch)
                if A3CUT >= 5:
                    for p in range(5):
                        emit_finish(i, p)
                if i + 1 < NITEM:
                    emit_fills(i + 1)
                    if qc == QCN - 1 and A3CUT >= 3:
                        emit_vbuild(h + 1)
                for ch in range(12, 16):
                    emit_av(i, ch)
            else:
                if i + 1 < NITEM:
                    emit_fills(i + 1)
                    if qc == QCN - 1 and A3CUT >= 3:
                        emit_vbuild(h + 1)
            if A3CUT >= 5:
                for p in range(5, 8):
                    emit_rbc(i, p)
                for p in range(5, 8):
                    emit_finish(i, p)
            if A3CUT < 5:
                continue
            P.wait("gpsimd", eFN, eFN.n)
            acc_dma[i] = _dma(P, "gpsimd",
                              accs[h * 64:(h + 1) * 64, qc * NQ:(qc + 1) * NQ],
                              accf[i % 2][:])
        barrier(P)
        P.emit()

    stackA.close()
    xres = outer.enter_context(nc.sbuf_tensor("xres", [128, NRT * E], F32))

    a4_on = nphase >= 3 and NH_RUN == H
    if a4_on:
     with ExitStack() as st:
        en = st.enter_context
        acct = en(nc.sbuf_tensor("acct", [128, 8 * ROWS], BF16))
        wos = en(nc.sbuf_tensor("wosA", [128, 8 * E], BF16))
        otT = en(nc.sbuf_tensor("otT", [128, 8 * ROWS], F32))
        xst = [en(nc.sbuf_tensor(f"xstA{i}", [128, E], F32)) for i in range(2)]
        ppj = [en(nc.psum_tensor(f"ppjA{i}", [128, 512], F32)) for i in range(2)]

        P = Prog(nc)
        c0, c2, c3, c4 = (sm[x] for x in ("c0", "c2", "c3", "c4"))
        eST, eEV, pTR, c5 = sm["eST"], sm["eEV"], sm["pTR"], sm["c5"]
        A4 = {x: sm[x].n for x in ("eST", "eEV", "pTR", "c5", "c0")}

        _dma(P, "gpsimd", wos[:].rearrange("p (c n) -> p c n", n=E), wo_a.rearrange("(c p) n -> p c n", p=128))
        _dma(P, "sync", acct[:].rearrange("p (c n) -> p c n", n=ROWS), accs.rearrange("(c p) n -> p c n", p=128))
        ldT = dS.n
        P.wait("tensor", dS, ldT)
        P.wait("tensor", dG, dG.n)
        nj = 0
        for dp in range(8):
            for rc in range(4):
                pj = ppj[nj % 2]
                if nj >= 2:
                    P.wait("tensor", eEV, A4["eEV"] + nj - 1)
                for fc in range(8):
                    if fc == 7:
                        eST.n += 1
                    P.on("tensor", lambda t, fc=fc, dp=dp, rc=rc, pj=pj: (
                        t.matmul(out=pj[:],
                                 lhsT=wos[:, fc * E + dp * 128: fc * E + dp * 128 + 128],
                                 rhs=acct[:, fc * ROWS + rc * 512: fc * ROWS + rc * 512 + 512],
                                 start=(fc == 0), stop=(fc == 7)).then_inc(eST.h, 1)
                        if fc == 7 else
                        t.matmul(out=pj[:],
                                 lhsT=wos[:, fc * E + dp * 128: fc * E + dp * 128 + 128],
                                 rhs=acct[:, fc * ROWS + rc * 512: fc * ROWS + rc * 512 + 512],
                                 start=(fc == 0), stop=(fc == 7))))
                P.wait("scalar", eST, eST.n)
                eEV.n += 1
                P.on("scalar", lambda s, pj=pj, dp=dp, rc=rc: s.activation(
                    out=otT[:, dp * ROWS + rc * 512: dp * ROWS + rc * 512 + 512],
                    in_=pj[:], func=AF.Identity,
                    bias=bcols[:, BC["bo_a"] + dp: BC["bo_a"] + dp + 1]).then_inc(eEV.h, 1))
                nj += 1
        for rt in range(NRT):
            if rt >= 2:
                P.wait("sync", c0, A4["c0"] + (rt - 1) * 8)
            dtot = _dma(P, "sync", xst[rt % 2][:], xb[rt * 128:(rt + 1) * 128, :],
                        q=DX[rt % 2])
            P.wait("tensor", eEV, eEV.n)
            P.wait("vector", DX[rt % 2], dtot)
            for fc in range(8):
                gt = rt * 8 + fc
                pj = ppj[gt % 2]
                if gt >= 2:
                    P.wait("tensor", c0, A4["c0"] + gt - 1)
                pTR.n += 1
                P.on("tensor", lambda t, fc=fc, rt=rt, pj=pj: t.transpose(
                    out=pj[:, 0:128], in_=otT[:, fc * ROWS + rt * 128: fc * ROWS + rt * 128 + 128],
                    identity=idf_sb[:]).then_inc(pTR.h, 1))
                P.wait("vector", pTR, pTR.n)
                c0.n += 1
                P.on("vector", lambda v, fc=fc, rt=rt, pj=pj: v.tensor_tensor(
                    out=xres[:, rt * E + fc * 128: rt * E + fc * 128 + 128],
                    in0=xst[rt % 2][:, fc * 128:(fc + 1) * 128], in1=pj[:, 0:128],
                    op=OP.add).then_inc(c0.h, 1))
        barrier(P)
        P.emit()

    def emit_dequant():
        with ExitStack() as st:
            en = st.enter_context
            qw = en(nc.sbuf_tensor("qw", [128, FF], I32))
            srow = en(nc.sbuf_tensor("srow", [1, FF], F32))
            scf = en(nc.sbuf_tensor("scf", [128, FF], F32))
            tqf = en(nc.sbuf_tensor("tqf", [128, FF], F32))
            wbb = en(nc.sbuf_tensor("wbb", [128, FF], BF16))
            pbq = [en(nc.psum_tensor(f"pbq{i}", [128, 512], F32)) for i in range(2)]

            P = Prog(nc)
            c2, eST = sm["c2"], sm["eST"]
            for w_src, s_src, wd, nkc, width, nw in [
                    (w1q, s1, w1ds, 8, FF, 32), (w2q, s2, w2ds, 32, E, 8)]:
                for kc in range(nkc):
                    P.wait("sync", dG, dG.n)
                    _dma(P, "sync", qw[:, 0:width], w_src[kc * 128:(kc + 1) * 128, :])
                    _dma(P, "sync", srow[0:1, 0:width], s_src[kc:kc + 1, :])
                    dtot = dS.n
                    P.wait("tensor", dS, dtot)
                    P.wait("tensor", c2, c2.n)
                    base_c2 = c2.n
                    for j in range(width // 512):
                        pj = pbq[j % 2]
                        if j >= 2:
                            P.wait("tensor", c2, base_c2 + j - 1)
                        eST.n += 1
                        P.on("tensor", lambda t, pj=pj, j=j: t.matmul(
                            out=pj[:], lhsT=ones_f[:], rhs=srow[0:1, j * 512:(j + 1) * 512],
                            start=True, stop=True).then_inc(eST.h, 1))
                        P.wait("vector", eST, eST.n)
                        c2.n += 1
                        P.on("vector", lambda v, pj=pj, j=j: v.tensor_copy(
                            out=scf[:, j * 512:(j + 1) * 512], in_=pj[:]).then_inc(c2.h, 1))
                    P.wait("vector", dS, dtot)
                    c2.n += 1
                    P.on("vector", lambda v, width=width: v.tensor_scalar(
                        out=tqf[:, 0:width], in0=qw[:, 0:width], scalar1=-128.0,
                        scalar2=None, op0=OP.add).then_inc(c2.h, 1))
                    P.wait("vector", c2, c2.n)
                    c2.n += 1
                    P.on("vector", lambda v, width=width: v.tensor_tensor(
                        out=wbb[:, 0:width], in0=tqf[:, 0:width], in1=scf[:, 0:width],
                        op=OP.mult).then_inc(c2.h, 1))
                    P.wait("gpsimd", c2, c2.n)
                    for oc in range(nw):
                        _dma(P, "gpsimd", wd[oc, :, kc, :], wbb[:, oc * 128:(oc + 1) * 128])
            barrier(P)
            P.emit()

    def emit_ffn(tag):
        with ExitStack() as st:
            en = st.enter_context
            ybuf = en(nc.sbuf_tensor(f"ybuf{tag}", [128, 4 * E], F32))
            cb = en(nc.sbuf_tensor(f"cb{tag}", [128, E], F32))
            sqb2 = en(nc.sbuf_tensor(f"sqb2{tag}", [128, E], F32))
            hrow = en(nc.sbuf_tensor(f"hrow{tag}", [128, E], BF16))
            hT = en(nc.sbuf_tensor(f"hT{tag}", [128, 8 * 512], BF16))
            y1T = en(nc.sbuf_tensor(f"y1T{tag}", [128, 32 * 512], BF16))
            zt = en(nc.sbuf_tensor(f"zt{tag}", [128, 8 * 512], F32))
            w1sb = [en(nc.sbuf_tensor(f"w1sb{tag}{i}", [128, 8 * 128], BF16)) for i in range(2)]
            w2sb = [en(nc.sbuf_tensor(f"w2sb{tag}{i}", [128, 32 * 128], BF16)) for i in range(2)]
            mcol = en(nc.sbuf_tensor(f"mcolF{tag}", [128, 4], F32))
            ppf = [en(nc.psum_tensor(f"ppf{tag}{i}", [128, 512], F32)) for i in range(2)]

            P = Prog(nc)
            c0, c2, c3, c4, c5 = (sm[x] for x in ("c0", "c2", "c3", "c4", "c5"))
            eST, eEV, pTR, pPJ = sm["eST"], sm["eEV"], sm["pTR"], sm["pPJ"]
            FB = {x: sm[x].n for x in ("eST", "eEV", "pTR", "c5", "c0", "pPJ")}
            w1dma = {}
            w2dma = {}
            hT_ptr = {}

            for rc in range(4):
                for r in range(4):
                    rt = rc * 4 + r
                    xsl = xres[:, rt * E:(rt + 1) * E]
                    ysl = ybuf[:, r * E:(r + 1) * E]
                    emit_ln(P, mcol, xsl, ysl, sqb2[:], c2, c3, c4)
                    c2.n += 1
                    P.on("vector", lambda v, ysl=ysl: v.tensor_scalar(
                        out=ysl, in0=ysl, scalar1=mcol[:, 3:4], scalar2=None,
                        op0=OP.mult).then_inc(c2.h, 1))
                    P.wait("vector", c2, c2.n)
                    c2.n += 1
                    P.on("vector", lambda v, xsl=xsl, ysl=ysl: v.tensor_tensor(
                        out=xsl, in0=xsl, in1=ysl, op=OP.add).then_inc(c2.h, 1))
                    P.wait("vector", c2, c2.n)
                    emit_ln(P, mcol, ysl, cb[:], sqb2[:], c2, c3, c4)
                    gr = rc * 4 + r
                    if gr >= 1:
                        P.wait("vector", pTR, hT_ptr[gr - 1])
                    c0.n += 1
                    P.on("vector", lambda v: v.tensor_scalar(
                        out=hrow[:], in0=cb[:], scalar1=mcol[:, 3:4], scalar2=None,
                        op0=OP.mult).then_inc(c0.h, 1))
                    P.wait("tensor", c0, c0.n)
                    if r == 0 and rc > 0:
                        P.wait("tensor", c0, FB["c0"] + rc * 36)
                    for fc in range(8):
                        gl = r * 8 + fc
                        pj = ppf[fc % 2]
                        if gl >= 2 or r > 0:
                            P.wait("tensor", c5, c5.n - 1)
                        pTR.n += 1
                        P.on("tensor", lambda t, fc=fc, pj=pj: t.transpose(
                            out=pj.bitcast(BF16)[:, 0:128], in_=hrow[:, fc * 128:(fc + 1) * 128],
                            identity=idb_sb[:]).then_inc(pTR.h, 1))
                        P.wait("scalar", pTR, pTR.n)
                        c5.n += 1
                        P.on("scalar", lambda s, pj=pj, fc=fc, r=r: s.activation(
                            out=hT[:, fc * 512 + r * 128: fc * 512 + r * 128 + 128],
                            in_=pj.bitcast(BF16)[:, 0:128], func=AF.Identity,
                            scale=bcols[:, BC["f_g"] + fc: BC["f_g"] + fc + 1],
                            bias=bcols[:, BC["f_b"] + fc: BC["f_b"] + fc + 1]).then_inc(c5.h, 1))
                        hT_ptr[gr] = pTR.n
                P.wait("tensor", c5, c5.n)
                for oc in range(32):
                    wb = w1sb[oc % 2]
                    g1 = rc * 32 + oc
                    if g1 >= 2:
                        rcp, ocp = divmod(g1 - 2, 32)
                        P.wait("sync", eEV, FB["eEV"] + rcp * 40 + ocp + 1)
                    w1dma[(rc, oc)] = _dma(P, "sync", wb[:],
                                           w1ds[oc].rearrange("p c n -> p (c n)"),
                                           q=DW[oc % 2])
                    pj = ppf[oc % 2]
                    P.wait("tensor", DW[oc % 2], w1dma[(rc, oc)])
                    if oc >= 2:
                        P.wait("tensor", eEV, FB["eEV"] + rc * 40 + oc - 1)
                    for kc in range(8):
                        if kc == 7:
                            eST.n += 1
                        P.on("tensor", lambda t, wb=wb, kc=kc, pj=pj: (
                            t.matmul(out=pj[:], lhsT=wb[:, kc * 128:(kc + 1) * 128],
                                     rhs=hT[:, kc * 512:(kc + 1) * 512],
                                     start=(kc == 0), stop=(kc == 7)).then_inc(eST.h, 1)
                            if kc == 7 else
                            t.matmul(out=pj[:], lhsT=wb[:, kc * 128:(kc + 1) * 128],
                                     rhs=hT[:, kc * 512:(kc + 1) * 512],
                                     start=(kc == 0), stop=(kc == 7))))
                    P.wait("scalar", eST, eST.n)
                    eEV.n += 1
                    P.on("scalar", lambda s, pj=pj, oc=oc: s.activation(
                        out=y1T[:, oc * 512:(oc + 1) * 512], in_=pj[:], func=AF.Silu,
                        bias=fb1c[:, oc:oc + 1]).then_inc(eEV.h, 1))
                for oc in range(8):
                    wb = w2sb[oc % 2]
                    g2 = rc * 8 + oc
                    if g2 >= 2:
                        rcp, ocp = divmod(g2 - 2, 8)
                        P.wait("sync", eEV, FB["eEV"] + rcp * 40 + 32 + ocp + 1)
                    w2dma[(rc, oc)] = _dma(P, "sync", wb[:],
                                           w2ds[oc].rearrange("p c n -> p (c n)"),
                                           q=DW[oc % 2])
                    pj = ppf[oc % 2]
                    P.wait("tensor", DW[oc % 2], w2dma[(rc, oc)])
                    if oc >= 2:
                        P.wait("tensor", eEV, FB["eEV"] + rc * 40 + 32 + oc - 1)
                    else:
                        P.wait("tensor", eEV, FB["eEV"] + rc * 40 + 32)
                    for kc in range(32):
                        if kc == 31:
                            eST.n += 1
                        P.on("tensor", lambda t, wb=wb, kc=kc, pj=pj: (
                            t.matmul(out=pj[:], lhsT=wb[:, kc * 128:(kc + 1) * 128],
                                     rhs=y1T[:, kc * 512:(kc + 1) * 512],
                                     start=(kc == 0), stop=(kc == 31)).then_inc(eST.h, 1)
                            if kc == 31 else
                            t.matmul(out=pj[:], lhsT=wb[:, kc * 128:(kc + 1) * 128],
                                     rhs=y1T[:, kc * 512:(kc + 1) * 512],
                                     start=(kc == 0), stop=(kc == 31))))
                    P.wait("scalar", eST, eST.n)
                    eEV.n += 1
                    P.on("scalar", lambda s, pj=pj, oc=oc: s.activation(
                        out=zt[:, oc * 512:(oc + 1) * 512], in_=pj[:], func=AF.Identity,
                        scale=0.5, bias=fb2c[:, oc:oc + 1]).then_inc(eEV.h, 1))
                P.wait("tensor", eEV, eEV.n)
                for r in range(4):
                    rt = rc * 4 + r
                    for fc in range(8):
                        pj = ppf[fc % 2]
                        P.wait("tensor", c0, c0.n - 1)
                        pTR.n += 1
                        P.on("tensor", lambda t, fc=fc, r=r, pj=pj: t.transpose(
                            out=pj[:, 0:128], in_=zt[:, fc * 512 + r * 128: fc * 512 + r * 128 + 128],
                            identity=idf_sb[:]).then_inc(pTR.h, 1))
                        P.wait("vector", pTR, pTR.n)
                        c0.n += 1
                        P.on("vector", lambda v, fc=fc, rt=rt, pj=pj: v.tensor_tensor(
                            out=xres[:, rt * E + fc * 128: rt * E + fc * 128 + 128],
                            in0=xres[:, rt * E + fc * 128: rt * E + fc * 128 + 128],
                            in1=pj[:, 0:128], op=OP.add).then_inc(c0.h, 1))
            barrier(P)
            P.emit()

    def emit_attnb():
        c_outer = ExitStack()
        enO = c_outer.enter_context
        c12 = ExitStack()
        yT = c12.enter_context(nc.sbuf_tensor("yTC", [128, 8 * ROWS], BF16))

        with ExitStack() as st:
            en = st.enter_context
            ybf = en(nc.sbuf_tensor("ybfC", [128, E], F32))
            sqbC = en(nc.sbuf_tensor("sqbC", [128, E], F32))
            ylb = [en(nc.sbuf_tensor(f"ylbC{i}", [128, E], BF16)) for i in range(2)]
            mcol = en(nc.sbuf_tensor("mcolC", [128, 4], F32))
            ptc = [en(nc.psum_tensor(f"ptcC{i}", [128, 512], F32)) for i in range(2)]

            P = Prog(nc)
            c0, c2, c3, c4, c5 = (sm[x] for x in ("c0", "c2", "c3", "c4", "c5"))
            pTR = sm["pTR"]
            C1 = {x: sm[x].n for x in ("pTR", "c5")}
            for rt in range(NRT):
                xsl = xres[:, rt * E:(rt + 1) * E]
                yl = ylb[rt % 2]
                if rt >= 2:
                    P.wait("vector", pTR, C1["pTR"] + (rt - 1) * 8)
                emit_ln(P, mcol, xsl, ybf[:], sqbC[:], c2, c3, c4)
                c0.n += 1
                P.on("vector", lambda v, yl=yl: v.tensor_scalar(
                    out=yl[:], in0=ybf[:], scalar1=mcol[:, 3:4], scalar2=None,
                    op0=OP.mult).then_inc(c0.h, 1))
                P.wait("tensor", c0, c0.n)
                for fc in range(8):
                    gt = rt * 8 + fc
                    pj = ptc[gt % 2]
                    if gt >= 2:
                        P.wait("tensor", c5, C1["c5"] + gt - 1)
                    pTR.n += 1
                    P.on("tensor", lambda t, yl=yl, fc=fc, pj=pj: t.transpose(
                        out=pj.bitcast(BF16)[:, 0:128], in_=yl[:, fc * 128:(fc + 1) * 128],
                        identity=idb_sb[:]).then_inc(pTR.h, 1))
                    P.wait("vector", pTR, pTR.n)
                    c5.n += 1
                    P.on("vector", lambda v, pj=pj, fc=fc, rt=rt: v.tensor_copy(
                        out=yT[:, fc * ROWS + rt * 128: fc * ROWS + rt * 128 + 128],
                        in_=pj.bitcast(BF16)[:, 0:128]).then_inc(c5.h, 1))
            barrier(P)
            P.emit()

        with ExitStack() as st:
            en = st.enter_context
            rowb = en(nc.sbuf_tensor("rowbC", [128, NRT * E], BF16))
            wcb = en(nc.sbuf_tensor("wcbC", [128, 8 * E], BF16))
            stgc = [en(nc.sbuf_tensor(f"stgC{i}", [128, 512], BF16)) for i in range(2)]
            res2 = [en(nc.sbuf_tensor(f"resC{i}", [128, 512], BF16)) for i in range(2)]
            ppc = [en(nc.psum_tensor(f"ppcC{i}", [128, 512], F32)) for i in range(2)]
            ptc2 = [en(nc.psum_tensor(f"ptc2C{i}", [128, 512], F32)) for i in range(2)]

            P = Prog(nc)
            c0, c2, c5 = sm["c0"], sm["c2"], sm["c5"]
            eST, eEV, pTR, pPJ = sm["eST"], sm["eEV"], sm["pTR"], sm["pPJ"]
            C2 = {x: sm[x].n for x in ("eST", "eEV", "pTR", "c0", "c2", "c5", "pPJ")}
            WLIST = [("q", wq_b, 0, None), ("k0", wk_b0, 1, ksr), ("k1", wk_b1, 2, ksr),
                     ("v0", wv_b0, 3, vsr), ("v1", wv_b1, 4, vsr)]
            nev = 0
            ntr = 0
            res2_cons = {}
            for wi, (wn, wsrc, bidx, dstt) in enumerate(WLIST):
                if wi > 0:
                    P.wait("gpsimd", eST, eST.n)
                _dma(P, "gpsimd", wcb[:], wsrc.rearrange("(c p) n -> p c n", p=128))
                P.wait("tensor", dG, dG.n)
                for dp in range(8):
                    for rc in range(4):
                        jj = nev
                        pj = ppc[jj % 2]
                        sgt = stgc[jj % 2]
                        rs2 = res2[jj % 2]
                        if jj >= 2:
                            P.wait("tensor", eEV, C2["eEV"] + jj - 1)
                        for fc in range(8):
                            if fc == 7:
                                eST.n += 1
                            P.on("tensor", lambda t, fc=fc, dp=dp, rc=rc, pj=pj: (
                                t.matmul(out=pj[:],
                                         lhsT=wcb[:, fc * E + dp * 128: fc * E + dp * 128 + 128],
                                         rhs=yT[:, fc * ROWS + rc * 512: fc * ROWS + rc * 512 + 512],
                                         start=(fc == 0), stop=(fc == 7)).then_inc(eST.h, 1)
                                if fc == 7 else
                                t.matmul(out=pj[:],
                                         lhsT=wcb[:, fc * E + dp * 128: fc * E + dp * 128 + 128],
                                         rhs=yT[:, fc * ROWS + rc * 512: fc * ROWS + rc * 512 + 512],
                                         start=(fc == 0), stop=(fc == 7))))
                        P.wait("scalar", eST, eST.n)
                        if jj >= 2:
                            P.wait("scalar", c2, C2["c2"] + jj - 1)
                        eEV.n += 1
                        P.on("scalar", lambda s, pj=pj, sgt=sgt, bidx=bidx, dp=dp: s.activation(
                            out=sgt[:], in_=pj[:], func=AF.Identity,
                            bias=bkvc[:, bidx * 8 + dp: bidx * 8 + dp + 1]).then_inc(eEV.h, 1))
                        P.wait("vector", eEV, eEV.n)
                        if jj - 2 in res2_cons:
                            kind, cnt = res2_cons[jj - 2]
                            if kind == "ptr":
                                P.wait("vector", pTR, cnt)
                            else:
                                P.wait("vector", DS4[0], cnt)
                        c2.n += 1
                        P.on("vector", lambda v, sgt=sgt, rs2=rs2, dp=dp, rc=rc: v.tensor_tensor(
                            out=rs2[:], in0=sgt[:],
                            in1=yT[:, dp * ROWS + rc * 512: dp * ROWS + rc * 512 + 512],
                            op=OP.add).then_inc(c2.h, 1))
                        if wn == "q":
                            P.wait("sync", c2, c2.n)
                            _dma(P, "sync", qtb[2 * dp, :, rc * 512:(rc + 1) * 512], rs2[0:64, :],
                                 q=DS4[0])
                            res2_cons[jj] = ("dma", _dma(
                                P, "sync", qtb[2 * dp + 1, :, rc * 512:(rc + 1) * 512],
                                rs2[64:128, :], q=DS4[0]))
                        else:
                            P.wait("tensor", c2, c2.n)
                            for tt in range(4):
                                gt = ntr
                                pj2 = ptc2[gt % 2]
                                if gt >= 2:
                                    P.wait("tensor", c5, C2["c5"] + gt - 1)
                                pTR.n += 1
                                P.on("tensor", lambda t, rs2=rs2, tt=tt, pj2=pj2: t.transpose(
                                    out=pj2.bitcast(BF16)[:, 0:128],
                                    in_=rs2[:, tt * 128:(tt + 1) * 128],
                                    identity=idb_sb[:]).then_inc(pTR.h, 1))
                                P.wait("vector", pTR, pTR.n)
                                c5.n += 1
                                P.on("vector", lambda v, pj2=pj2, rc=rc, tt=tt, dp=dp: v.tensor_copy(
                                    out=rowb[:, (rc * 4 + tt) * E + dp * 128: (rc * 4 + tt) * E + dp * 128 + 128],
                                    in_=pj2.bitcast(BF16)[:, 0:128]).then_inc(c5.h, 1))
                                ntr += 1
                            res2_cons[jj] = ("ptr", pTR.n)
                        nev += 1
                if wn != "q":
                    k = int(wn[1])
                    P.wait("sync", c5, c5.n)
                    nrb = _dma(P, "sync", dstt[k].rearrange("(t p) d -> p t d", p=128),
                               rowb[:].rearrange("p (t d) -> p t d", d=E), q=DRB)
                    P.wait("vector", DRB, nrb)
            barrier(P)
            P.emit()

        c12.close()

        outT = enO(nc.sbuf_tensor("outTC", [128, 8 * ROWS], F32))
        with ExitStack() as st:
            en = st.enter_context
            krb1 = en(nc.sbuf_tensor("krbC", [128, 16 * E], BF16))
            vrb1 = en(nc.sbuf_tensor("vrbC", [128, 16 * E], BF16))
            krb = [krb1, krb1]
            vrb = [vrb1, vrb1]
            s01 = en(nc.sbuf_tensor("s01C", [128, E], F32))
            s1tmp = en(nc.sbuf_tensor("s1tmpC", [64, E], F32))
            stb = s1tmp[:].bitcast(BF16)[:, 0:E]
            qth1 = en(nc.sbuf_tensor("qthC0", [64, ROWS], BF16))
            qth = [qth1, qth1]
            stk = [en(nc.psum_tensor(f"stkC{i}", [64, 1024], F32)) for i in range(2)]
            pqc = [en(nc.psum_tensor(f"pqcC{i}", [64, 512], F32)) for i in range(2)]

            P = Prog(nc)
            c0, c2, c5 = sm["c0"], sm["c2"], sm["c5"]
            eST, eEV = sm["eST"], sm["eEV"]
            cc = sm["cAC"]
            C3 = {x: sm[x].n for x in ("eST", "c2", "c0")}
            for k in range(2):
                if k == 1:
                    P.wait("sync", eST, eST.n)
                _dma(P, "sync", krb[k][:].rearrange("p (t d) -> p t d", d=E),
                     ksr[k].rearrange("(t p) d -> p t d", p=128), q=DN[k])
                dtot = _dma(P, "sync", vrb[k][:].rearrange("p (t d) -> p t d", d=E),
                            vsr[k].rearrange("(t p) d -> p t d", p=128), q=DN[k])
                P.wait("tensor", DN[k], dtot)
                for h in range(H):
                    for ns in range(16):
                        if ns == 15:
                            eST.n += 1
                        P.on("tensor", lambda t, k=k, h=h, ns=ns: (
                            t.matmul(out=stk[k][:, h * 64:(h + 1) * 64],
                                     lhsT=krb[k][:, ns * E + h * 64: ns * E + h * 64 + 64],
                                     rhs=vrb[k][:, ns * E + h * 64: ns * E + h * 64 + 64],
                                     start=(ns == 0), stop=(ns == 15)).then_inc(eST.h, 1)
                            if ns == 15 else
                            t.matmul(out=stk[k][:, h * 64:(h + 1) * 64],
                                     lhsT=krb[k][:, ns * E + h * 64: ns * E + h * 64 + 64],
                                     rhs=vrb[k][:, ns * E + h * 64: ns * E + h * 64 + 64],
                                     start=(ns == 0), stop=(ns == 15))))
            P.wait("vector", eST, eST.n)
            c2.n += 2
            P.on("vector", lambda v: v.tensor_copy(out=s01[0:64, :], in_=stk[0][:]).then_inc(sm["c2"].h, 1))
            P.on("vector", lambda v: v.tensor_copy(out=s01[64:128, :], in_=stk[1][:]).then_inc(sm["c2"].h, 1))
            P.wait("gpsimd", c2, c2.n)
            _dma(P, "gpsimd", cin_t[:, :], s01[:])
            P.wait("gpsimd", dG, dG.n)
            cc.n += 1
            P.on("gpsimd", lambda g: g.collective_compute(
                "AllReduce", OP.add, replica_groups=[[0, 1, 2, 3], [4, 5, 6, 7]],
                ins=[cin_t.ap().opt()], outs=[cout_t.ap().opt()]).then_inc(cc.h, 1))
            P.wait("gpsimd", cc, cc.n)
            _dma(P, "gpsimd", s01[:], cout_t[:, :])
            P.wait("vector", dG, dG.n)
            c2.n += 1
            P.on("vector", lambda v: v.tensor_copy(out=s1tmp[:], in_=s01[64:128, :]
                                                   ).then_inc(c2.h, 1))
            P.wait("vector", c2, c2.n)
            c2.n += 1
            P.on("vector", lambda v: v.tensor_tensor(out=s01[0:64, :], in0=s01[0:64, :],
                                                     in1=s1tmp[:], op=OP.mult
                                                     ).then_inc(c2.h, 1))
            P.wait("vector", c2, c2.n)
            c2.n += 1
            P.on("vector", lambda v: v.tensor_scalar(out=stb, in0=s01[0:64, :],
                                                     scalar1=0.5 ** 24, scalar2=None,
                                                     op0=OP.mult).then_inc(c2.h, 1))
            stb_done = c2.n
            for h in range(H):
                if h >= 1:
                    P.wait("sync", c0, c0.n)
                dtot = _dma(P, "sync", qth[h % 2][:], qtb[h], q=DQ[h % 2])
                P.wait("tensor", DQ[h % 2], dtot)
                P.wait("tensor", c2, stb_done)
                for rc in range(4):
                    pj = pqc[rc % 2]
                    if h * 4 + rc >= 2:
                        P.wait("tensor", c0, C3["c0"] + h * 4 + rc - 1)
                    eST.n += 1
                    P.on("tensor", lambda t, h=h, rc=rc, pj=pj: t.matmul(
                        out=pj[:], lhsT=stb[:, h * 64:(h + 1) * 64],
                        rhs=qth[h % 2][:, rc * 512:(rc + 1) * 512],
                        start=True, stop=True).then_inc(eST.h, 1))
                    P.wait("vector", eST, eST.n)
                    c0.n += 1
                    P.on("vector", lambda v, h=h, rc=rc, pj=pj: v.tensor_copy(
                        out=outT[(h % 2) * 64:(h % 2) * 64 + 64,
                                 (h // 2) * ROWS + rc * 512:(h // 2) * ROWS + rc * 512 + 512],
                        in_=pj[:]).then_inc(c0.h, 1))
            barrier(P)
            P.emit()

        with ExitStack() as st:
            en = st.enter_context
            orow = en(nc.sbuf_tensor("orowC", [128, E], F32))
            sqbC2 = en(nc.sbuf_tensor("sqbC2", [128, E], F32))
            xh = [en(nc.sbuf_tensor(f"xhC{i}", [128, E], BF16)) for i in range(2)]
            wob = en(nc.sbuf_tensor("wobC", [128, 8 * E], BF16))
            ostg = [en(nc.sbuf_tensor(f"ostgC{i}", [128, 512], F32)) for i in range(2)]
            mcol = en(nc.sbuf_tensor("mcolC7", [128, 4], F32))
            xhT = en(nc.sbuf_tensor("xhTC", [128, 8 * ROWS], BF16))
            pt7 = [en(nc.psum_tensor(f"pt7C{i}", [128, 512], F32)) for i in range(2)]
            px7 = [en(nc.psum_tensor(f"px7C{i}", [128, 512], F32)) for i in range(2)]
            pp7 = [en(nc.psum_tensor(f"pp7C{i}", [128, 512], F32)) for i in range(2)]

            P = Prog(nc)
            c0, c2, c3, c4, c5 = (sm[x] for x in ("c0", "c2", "c3", "c4", "c5"))
            eST, eEV, pTR = sm["eST"], sm["eEV"], sm["pTR"]
            C7 = {x: sm[x].n for x in ("eST", "eEV", "pTR", "c0", "c2", "c5")}
            _dma(P, "gpsimd", wob[:].rearrange("p (c n) -> p c n", n=E), wo_b.rearrange("(c p) n -> p c n", p=128))
            ntr = 0
            evac_c2 = []
            nor = 0
            for rt in range(NRT):
                for fc in range(8):
                    pj = pt7[nor % 2]
                    if nor >= 2:
                        P.wait("tensor", c2, evac_c2[nor - 2])
                    pTR.n += 1
                    P.on("tensor", lambda t, fc=fc, rt=rt, pj=pj: t.transpose(
                        out=pj[:, 0:128],
                        in_=outT[:, fc * ROWS + rt * 128: fc * ROWS + rt * 128 + 128],
                        identity=idf_sb[:]).then_inc(pTR.h, 1))
                    P.wait("vector", pTR, pTR.n)
                    c2.n += 1
                    evac_c2.append(c2.n)
                    P.on("vector", lambda v, fc=fc, pj=pj: v.tensor_copy(
                        out=orow[:, fc * 128:(fc + 1) * 128], in_=pj[:, 0:128]).then_inc(c2.h, 1))
                    nor += 1
                xhl = xh[rt % 2]
                if rt >= 2:
                    P.wait("vector", pTR, C7["pTR"] + (rt - 1) * 16)
                emit_ln(P, mcol, orow[:], orow[:], sqbC2[:], c2, c3, c4)
                c0.n += 1
                P.on("vector", lambda v, xhl=xhl: v.tensor_scalar(
                    out=xhl[:], in0=orow[:], scalar1=mcol[:, 3:4], scalar2=None,
                    op0=OP.mult).then_inc(c0.h, 1))
                P.wait("tensor", c0, c0.n)
                for fc in range(8):
                    gx = rt * 8 + fc
                    pj = px7[gx % 2]
                    if gx >= 2:
                        P.wait("tensor", c5, C7["c5"] + gx - 1)
                    pTR.n += 1
                    P.on("tensor", lambda t, xhl=xhl, fc=fc, pj=pj: t.transpose(
                        out=pj.bitcast(BF16)[:, 0:128], in_=xhl[:, fc * 128:(fc + 1) * 128],
                        identity=idb_sb[:]).then_inc(pTR.h, 1))
                    P.wait("scalar", pTR, pTR.n)
                    c5.n += 1
                    P.on("scalar", lambda s, pj=pj, fc=fc, rt=rt: s.activation(
                        out=xhT[:, fc * ROWS + rt * 128: fc * ROWS + rt * 128 + 128],
                        in_=pj.bitcast(BF16)[:, 0:128], func=AF.Identity,
                        scale=bcols[:, BC["lng"] + fc: BC["lng"] + fc + 1],
                        bias=bcols[:, BC["lnb"] + fc: BC["lnb"] + fc + 1]).then_inc(c5.h, 1))
                    ntr += 1
            P.wait("tensor", c5, c5.n)
            P.wait("tensor", c2, c2.n)
            P.wait("tensor", dG, dG.n)
            nj = 0
            for dp in range(8):
                for rc in range(4):
                    pj = pp7[nj % 2]
                    if nj >= 2:
                        P.wait("tensor", eEV, C7["eEV"] + nj - 1)
                    for fc in range(8):
                        if fc == 7:
                            eST.n += 1
                        P.on("tensor", lambda t, fc=fc, dp=dp, rc=rc, pj=pj: (
                            t.matmul(out=pj[:],
                                     lhsT=wob[:, fc * E + dp * 128: fc * E + dp * 128 + 128],
                                     rhs=xhT[:, fc * ROWS + rc * 512: fc * ROWS + rc * 512 + 512],
                                     start=(fc == 0), stop=(fc == 7)).then_inc(eST.h, 1)
                            if fc == 7 else
                            t.matmul(out=pj[:],
                                     lhsT=wob[:, fc * E + dp * 128: fc * E + dp * 128 + 128],
                                     rhs=xhT[:, fc * ROWS + rc * 512: fc * ROWS + rc * 512 + 512],
                                     start=(fc == 0), stop=(fc == 7))))
                    P.wait("scalar", eST, eST.n)
                    if nj >= 2:
                        P.wait("scalar", c0, C7["c0"] + 16 + (nj - 1) * 4)
                    eEV.n += 1
                    osg = ostg[nj % 2]
                    P.on("scalar", lambda s, pj=pj, dp=dp, osg=osg: s.activation(
                        out=osg[:], in_=pj[:], func=AF.Identity,
                        bias=bcols[:, BC["bo_b"] + dp: BC["bo_b"] + dp + 1]).then_inc(eEV.h, 1))
                    P.wait("tensor", eEV, eEV.n)
                    for tt in range(4):
                        pj2 = pp7[(nj + 1) % 2] if False else pt7[tt % 2]
                        P.wait("tensor", c0, c0.n - 1)
                        pTR.n += 1
                        P.on("tensor", lambda t, osg=osg, tt=tt, pj2=pj2: t.transpose(
                            out=pj2[:, 0:128], in_=osg[:, tt * 128:(tt + 1) * 128],
                            identity=idf_sb[:]).then_inc(pTR.h, 1))
                        P.wait("vector", pTR, pTR.n)
                        rt = rc * 4 + tt
                        c0.n += 1
                        P.on("vector", lambda v, rt=rt, dp=dp, pj2=pj2: v.tensor_tensor(
                            out=xres[:, rt * E + dp * 128: rt * E + dp * 128 + 128],
                            in0=xres[:, rt * E + dp * 128: rt * E + dp * 128 + 128],
                            in1=pj2[:, 0:128], op=OP.add).then_inc(c0.h, 1))
                    nj += 1
            barrier(P)
            P.emit()
        c_outer.close()

    if nphase >= 4:
        emit_dequant()
        emit_ffn("F1")
    if nphase >= 5:
        emit_attnb()
    if nphase >= 6:
        emit_ffn("F2")

    P = Prog(nc)
    ov = out.rearrange("(t p) d -> p t d", p=128)
    xv = xres[:].rearrange("p (t d) -> p t d", d=E)
    for j in range(4):
        _dma(P, "sync", ov[:, j * 4:(j + 1) * 4, :], xv[:, j * 4:(j + 1) * 4, :])
    dbgsel = os.environ.get("MK_DBG", "")
    if dbgsel:
        dsrc = {
            "ksr0": ksr[0], "ksr1": ksr[1], "vsr0": vsr[0], "vsr1": vsr[1],
            "qtb": qtb.rearrange("a d (m n) -> (a d m) n", n=E),
            "cout": cout_t.ap(),
            "accs": accs,
        }[dbgsel]
        shp = dsrc.shape
        P.wait("gpsimd", dS, dS.n)
        dn = _dma(P, "gpsimd", out[0:shp[0], 0:shp[1]], dsrc)
        P.wait("sync", dG, dn)
    P.wait("sync", dS, dS.n)
    P.emit()

    outer.close()
    return nc


SCALE = 1.0 / math.sqrt(D)


_NC_CACHE = {}
LAST_RESULTS = None


def kernel(L0_attn_a_x, L0_attn_a_Wq, L0_attn_a_bq, L0_attn_a_Wk, L0_attn_a_bk,
           L0_attn_a_Wv, L0_attn_a_bv, L0_attn_a_Wo, L0_attn_a_bo,
           L0_attn_a_phase_modulators, L0_attn_a_interference_weights,
           L0_attn_b_Wq, L0_attn_b_bq, L0_attn_b_Wk, L0_attn_b_bk,
           L0_attn_b_Wv, L0_attn_b_bv, L0_attn_b_Wo, L0_attn_b_bo,
           L0_attn_b_ln_g, L0_attn_b_ln_b,
           L0_ffn_ln_gamma, L0_ffn_ln_beta, L0_ffn_fc1_q, L0_ffn_fc1_scales,
           L0_ffn_fc1_bias, L0_ffn_fc2_q, L0_ffn_fc2_scales, L0_ffn_fc2_bias):
    nphase = int(os.environ.get("MK_NPHASE", "6"))
    x = np.ascontiguousarray(np.asarray(L0_attn_a_x, dtype=np.float32))
    f32 = lambda a: np.ascontiguousarray(np.asarray(a, dtype=np.float32))
    i32 = lambda a: np.ascontiguousarray(np.asarray(a, dtype=np.int32))

    common = dict(
        wq_a=f32(L0_attn_a_Wq), wk_a=f32(L0_attn_a_Wk), wv_a=f32(L0_attn_a_Wv),
        wo_a=f32(L0_attn_a_Wo), bq_a=f32(L0_attn_a_bq), bk_a=f32(L0_attn_a_bk),
        bv_a=f32(L0_attn_a_bv), bo_a=f32(L0_attn_a_bo),
        pmt=np.ascontiguousarray(f32(L0_attn_a_phase_modulators).T),
        iw=f32(L0_attn_a_interference_weights),
        f_g=f32(L0_ffn_ln_gamma), f_b=f32(L0_ffn_ln_beta),
        w1q=i32(L0_ffn_fc1_q), s1=f32(L0_ffn_fc1_scales), fb1=f32(L0_ffn_fc1_bias),
        w2q=i32(L0_ffn_fc2_q), s2=f32(L0_ffn_fc2_scales), fb2=f32(L0_ffn_fc2_bias),
        wq_b=f32(L0_attn_b_Wq), wk_b0=f32(L0_attn_b_Wk[0]), wk_b1=f32(L0_attn_b_Wk[1]),
        wv_b0=f32(L0_attn_b_Wv[0]), wv_b1=f32(L0_attn_b_Wv[1]), wo_b=f32(L0_attn_b_Wo),
        bq_b=f32(L0_attn_b_bq), bk_b0=f32(L0_attn_b_bk[0]), bk_b1=f32(L0_attn_b_bk[1]),
        bv_b0=f32(L0_attn_b_bv[0]), bv_b1=f32(L0_attn_b_bv[1]), bo_b=f32(L0_attn_b_bo),
        lng=f32(L0_attn_b_ln_g), lnb=f32(L0_attn_b_ln_b),
        idf=np.eye(128, dtype=np.float32),
    )

    key = nphase
    if key not in _NC_CACHE:
        _NC_CACHE[key] = build(nphase)
    nc = _NC_CACHE[key]

    in_maps = []
    for c in range(8):
        b = c // 4
        off = (c % 4) * ROWS
        xq = np.roll(x[b], -off, axis=0)
        m = dict(common)
        m["xb"] = np.ascontiguousarray(xq)
        in_maps.append(m)

    res = run_bass_kernel_spmd(nc, in_maps, core_ids=list(range(8)))
    global LAST_RESULTS
    LAST_RESULTS = res.results
    shards = [np.asarray(res.results[c]["out"]) for c in range(8)]
    full = np.concatenate(shards, axis=0).reshape(B, S, E)
    return full.astype(np.float32)


# version 06
# speedup vs baseline: 1.0179x; 1.0179x over previous
import math
import os
from contextlib import ExitStack

import numpy as np
import ml_dtypes

import concourse.bass as bass
import concourse.mybir as mybir
from concourse.alu_op_type import AluOpType as OP
from concourse.bass_utils import run_bass_kernel_spmd

F32 = mybir.dt.float32
BF16 = mybir.dt.bfloat16
I32 = mybir.dt.int32
AF = mybir.ActivationFunctionType
AX = mybir.AxisListType

B, S, E = 2, 8192, 1024
H, D = 16, 64
WL = 16
ROWS = 2048
NRT = ROWS // 128
KERR = 2.7e-18 * 1e6
EPS = 1e-5
FF = 4096
NT = S // 128
NQ = 256
QCN = ROWS // NQ
TWO_PI = 2.0 * math.pi
ENGS = ("sync", "vector", "tensor", "scalar", "gpsimd")


class Sem:
    def __init__(self, handle):
        self.h = handle
        self.n = 0


class Prog:
    def __init__(self, nc):
        self.nc = nc
        self.q = {k: [] for k in ENGS}

    def on(self, eng, fn):
        self.q[eng].append(fn)

    def wait(self, eng, sem, val):
        if val > 0:
            self.q[eng].append(lambda e, s=sem.h, v=val: e.wait_ge(s, v))

    def emit(self):
        q = self.q
        with self.nc.Block() as blk:
            @blk.sync
            def _(e):
                for fn in q["sync"]:
                    fn(e)

            @blk.vector
            def _(e):
                for fn in q["vector"]:
                    fn(e)

            @blk.tensor
            def _(e):
                for fn in q["tensor"]:
                    fn(e)

            @blk.scalar
            def _(e):
                for fn in q["scalar"]:
                    fn(e)

            @blk.gpsimd
            def _(e):
                for fn in q["gpsimd"]:
                    fn(e)


def build(nphase=6):
    nc = bass.Bass("TRN2", target_bir_lowering=False)

    dri = lambda nm, shp, dt: nc.dram_tensor(nm, shp, dt, kind="ExternalInput").ap()
    xb = dri("xb", [S, E], F32)
    wq_a, wk_a, wv_a, wo_a = (dri(n, [E, E], F32) for n in ("wq_a", "wk_a", "wv_a", "wo_a"))
    bq_a, bk_a, bv_a, bo_a = (dri(n, [E], F32) for n in ("bq_a", "bk_a", "bv_a", "bo_a"))
    pmt = dri("pmt", [H, WL], F32)
    iw = dri("iw", [WL], F32)
    f_g, f_b = dri("f_g", [E], F32), dri("f_b", [E], F32)
    w1q = dri("w1q", [E, FF], I32)
    s1 = dri("s1", [8, FF], F32)
    fb1 = dri("fb1", [FF], F32)
    w2q = dri("w2q", [FF, E], I32)
    s2 = dri("s2", [32, E], F32)
    fb2 = dri("fb2", [E], F32)
    wq_b, wk_b0, wk_b1, wv_b0, wv_b1, wo_b = (
        dri(n, [E, E], F32) for n in ("wq_b", "wk_b0", "wk_b1", "wv_b0", "wv_b1", "wo_b"))
    bq_b, bk_b0, bk_b1, bv_b0, bv_b1, bo_b = (
        dri(n, [E], F32) for n in ("bq_b", "bk_b0", "bk_b1", "bv_b0", "bv_b1", "bo_b"))
    lng, lnb = dri("lng", [E], F32), dri("lnb", [E], F32)
    idf = dri("idf", [128, 128], F32)
    out = nc.dram_tensor("out", [ROWS, E], F32, kind="ExternalOutput").ap()

    kts = nc.dram_tensor("kts", [H, D, S], BF16).ap()
    vts = nc.dram_tensor("vts", [H, D, S], BF16).ap()
    qts = nc.dram_tensor("qts", [H, D, ROWS], BF16).ap()
    accs = nc.dram_tensor("accs", [E, ROWS], BF16).ap()
    flatr = nc.dram_tensor("flatr", [WL * H], F32).ap()
    w1ds = nc.dram_tensor("w1ds", [32, 128, 8, 128], BF16).ap()
    w2ds = nc.dram_tensor("w2ds", [8, 128, 32, 128], BF16).ap()
    ksr = nc.dram_tensor("ksr", [2, ROWS, E], BF16).ap()
    vsr = nc.dram_tensor("vsr", [2, ROWS, E], BF16).ap()
    qtb = nc.dram_tensor("qtb", [H, D, ROWS], BF16).ap()
    cin_t = nc.dram_tensor("cin_st", [128, E], F32)
    cout_t = nc.dram_tensor("cout_st", [128, E], F32)

    outer = ExitStack()
    names = ["dS", "dG", "c0", "c2", "c3", "c4", "c5",
             "eST", "eEV", "eXP", "eAQ", "eFN", "rRQ", "rBC", "rTM",
             "pVT", "vSB", "cAC", "pPJ", "pTR",
             "dx0", "dx1", "dx2", "dx3", "ds0", "ds1", "ds2", "ds3",
             "dk0", "dk1", "dn0", "dn1", "dq0", "dq1", "dw0", "dw1", "drb",
             "bS", "bV", "bT", "bA", "bG"]
    sm = {n: Sem(outer.enter_context(nc.semaphore(n))) for n in names}
    dS, dG = sm["dS"], sm["dG"]
    DX = [sm[f"dx{i}"] for i in range(4)]
    DS4 = [sm[f"ds{i}"] for i in range(4)]
    DK = [sm[f"dk{i}"] for i in range(2)]
    DN = [sm[f"dn{i}"] for i in range(2)]
    DQ = [sm[f"dq{i}"] for i in range(2)]
    DW = [sm[f"dw{i}"] for i in range(2)]
    DRB = sm["drb"]
    SYNC_QS = [dS] + DX + DS4 + DK + DN + DQ + DW + [DRB]
    BARMAP = dict(zip(ENGS, [sm["bS"], sm["bV"], sm["bT"], sm["bA"], sm["bG"]]))
    PH = [0]

    def barrier(P):
        PH[0] += 1
        tgt = PH[0]
        for q in SYNC_QS:
            P.wait("sync", q, q.n)
        P.wait("gpsimd", dG, dG.n)
        for en in ENGS:
            b = BARMAP[en]
            b.n += 1
            P.on(en, lambda e, _b=b: e.sem_inc(_b.h, 1))
        for en in ENGS:
            for b in BARMAP.values():
                P.wait(en, b, tgt)

    def _dma(P, eng, dst, src, q=None, slow=False):
        q = q if q is not None else (dG if eng == "gpsimd" else dS)
        q.n += 16
        P.on(eng, lambda e, d=dst, s=src, h=q.h, sl=slow: e.dma_start(
            out=d, in_=s, allow_slow_non_contiguous=sl).then_inc(h, 16))
        return q.n

    ones_f = outer.enter_context(nc.sbuf_tensor("ones_f", [1, 128], F32))
    ones_b = outer.enter_context(nc.sbuf_tensor("ones_b", [64, 1], BF16))
    idf_sb = outer.enter_context(nc.sbuf_tensor("idf_sb", [128, 128], F32))
    idb_sb = outer.enter_context(nc.sbuf_tensor("idb_sb", [128, 128], BF16))
    epsc = outer.enter_context(nc.sbuf_tensor("epsc", [128, 1], F32))
    bcols = outer.enter_context(nc.sbuf_tensor("bcols", [128, 8 * 11], F32))
    BCN = ["bq_a", "bk_a", "bv_a", "bo_a", "f_g", "f_b", "fb2h", "bo_b", "lng", "lnb", "bq_b"]
    BC = {n: i * 8 for i, n in enumerate(BCN)}
    fb1c = outer.enter_context(nc.sbuf_tensor("fb1c", [128, 32], F32))
    fb2c = outer.enter_context(nc.sbuf_tensor("fb2c", [128, 8], F32))
    bkvc = outer.enter_context(nc.sbuf_tensor("bkvc", [128, 5 * 8], F32))
    stackA = ExitStack()
    sccol = stackA.enter_context(nc.sbuf_tensor("sccol", [128, 512], F32))
    iwd = stackA.enter_context(nc.sbuf_tensor("iwd", [1, WL], F32))
    misc = stackA.enter_context(nc.sbuf_tensor("misc", [1, 16], F32))
    arow = stackA.enter_context(nc.sbuf_tensor("arow", [1, 512], F32))
    ncur = stackA.enter_context(nc.sbuf_tensor("ncur", [1, 4], F32))

    P = Prog(nc)
    _dma(P, "sync", idf_sb[:], idf)
    for nm, src in [("bq_a", bq_a), ("bk_a", bk_a), ("bv_a", bv_a), ("bo_a", bo_a),
                    ("f_g", f_g), ("f_b", f_b), ("fb2h", fb2), ("bo_b", bo_b),
                    ("lng", lng), ("lnb", lnb), ("bq_b", bq_b)]:
        _dma(P, "sync", bcols[:, BC[nm]:BC[nm] + 8], src.rearrange("(c p) -> p c", p=128), slow=True)
    _dma(P, "sync", fb1c[:], fb1.rearrange("(c p) -> p c", p=128), slow=True)
    for i, src in enumerate([bq_b, bk_b0, bk_b1, bv_b0, bv_b1]):
        _dma(P, "sync", bkvc[:, i * 8:(i + 1) * 8], src.rearrange("(c p) -> p c", p=128), slow=True)
    _dma(P, "sync", iwd[:], iw.unsqueeze(0))
    P.wait("vector", dS, dS.n)
    P.on("vector", lambda v: v.memset(ones_f[:], 1.0))
    P.on("vector", lambda v: v.memset(ones_b[:], 1.0))
    P.on("vector", lambda v: v.memset(epsc[:], EPS))
    P.on("vector", lambda v: v.memset(ncur[:], 0.0))
    P.on("vector", lambda v: v.tensor_copy(out=idb_sb[:], in_=idf_sb[:]))
    P.on("vector", lambda v: v.tensor_scalar(out=iwd[:], in0=iwd[:], scalar1=1.0 / 16.0,
                                             scalar2=None, op0=OP.mult))
    P.on("vector", lambda v: v.tensor_scalar(out=fb2c[:], in0=bcols[:, BC["fb2h"]:BC["fb2h"] + 8],
                                             scalar1=0.5, scalar2=None, op0=OP.mult))
    barrier(P)
    P.emit()

    def emit_ln(P, mcol, src, center_dst, sqbuf, c2, c3, c4):
        c2.n += 1
        P.on("vector", lambda v: v.reduce_sum(out=mcol[:, 0:1], in_=src, axis=AX.X,
                                              negate=True).then_inc(c2.h, 1))
        P.wait("vector", c2, c2.n)
        c2.n += 1
        P.on("vector", lambda v: v.tensor_scalar(out=mcol[:, 0:1], in0=mcol[:, 0:1],
                                                 scalar1=1.0 / E, scalar2=None,
                                                 op0=OP.mult).then_inc(c2.h, 1))
        P.wait("vector", c2, c2.n)
        c2.n += 1
        P.on("vector", lambda v: v.tensor_scalar(out=center_dst, in0=src,
                                                 scalar1=mcol[:, 0:1], scalar2=None,
                                                 op0=OP.add).then_inc(c2.h, 1))
        P.wait("vector", c2, c2.n)
        c2.n += 1
        P.on("vector", lambda v: v.tensor_tensor(out=sqbuf, in0=center_dst, in1=center_dst,
                                                 op=OP.mult).then_inc(c2.h, 1))
        P.wait("vector", c2, c2.n)
        c3.n += 1
        P.on("vector", lambda v: v.reduce_sum(out=mcol[:, 1:2], in_=sqbuf,
                                              axis=AX.X).then_inc(c3.h, 1))
        P.wait("scalar", c3, c3.n)
        c4.n += 1
        P.on("scalar", lambda s: s.activation(out=mcol[:, 2:3], in_=mcol[:, 1:2], func=AF.Ln,
                                              scale=1.0 / E, bias=epsc[:]).then_inc(c4.h, 1))
        P.wait("scalar", c4, c4.n)
        c4.n += 1
        P.on("scalar", lambda s: s.activation(out=mcol[:, 3:4], in_=mcol[:, 2:3], func=AF.Exp,
                                              scale=-0.5).then_inc(c4.h, 1))
        P.wait("vector", c4, c4.n)

    with ExitStack() as st:
        en = st.enter_context
        wqs = en(nc.sbuf_tensor("wqs", [128, 8 * E], BF16))
        wks = en(nc.sbuf_tensor("wks", [128, 8 * E], BF16))
        wvs = en(nc.sbuf_tensor("wvs", [128, 8 * E], BF16))
        xt = [en(nc.sbuf_tensor(f"xt{i}", [128, E], F32)) for i in range(4)]
        sqb = en(nc.sbuf_tensor("sqb", [128, E], F32))
        xlb = [en(nc.sbuf_tensor(f"xlb{i}", [128, E], BF16)) for i in range(2)]
        xlnt = en(nc.sbuf_tensor("xlnt", [128, 8 * 512], BF16))
        stg = [en(nc.sbuf_tensor(f"stg{i}", [128, 512], BF16)) for i in range(4)]
        mcol = en(nc.sbuf_tensor("mcolA", [128, 4], F32))
        ptrp = [en(nc.psum_tensor(f"ptrp{i}", [128, 512], F32)) for i in range(2)]
        ppp = [en(nc.psum_tensor(f"pppA{i}", [128, 512], F32)) for i in range(2)]

        P = Prog(nc)
        c0, c2, c3, c4, c5 = (sm[x] for x in ("c0", "c2", "c3", "c4", "c5"))
        pTR, pPJ, eST, eEV = sm["pTR"], sm["pPJ"], sm["eST"], sm["eEV"]
        b0 = {x: sm[x].n for x in ("c0", "eST", "eEV", "pTR", "pPJ", "c5")}

        for wsb, wsrc in [(wqs, wq_a), (wks, wk_a), (wvs, wv_a)]:
            _dma(P, "gpsimd", wsb[:].rearrange("p (c n) -> p c n", n=E), wsrc.rearrange("(c p) n -> p c n", p=128))
        wload = dG.n

        stg_dma = {}
        chunk_est = {}
        nstg = 0
        for ch in range(16):
            own = ch < 4
            for r in range(4):
                i = ch * 4 + r
                if i >= 4:
                    P.wait("sync", c0, b0["c0"] + i - 3)
                dtot = _dma(P, "sync", xt[i % 4][:], xb[i * 128:(i + 1) * 128, :],
                            q=DX[i % 4])
                x = xt[i % 4][:]
                xl = xlb[i % 2]
                P.wait("vector", DX[i % 4], dtot)
                if i >= 2:
                    P.wait("vector", pTR, b0["pTR"] + (i - 1) * 8)
                emit_ln(P, mcol, x, x, sqb[:], c2, c3, c4)
                c0.n += 1
                P.on("vector", lambda v, x=x, xl=xl: v.tensor_scalar(
                    out=xl[:], in0=x, scalar1=mcol[:, 3:4], scalar2=None,
                    op0=OP.mult).then_inc(c0.h, 1))
                P.wait("tensor", c0, c0.n)
                if r == 0:
                    pass
                for fc in range(8):
                    gt = i * 8 + fc
                    pj = ptrp[gt % 2]
                    if gt >= 2:
                        P.wait("tensor", c5, b0["c5"] + gt - 1)
                    pTR.n += 1
                    P.on("tensor", lambda t, xl=xl, fc=fc, pj=pj: t.transpose(
                        out=pj.bitcast(BF16)[:, 0:128], in_=xl[:, fc * 128:(fc + 1) * 128],
                        identity=idb_sb[:]).then_inc(pTR.h, 1))
                    P.wait("vector", pTR, pTR.n)
                    if fc == 0 and r == 0 and ch > 0:
                        P.wait("vector", eST, chunk_est[ch - 1])
                    c5.n += 1
                    P.on("vector", lambda v, pj=pj, fc=fc, r=r: v.tensor_copy(
                        out=xlnt[:, fc * 512 + r * 128: fc * 512 + r * 128 + 128],
                        in_=pj.bitcast(BF16)[:, 0:128]).then_inc(c5.h, 1))
            xlnt_ready = c5.n
            P.wait("tensor", c5, xlnt_ready)
            if ch == 0:
                P.wait("tensor", dG, wload)
            groups = [("K", wks, BC["bk_a"], kts), ("V", wvs, BC["bv_a"], vts)]
            if own:
                groups.append(("Q", wqs, BC["bq_a"], qts))
            for gn, wsb, bcoff, dstt in groups:
                for dp in range(8):
                    jj = nstg
                    pj = ppp[jj % 2]
                    sgt = stg[jj % 4]
                    if jj >= 2:
                        P.wait("tensor", eEV, b0["eEV"] + jj - 1)
                    for fc in range(8):
                        if fc == 7:
                            eST.n += 1
                        P.on("tensor", lambda t, wsb=wsb, fc=fc, dp=dp, pj=pj: (
                            t.matmul(out=pj[:],
                                     lhsT=wsb[:, fc * E + dp * 128: fc * E + dp * 128 + 128],
                                     rhs=xlnt[:, fc * 512:(fc + 1) * 512],
                                     start=(fc == 0), stop=(fc == 7)).then_inc(eST.h, 1)
                            if fc == 7 else
                            t.matmul(out=pj[:],
                                     lhsT=wsb[:, fc * E + dp * 128: fc * E + dp * 128 + 128],
                                     rhs=xlnt[:, fc * 512:(fc + 1) * 512],
                                     start=(fc == 0), stop=(fc == 7))))
                    P.wait("scalar", eST, eST.n)
                    if jj >= 4:
                        P.wait("scalar", DS4[jj % 4], stg_dma[jj - 4])
                    eEV.n += 1
                    P.on("scalar", lambda s, pj=pj, sgt=sgt, bcoff=bcoff, dp=dp: s.activation(
                        out=sgt[:], in_=pj[:], func=AF.Identity,
                        bias=bcols[:, bcoff + dp: bcoff + dp + 1]).then_inc(eEV.h, 1))
                    P.wait("sync", eEV, eEV.n)
                    base = ch * 512
                    _dma(P, "sync", dstt[2 * dp, :, base:base + 512], sgt[0:64, :],
                         q=DS4[jj % 4])
                    stg_dma[jj] = _dma(P, "sync", dstt[2 * dp + 1, :, base:base + 512],
                                       sgt[64:128, :], q=DS4[jj % 4])
                    nstg += 1
            chunk_est[ch] = eST.n
        barrier(P)
        P.emit()

    with ExitStack() as st:
        en = st.enter_context
        nbuf = en(nc.sbuf_tensor("nbuf", [64, S], BF16))
        nps = en(nc.sbuf_tensor("nps", [1, 512], F32))
        pm_sb = en(nc.sbuf_tensor("pm_sb", [H, WL], F32))
        t16 = en(nc.sbuf_tensor("t16", [H, WL], F32))
        t16b = en(nc.sbuf_tensor("t16b", [H, WL], F32))
        pnp = [en(nc.psum_tensor(f"pnp{i}", [128, 512], F32)) for i in range(2)]
        psc = en(nc.psum_tensor("psc", [128, 512], F32))

        P = Prog(nc)
        c2, c3 = sm["c2"], sm["c3"]
        eST = sm["eST"]

        def vstep(fn):
            c2.n += 1
            P.on("vector", lambda v, f=fn: f(v).then_inc(c2.h, 1))
            P.wait("vector", c2, c2.n)

        def sstep(fn):
            P.wait("scalar", c2, c2.n)
            c3.n += 1
            P.on("scalar", lambda s, f=fn: f(s).then_inc(c3.h, 1))
            P.wait("vector", c3, c3.n)

        for src, ncol, ncols in [(kts, 1, S), (qts, 0, ROWS)]:
            for h in range(H):
                P.wait("sync", c2, c2.n)
                _dma(P, "sync", nbuf[:, 0:ncols], src[h])
                P.wait("vector", dS, dS.n)
                vstep(lambda v, n=ncols: v.tensor_tensor(
                    out=nbuf[:, 0:n], in0=nbuf[:, 0:n], in1=nbuf[:, 0:n], op=OP.mult))
                for j in range(ncols // 512):
                    pj = pnp[j % 2]
                    P.wait("tensor", c2, c2.n)
                    eST.n += 1
                    P.on("tensor", lambda t, pj=pj, j=j: t.matmul(
                        out=pj[0:1, :], lhsT=ones_b[:], rhs=nbuf[:, j * 512:(j + 1) * 512],
                        start=True, stop=True).then_inc(eST.h, 1))
                    P.wait("vector", eST, eST.n)
                    vstep(lambda v, pj=pj: v.tensor_copy(out=nps[:], in_=pj[0:1, :]))
                    vstep(lambda v: v.reduce_max(out=nps[0:1, 0:1], in_=nps[:], axis=AX.X))
                    vstep(lambda v, c=ncol: v.tensor_tensor(
                        out=ncur[0:1, c:c + 1], in0=ncur[0:1, c:c + 1],
                        in1=nps[0:1, 0:1], op=OP.max))

        P.wait("sync", c2, c2.n)
        _dma(P, "sync", pm_sb[:], pmt)
        P.wait("vector", dS, dS.n)
        MAGIC = 12582912.0
        vstep(lambda v: v.tensor_scalar(out=t16[:], in0=pm_sb[:], scalar1=1.0 / TWO_PI,
                                        scalar2=MAGIC, op0=OP.mult, op1=OP.add))
        vstep(lambda v: v.tensor_scalar(out=t16[:], in0=t16[:], scalar1=-MAGIC,
                                        scalar2=None, op0=OP.add))
        vstep(lambda v: v.scalar_tensor_tensor(out=t16[:], in0=t16[:], scalar=-TWO_PI,
                                               in1=pm_sb[:], op0=OP.mult, op1=OP.add))
        sstep(lambda s: s.activation(out=t16b[:], in_=t16[:], func=AF.Sin, scale=0.5))
        vstep(lambda v: v.tensor_tensor(out=t16[:], in0=t16b[:], in1=t16b[:], op=OP.mult))
        vstep(lambda v: v.tensor_scalar(out=t16[:], in0=t16[:], scalar1=-2.0, scalar2=1.0,
                                        op0=OP.mult, op1=OP.add))
        vstep(lambda v: v.tensor_tensor(out=t16b[:], in0=t16[:], in1=t16[:], op=OP.mult))
        vstep(lambda v: v.tensor_scalar(out=t16b[:], in0=t16b[:], scalar1=0.125,
                                        scalar2=None, op0=OP.mult))
        P.wait("sync", c2, c2.n)
        _dma(P, "sync", flatr.rearrange("(h c) -> h c", c=WL), t16b[:])
        P.wait("sync", dS, dS.n)
        _dma(P, "sync", arow[0:1, 0:256], flatr.unsqueeze(0))
        P.wait("vector", dS, dS.n)
        vstep(lambda v: v.reduce_max(out=misc[0:1, 4:5], in_=arow[0:1, 0:256], axis=AX.X))
        vstep(lambda v: v.tensor_tensor(out=misc[0:1, 5:6], in0=ncur[0:1, 0:1],
                                        in1=ncur[0:1, 1:2], op=OP.mult))
        sstep(lambda s: s.activation(out=misc[0:1, 6:7], in_=misc[0:1, 5:6], func=AF.Ln))
        P.wait("scalar", c3, c3.n)
        c3.n += 1
        P.on("scalar", lambda s: s.activation(out=misc[0:1, 2:3], in_=misc[0:1, 6:7],
                                              func=AF.Exp, scale=0.5).then_inc(c3.h, 1))
        P.wait("vector", c3, c3.n)
        vstep(lambda v: v.tensor_tensor(out=misc[0:1, 7:8], in0=misc[0:1, 4:5],
                                        in1=misc[0:1, 2:3], op=OP.mult))
        vstep(lambda v: v.tensor_tensor(out=misc[0:1, 8:9], in0=misc[0:1, 7:8],
                                        in1=misc[0:1, 7:8], op=OP.mult))
        sstep(lambda s: s.activation(out=misc[0:1, 9:10], in_=misc[0:1, 8:9], func=AF.Sin,
                                     scale=KERR / 2.0))
        vstep(lambda v: v.tensor_tensor(out=misc[0:1, 10:11], in0=misc[0:1, 9:10],
                                        in1=misc[0:1, 9:10], op=OP.mult))
        vstep(lambda v: v.tensor_scalar(out=misc[0:1, 3:4], in0=misc[0:1, 10:11],
                                        scalar1=-2.0, scalar2=1.0, op0=OP.mult, op1=OP.add))
        vstep(lambda v: v.tensor_scalar(out=arow[0:1, 0:256], in0=arow[0:1, 0:256],
                                        scalar1=misc[0:1, 3:4], scalar2=None, op0=OP.mult))
        vstep(lambda v: v.tensor_scalar(out=arow[0:1, 256:512], in0=arow[0:1, 0:256],
                                        scalar1=misc[0:1, 2:3], scalar2=-1.0,
                                        op0=OP.mult, op1=OP.mult))
        P.wait("tensor", c2, c2.n)
        eST.n += 1
        P.on("tensor", lambda t: t.matmul(out=psc[:], lhsT=ones_f[:], rhs=arow[:],
                                          start=True, stop=True).then_inc(eST.h, 1))
        P.wait("vector", eST, eST.n)
        vstep(lambda v: v.tensor_copy(out=sccol[:], in_=psc[:]))
        barrier(P)
        P.emit()

    if nphase < 2:
        NH_RUN = 0
    else:
        NH_RUN = int(os.environ.get("MK_NH", str(H)))
    A3CUT = int(os.environ.get("MK_A3CUT", "5"))
    with ExitStack() as st:
        en = st.enter_context
        spr = en(nc.sbuf_tensor("spr", [128, 3 * 8192], F32))
        pbuf = en(nc.sbuf_tensor("pbuf", [128, 2 * 32 * NQ], BF16))
        ktb = [en(nc.sbuf_tensor(f"ktb{i}", [64, S], BF16)) for i in range(2)]
        qtbu = [en(nc.sbuf_tensor(f"qtbu{i}", [64, ROWS], BF16)) for i in range(2)]
        vtb1 = en(nc.sbuf_tensor("vtb0", [128, S // 2], BF16))
        vtb = [vtb1, vtb1]
        vsb = [en(nc.sbuf_tensor(f"vsb{i}", [128, NT * 65], BF16)) for i in range(2)]
        accf = [en(nc.sbuf_tensor(f"accf{i}", [64, NQ], F32)) for i in range(2)]
        rpair = en(nc.sbuf_tensor("rpair", [1, 512], F32))
        osb = en(nc.sbuf_tensor("osb", [64, NQ], F32))
        tmpf = en(nc.sbuf_tensor("tmpf", [64, NQ], F32))
        stp = [en(nc.psum_tensor(f"stp{i}", [128, 512], F32)) for i in range(3)]
        avp = [en(nc.psum_tensor(f"avp{i}", [128, 512], F32)) for i in range(3)]
        rbcp = en(nc.psum_tensor("rbcp", [128, 512], F32))
        vtpp = [en(nc.psum_tensor("vtpp0", [128, 256], BF16))] * 2

        P = Prog(nc)
        eST, eEV, eXP, eAQ, eFN = (sm[x] for x in ("eST", "eEV", "eXP", "eAQ", "eFN"))
        rRQ, rBC, rTM = sm["rRQ"], sm["rBC"], sm["rTM"]
        pVT, vSB, cAC, c2 = sm["pVT"], sm["vSB"], sm["cAC"], sm["c2"]
        A = {x: sm[x].n for x in ("eST", "eEV", "eXP", "eAQ", "eFN", "rRQ", "rBC",
                                  "rTM", "pVT", "vSB", "c2", "cAC")}
        dS_kv = {}
        acc_dma = {}
        vsb_done = {}

        for b_ in range(2):
            P.on("vector", lambda v, b_=b_: v.memset(
                vsb[b_].rearrange("p (t c) -> p t c", c=65)[:, :, 64:65], 1.0))

        for h in range(NH_RUN):
            if h >= 2:
                P.wait("sync", eST, A["eST"] + (h - 1) * 256)
            if h >= 1 and A3CUT >= 3:
                P.wait("sync", pVT, A["pVT"] + h * (NT // 2))
            _dma(P, "sync", ktb[h % 2][:], kts[h], q=DK[h % 2])
            _dma(P, "sync", qtbu[h % 2][:], qts[h], q=DK[h % 2])
            vsrc = vts[h].rearrange("d (u two j) -> d two u j", two=2, j=128)
            _dma(P, "sync", vtb[h % 2][0:64, :].rearrange("d (u j) -> d u j", j=128),
                 vsrc[:, 0, :, :], q=DK[h % 2])
            dS_kv[h] = _dma(P, "sync",
                            vtb[h % 2][64:128, :].rearrange("d (u j) -> d u j", j=128),
                            vsrc[:, 1, :, :], q=DK[h % 2])

        def emit_fills(i):
            h, qc = divmod(i, QCN)
            P.wait("tensor", DK[h % 2], dS_kv[h])
            for f in range(32):
                gf = i * 32 + f
                pj = stp[gf % 3]
                if gf >= 3:
                    P.wait("tensor", eEV, A["eEV"] + gf - 2)
                for tl in range(2):
                    t_ = f * 2 + tl
                    if tl == 1:
                        eST.n += 1
                    P.on("tensor", lambda t, h=h, t_=t_, tl=tl, pj=pj, qc=qc: (
                        t.matmul(out=pj[:, (t_ % 2) * 256:(t_ % 2) * 256 + 256],
                                 lhsT=ktb[h % 2][:, t_ * 128:(t_ + 1) * 128],
                                 rhs=qtbu[h % 2][:, qc * NQ:(qc + 1) * NQ],
                                 start=True, stop=True).then_inc(eST.h, 1)
                        if tl == 1 else
                        t.matmul(out=pj[:, (t_ % 2) * 256:(t_ % 2) * 256 + 256],
                                 lhsT=ktb[h % 2][:, t_ * 128:(t_ + 1) * 128],
                                 rhs=qtbu[h % 2][:, qc * NQ:(qc + 1) * NQ],
                                 start=True, stop=True)))
                P.wait("vector", eST, eST.n)
                g = i * 2 + f // 16
                if g >= 3 and A3CUT >= 2:
                    P.wait("vector", eXP, A["eXP"] + 16 * (g - 2))
                eEV.n += 1
                sg = g % 3
                P.on("vector", lambda v, pj=pj, f=f, sg=sg: v.tensor_copy(
                    out=spr[:, sg * 8192 + (f % 16) * 512: sg * 8192 + (f % 16) * 512 + 512],
                    in_=pj[:]).then_inc(eEV.h, 1))

        def emit_vbuild(h):
            b_ = h % 2
            P.wait("tensor", DK[h % 2], dS_kv[h])
            if h >= 2:
                if A3CUT >= 4:
                    P.wait("tensor", eAQ, A["eAQ"] + (h - 1) * 256)
                if A3CUT >= 5:
                    P.wait("tensor", rTM, A["rTM"] + (h - 1) * 8 * 16)
            for u in range(NT // 2):
                vv = vtpp[u % 2][:, 0:128]
                P.wait("tensor", vSB, vSB.n)
                pVT.n += 1
                P.on("tensor", lambda t, b_=b_, u=u, vv=vv: t.transpose(
                    out=vv, in_=vtb[b_][:, u * 128:(u + 1) * 128],
                    identity=idb_sb[:]).then_inc(pVT.h, 1))
                P.wait("vector", pVT, pVT.n)
                for par in range(2):
                    t_ = 2 * u + par
                    vSB.n += 1
                    P.on("vector", lambda v, b_=b_, t_=t_, vv=vv, par=par: v.tensor_copy(
                        out=vsb[b_][:, t_ * 65: t_ * 65 + 64],
                        in_=vv[:, par * 64:par * 64 + 64]).then_inc(vSB.h, 1))
            vsb_done[h] = vSB.n

        EFN_REQ = {0: -2, 1: -2, 2: 0, 3: 0, 4: -4, 5: -4, 6: 2, 7: 2,
                   8: 4, 9: 4, 10: 6, 11: 6, 12: 8, 13: 8, 14: 10, 15: 10}

        def emit_av(i, ch):
            h, qc = divmod(i, QCN)
            if A3CUT >= 5:
                P.wait("tensor", eFN, A["eFN"] + i * 16 + EFN_REQ[ch])
            if A3CUT >= 3:
                P.wait("tensor", vSB, vsb_done[h])
            for hh in range(2):
                idx = i * 32 + ch * 2 + hh
                slot = idx % 2
                P.wait("tensor", eXP, A["eXP"] + idx + 1)
                for tl in range(32):
                    t_ = hh * 32 + tl
                    if tl == 31:
                        eAQ.n += 1
                    P.on("tensor", lambda t, h=h, ch=ch, t_=t_, tl=tl, slot=slot: (
                        t.matmul(out=avp[(ch // 2) % 3][0:65, (ch % 2) * 256:(ch % 2) * 256 + 256],
                                 lhsT=vsb[h % 2][:, t_ * 65:(t_ + 1) * 65],
                                 rhs=pbuf[:, slot * 8192 + tl * 256: slot * 8192 + tl * 256 + 256],
                                 start=(t_ == 0), stop=(t_ == 63)).then_inc(eAQ.h, 1)
                        if tl == 31 else
                        t.matmul(out=avp[(ch // 2) % 3][0:65, (ch % 2) * 256:(ch % 2) * 256 + 256],
                                 lhsT=vsb[h % 2][:, t_ * 65:(t_ + 1) * 65],
                                 rhs=pbuf[:, slot * 8192 + tl * 256: slot * 8192 + tl * 256 + 256],
                                 start=(t_ == 0), stop=(t_ == 63))))
            if ch % 2 == 1 and ch <= 9:
                emit_rbc(i, ch // 2)

        def emit_rbc(i, p):
            g = i * 8 + p
            P.wait("tensor", rRQ, A["rRQ"] + g + 1)
            P.wait("tensor", rTM, A["rTM"] + 2 * g)
            rBC.n += 1
            P.on("tensor", lambda t: t.matmul(
                out=rbcp[0:64, 0:512], lhsT=ones_f[0:1, 0:64], rhs=rpair[:],
                start=True, stop=True).then_inc(rBC.h, 1))

        def emit_finish(i, p):
            che, cho = 2 * p, 2 * p + 1
            pt = avp[p % 3]
            g = i * 8 + p
            if p == 0 and i > 1:
                P.wait("vector", dG, acc_dma[i - 2])
            P.wait("vector", eAQ, A["eAQ"] + i * 32 + cho * 2 + 2)
            P.wait("vector", rBC, A["rBC"] + g)
            for chl, col in ((che, 0), (cho, 256)):
                c2.n += 1
                P.on("vector", lambda v, pt=pt, col=col: v.reciprocal(
                    out=rpair[0:1, col:col + 256],
                    in_=pt[64:65, col:col + 256]).then_inc(c2.h, 1))
                P.wait("vector", c2, c2.n)
                if chl == cho:
                    rRQ.n += 1
                P.on("vector", lambda v, chl=chl, col=col, last=(chl == cho): (
                    v.tensor_scalar(out=rpair[0:1, col:col + 256],
                                    in0=rpair[0:1, col:col + 256],
                                    scalar1=iwd[0:1, chl:chl + 1], scalar2=None,
                                    op0=OP.mult).then_inc(rRQ.h, 1)
                    if last else
                    v.tensor_scalar(out=rpair[0:1, col:col + 256],
                                    in0=rpair[0:1, col:col + 256],
                                    scalar1=iwd[0:1, chl:chl + 1], scalar2=None,
                                    op0=OP.mult)))
            P.wait("vector", rBC, A["rBC"] + g + 1)
            for chl, col in ((che, 0), (cho, 256)):
                c2.n += 1
                P.on("vector", lambda v, pt=pt, col=col: v.tensor_copy(
                    out=osb[:], in_=pt[0:64, col:col + 256]).then_inc(c2.h, 1))
                P.wait("vector", c2, c2.n)
                rTM.n += 1
                P.on("vector", lambda v, col=col: v.tensor_tensor(
                    out=tmpf[:], in0=osb[:], in1=rbcp[0:64, col:col + 256],
                    op=OP.mult).then_inc(rTM.h, 1))
                P.wait("vector", rTM, rTM.n)
                eFN.n += 1
                if chl == 0:
                    P.on("vector", lambda v, i=i: v.tensor_copy(
                        out=accf[i % 2][:], in_=tmpf[:]).then_inc(eFN.h, 1))
                else:
                    P.on("vector", lambda v, i=i: v.tensor_tensor(
                        out=accf[i % 2][:], in0=accf[i % 2][:], in1=tmpf[:],
                        op=OP.add).then_inc(eFN.h, 1))

        NITEM = NH_RUN * QCN
        if NITEM:
            if A3CUT >= 3:
                emit_vbuild(0)
            emit_fills(0)
        for i in range(NITEM):
            h, qc = divmod(i, QCN)
            for ch in range(16) if A3CUT >= 2 else []:
                for hh in range(2):
                    idx = i * 32 + ch * 2 + hh
                    slot = idx % 2
                    if ch == 0:
                        P.wait("scalar", eEV, A["eEV"] + i * 32 + (hh + 1) * 16)
                    if idx >= 2 and A3CUT >= 4:
                        P.wait("scalar", eAQ, A["eAQ"] + idx - 1)
                    eXP.n += 1
                    hc = h * 16 + ch
                    sg = (i * 2 + hh) % 3
                    P.on("scalar", lambda s, sg=sg, slot=slot, hc=hc: s.activation(
                        out=pbuf[:, slot * 8192:(slot + 1) * 8192],
                        in_=spr[:, sg * 8192:(sg + 1) * 8192], func=AF.Exp,
                        scale=sccol[:, hc:hc + 1],
                        bias=sccol[:, 256 + hc:256 + hc + 1]).then_inc(eXP.h, 1))
            if A3CUT >= 4:
                for ch in range(12):
                    emit_av(i, ch)
                if A3CUT >= 5:
                    for p in range(5):
                        emit_finish(i, p)
                if i + 1 < NITEM:
                    emit_fills(i + 1)
                    if qc == QCN - 1 and A3CUT >= 3:
                        emit_vbuild(h + 1)
                for ch in range(12, 16):
                    emit_av(i, ch)
            else:
                if i + 1 < NITEM:
                    emit_fills(i + 1)
                    if qc == QCN - 1 and A3CUT >= 3:
                        emit_vbuild(h + 1)
            if A3CUT >= 5:
                for p in range(5, 8):
                    emit_rbc(i, p)
                for p in range(5, 8):
                    emit_finish(i, p)
            if A3CUT < 5:
                continue
            P.wait("gpsimd", eFN, eFN.n)
            acc_dma[i] = _dma(P, "gpsimd",
                              accs[h * 64:(h + 1) * 64, qc * NQ:(qc + 1) * NQ],
                              accf[i % 2][:])
        barrier(P)
        P.emit()

    stackA.close()
    xres = outer.enter_context(nc.sbuf_tensor("xres", [128, NRT * E], F32))

    a4_on = nphase >= 3 and NH_RUN == H
    if a4_on:
     with ExitStack() as st:
        en = st.enter_context
        acct = en(nc.sbuf_tensor("acct", [128, 8 * ROWS], BF16))
        wos = en(nc.sbuf_tensor("wosA", [128, 8 * E], BF16))
        otT = en(nc.sbuf_tensor("otT", [128, 8 * ROWS], F32))
        xst = [en(nc.sbuf_tensor(f"xstA{i}", [128, E], F32)) for i in range(2)]
        ppj = [en(nc.psum_tensor(f"ppjA{i}", [128, 512], F32)) for i in range(2)]

        P = Prog(nc)
        c0, c2, c3, c4 = (sm[x] for x in ("c0", "c2", "c3", "c4"))
        eST, eEV, pTR, c5 = sm["eST"], sm["eEV"], sm["pTR"], sm["c5"]
        A4 = {x: sm[x].n for x in ("eST", "eEV", "pTR", "c5", "c0")}

        _dma(P, "gpsimd", wos[:].rearrange("p (c n) -> p c n", n=E), wo_a.rearrange("(c p) n -> p c n", p=128))
        _dma(P, "sync", acct[:].rearrange("p (c n) -> p c n", n=ROWS), accs.rearrange("(c p) n -> p c n", p=128))
        ldT = dS.n
        P.wait("tensor", dS, ldT)
        P.wait("tensor", dG, dG.n)
        nj = 0
        for dp in range(8):
            for rc in range(4):
                pj = ppj[nj % 2]
                if nj >= 2:
                    P.wait("tensor", eEV, A4["eEV"] + nj - 1)
                for fc in range(8):
                    if fc == 7:
                        eST.n += 1
                    P.on("tensor", lambda t, fc=fc, dp=dp, rc=rc, pj=pj: (
                        t.matmul(out=pj[:],
                                 lhsT=wos[:, fc * E + dp * 128: fc * E + dp * 128 + 128],
                                 rhs=acct[:, fc * ROWS + rc * 512: fc * ROWS + rc * 512 + 512],
                                 start=(fc == 0), stop=(fc == 7)).then_inc(eST.h, 1)
                        if fc == 7 else
                        t.matmul(out=pj[:],
                                 lhsT=wos[:, fc * E + dp * 128: fc * E + dp * 128 + 128],
                                 rhs=acct[:, fc * ROWS + rc * 512: fc * ROWS + rc * 512 + 512],
                                 start=(fc == 0), stop=(fc == 7))))
                P.wait("scalar", eST, eST.n)
                eEV.n += 1
                P.on("scalar", lambda s, pj=pj, dp=dp, rc=rc: s.activation(
                    out=otT[:, dp * ROWS + rc * 512: dp * ROWS + rc * 512 + 512],
                    in_=pj[:], func=AF.Identity,
                    bias=bcols[:, BC["bo_a"] + dp: BC["bo_a"] + dp + 1]).then_inc(eEV.h, 1))
                nj += 1
        for rt in range(NRT):
            if rt >= 2:
                P.wait("sync", c0, A4["c0"] + (rt - 1) * 8)
            dtot = _dma(P, "sync", xst[rt % 2][:], xb[rt * 128:(rt + 1) * 128, :],
                        q=DX[rt % 2])
            P.wait("tensor", eEV, eEV.n)
            P.wait("vector", DX[rt % 2], dtot)
            for fc in range(8):
                gt = rt * 8 + fc
                pj = ppj[gt % 2]
                if gt >= 2:
                    P.wait("tensor", c0, A4["c0"] + gt - 1)
                pTR.n += 1
                P.on("tensor", lambda t, fc=fc, rt=rt, pj=pj: t.transpose(
                    out=pj[:, 0:128], in_=otT[:, fc * ROWS + rt * 128: fc * ROWS + rt * 128 + 128],
                    identity=idf_sb[:]).then_inc(pTR.h, 1))
                P.wait("vector", pTR, pTR.n)
                c0.n += 1
                P.on("vector", lambda v, fc=fc, rt=rt, pj=pj: v.tensor_tensor(
                    out=xres[:, rt * E + fc * 128: rt * E + fc * 128 + 128],
                    in0=xst[rt % 2][:, fc * 128:(fc + 1) * 128], in1=pj[:, 0:128],
                    op=OP.add).then_inc(c0.h, 1))
        barrier(P)
        P.emit()

    def emit_dequant():
        with ExitStack() as st:
            en = st.enter_context
            qw = en(nc.sbuf_tensor("qw", [128, FF], I32))
            srow = en(nc.sbuf_tensor("srow", [1, FF], F32))
            scf = en(nc.sbuf_tensor("scf", [128, FF], F32))
            tqf = en(nc.sbuf_tensor("tqf", [128, FF], F32))
            wbb = en(nc.sbuf_tensor("wbb", [128, FF], BF16))
            pbq = [en(nc.psum_tensor(f"pbq{i}", [128, 512], F32)) for i in range(2)]

            P = Prog(nc)
            c2, eST = sm["c2"], sm["eST"]
            for w_src, s_src, wd, nkc, width, nw in [
                    (w1q, s1, w1ds, 8, FF, 32), (w2q, s2, w2ds, 32, E, 8)]:
                for kc in range(nkc):
                    P.wait("sync", dG, dG.n)
                    _dma(P, "sync", qw[:, 0:width], w_src[kc * 128:(kc + 1) * 128, :])
                    _dma(P, "sync", srow[0:1, 0:width], s_src[kc:kc + 1, :])
                    dtot = dS.n
                    P.wait("tensor", dS, dtot)
                    P.wait("tensor", c2, c2.n)
                    base_c2 = c2.n
                    for j in range(width // 512):
                        pj = pbq[j % 2]
                        if j >= 2:
                            P.wait("tensor", c2, base_c2 + j - 1)
                        eST.n += 1
                        P.on("tensor", lambda t, pj=pj, j=j: t.matmul(
                            out=pj[:], lhsT=ones_f[:], rhs=srow[0:1, j * 512:(j + 1) * 512],
                            start=True, stop=True).then_inc(eST.h, 1))
                        P.wait("vector", eST, eST.n)
                        c2.n += 1
                        P.on("vector", lambda v, pj=pj, j=j: v.tensor_copy(
                            out=scf[:, j * 512:(j + 1) * 512], in_=pj[:]).then_inc(c2.h, 1))
                    P.wait("vector", dS, dtot)
                    c2.n += 1
                    P.on("vector", lambda v, width=width: v.tensor_scalar(
                        out=tqf[:, 0:width], in0=qw[:, 0:width], scalar1=-128.0,
                        scalar2=None, op0=OP.add).then_inc(c2.h, 1))
                    P.wait("vector", c2, c2.n)
                    c2.n += 1
                    P.on("vector", lambda v, width=width: v.tensor_tensor(
                        out=wbb[:, 0:width], in0=tqf[:, 0:width], in1=scf[:, 0:width],
                        op=OP.mult).then_inc(c2.h, 1))
                    P.wait("gpsimd", c2, c2.n)
                    for oc in range(nw):
                        _dma(P, "gpsimd", wd[oc, :, kc, :], wbb[:, oc * 128:(oc + 1) * 128])
            barrier(P)
            P.emit()

    def emit_ffn(tag):
        with ExitStack() as st:
            en = st.enter_context
            ybuf = en(nc.sbuf_tensor(f"ybuf{tag}", [128, 4 * E], F32))
            cb = en(nc.sbuf_tensor(f"cb{tag}", [128, E], F32))
            sqb2 = en(nc.sbuf_tensor(f"sqb2{tag}", [128, E], F32))
            hrow = en(nc.sbuf_tensor(f"hrow{tag}", [128, E], BF16))
            hT = en(nc.sbuf_tensor(f"hT{tag}", [128, 8 * 512], BF16))
            y1T = en(nc.sbuf_tensor(f"y1T{tag}", [128, 32 * 512], BF16))
            zt = en(nc.sbuf_tensor(f"zt{tag}", [128, 8 * 512], F32))
            w1sb = [en(nc.sbuf_tensor(f"w1sb{tag}{i}", [128, 8 * 128], BF16)) for i in range(2)]
            w2sb = [en(nc.sbuf_tensor(f"w2sb{tag}{i}", [128, 32 * 128], BF16)) for i in range(2)]
            mcol = en(nc.sbuf_tensor(f"mcolF{tag}", [128, 4], F32))
            ppf = [en(nc.psum_tensor(f"ppf{tag}{i}", [128, 512], F32)) for i in range(2)]

            P = Prog(nc)
            c0, c2, c3, c4, c5 = (sm[x] for x in ("c0", "c2", "c3", "c4", "c5"))
            eST, eEV, pTR, pPJ = sm["eST"], sm["eEV"], sm["pTR"], sm["pPJ"]
            FB = {x: sm[x].n for x in ("eST", "eEV", "pTR", "c5", "c0", "pPJ")}
            w1dma = {}
            w2dma = {}
            hT_ptr = {}

            for rc in range(4):
                for r in range(4):
                    rt = rc * 4 + r
                    xsl = xres[:, rt * E:(rt + 1) * E]
                    ysl = ybuf[:, r * E:(r + 1) * E]
                    emit_ln(P, mcol, xsl, ysl, sqb2[:], c2, c3, c4)
                    c2.n += 1
                    P.on("vector", lambda v, ysl=ysl: v.tensor_scalar(
                        out=ysl, in0=ysl, scalar1=mcol[:, 3:4], scalar2=None,
                        op0=OP.mult).then_inc(c2.h, 1))
                    P.wait("vector", c2, c2.n)
                    c2.n += 1
                    P.on("vector", lambda v, xsl=xsl, ysl=ysl: v.tensor_tensor(
                        out=xsl, in0=xsl, in1=ysl, op=OP.add).then_inc(c2.h, 1))
                    P.wait("vector", c2, c2.n)
                    emit_ln(P, mcol, ysl, cb[:], sqb2[:], c2, c3, c4)
                    gr = rc * 4 + r
                    if gr >= 1:
                        P.wait("vector", pTR, hT_ptr[gr - 1])
                    c0.n += 1
                    P.on("vector", lambda v: v.tensor_scalar(
                        out=hrow[:], in0=cb[:], scalar1=mcol[:, 3:4], scalar2=None,
                        op0=OP.mult).then_inc(c0.h, 1))
                    P.wait("tensor", c0, c0.n)
                    if r == 0 and rc > 0:
                        P.wait("tensor", c0, FB["c0"] + rc * 36)
                    for fc in range(8):
                        gl = r * 8 + fc
                        pj = ppf[fc % 2]
                        if gl >= 2 or r > 0:
                            P.wait("tensor", c5, c5.n - 1)
                        pTR.n += 1
                        P.on("tensor", lambda t, fc=fc, pj=pj: t.transpose(
                            out=pj.bitcast(BF16)[:, 0:128], in_=hrow[:, fc * 128:(fc + 1) * 128],
                            identity=idb_sb[:]).then_inc(pTR.h, 1))
                        P.wait("scalar", pTR, pTR.n)
                        c5.n += 1
                        P.on("scalar", lambda s, pj=pj, fc=fc, r=r: s.activation(
                            out=hT[:, fc * 512 + r * 128: fc * 512 + r * 128 + 128],
                            in_=pj.bitcast(BF16)[:, 0:128], func=AF.Identity,
                            scale=bcols[:, BC["f_g"] + fc: BC["f_g"] + fc + 1],
                            bias=bcols[:, BC["f_b"] + fc: BC["f_b"] + fc + 1]).then_inc(c5.h, 1))
                        hT_ptr[gr] = pTR.n
                P.wait("tensor", c5, c5.n)
                for oc in range(32):
                    wb = w1sb[oc % 2]
                    g1 = rc * 32 + oc
                    if g1 >= 2:
                        rcp, ocp = divmod(g1 - 2, 32)
                        P.wait("sync", eEV, FB["eEV"] + rcp * 40 + ocp + 1)
                    w1dma[(rc, oc)] = _dma(P, "sync", wb[:],
                                           w1ds[oc].rearrange("p c n -> p (c n)"),
                                           q=DW[oc % 2])
                    pj = ppf[oc % 2]
                    P.wait("tensor", DW[oc % 2], w1dma[(rc, oc)])
                    if oc >= 2:
                        P.wait("tensor", eEV, FB["eEV"] + rc * 40 + oc - 1)
                    for kc in range(8):
                        if kc == 7:
                            eST.n += 1
                        P.on("tensor", lambda t, wb=wb, kc=kc, pj=pj: (
                            t.matmul(out=pj[:], lhsT=wb[:, kc * 128:(kc + 1) * 128],
                                     rhs=hT[:, kc * 512:(kc + 1) * 512],
                                     start=(kc == 0), stop=(kc == 7)).then_inc(eST.h, 1)
                            if kc == 7 else
                            t.matmul(out=pj[:], lhsT=wb[:, kc * 128:(kc + 1) * 128],
                                     rhs=hT[:, kc * 512:(kc + 1) * 512],
                                     start=(kc == 0), stop=(kc == 7))))
                    P.wait("scalar", eST, eST.n)
                    eEV.n += 1
                    P.on("scalar", lambda s, pj=pj, oc=oc: s.activation(
                        out=y1T[:, oc * 512:(oc + 1) * 512], in_=pj[:], func=AF.Silu,
                        bias=fb1c[:, oc:oc + 1]).then_inc(eEV.h, 1))
                for oc in range(8):
                    wb = w2sb[oc % 2]
                    g2 = rc * 8 + oc
                    if g2 >= 2:
                        rcp, ocp = divmod(g2 - 2, 8)
                        P.wait("sync", eEV, FB["eEV"] + rcp * 40 + 32 + ocp + 1)
                    w2dma[(rc, oc)] = _dma(P, "sync", wb[:],
                                           w2ds[oc].rearrange("p c n -> p (c n)"),
                                           q=DW[oc % 2])
                    pj = ppf[oc % 2]
                    P.wait("tensor", DW[oc % 2], w2dma[(rc, oc)])
                    if oc >= 2:
                        P.wait("tensor", eEV, FB["eEV"] + rc * 40 + 32 + oc - 1)
                    else:
                        P.wait("tensor", eEV, FB["eEV"] + rc * 40 + 32)
                    for kc in range(32):
                        if kc == 31:
                            eST.n += 1
                        P.on("tensor", lambda t, wb=wb, kc=kc, pj=pj: (
                            t.matmul(out=pj[:], lhsT=wb[:, kc * 128:(kc + 1) * 128],
                                     rhs=y1T[:, kc * 512:(kc + 1) * 512],
                                     start=(kc == 0), stop=(kc == 31)).then_inc(eST.h, 1)
                            if kc == 31 else
                            t.matmul(out=pj[:], lhsT=wb[:, kc * 128:(kc + 1) * 128],
                                     rhs=y1T[:, kc * 512:(kc + 1) * 512],
                                     start=(kc == 0), stop=(kc == 31))))
                    P.wait("scalar", eST, eST.n)
                    eEV.n += 1
                    P.on("scalar", lambda s, pj=pj, oc=oc: s.activation(
                        out=zt[:, oc * 512:(oc + 1) * 512], in_=pj[:], func=AF.Identity,
                        scale=0.5, bias=fb2c[:, oc:oc + 1]).then_inc(eEV.h, 1))
                P.wait("tensor", eEV, eEV.n)
                for r in range(4):
                    rt = rc * 4 + r
                    for fc in range(8):
                        pj = ppf[fc % 2]
                        P.wait("tensor", c0, c0.n - 1)
                        pTR.n += 1
                        P.on("tensor", lambda t, fc=fc, r=r, pj=pj: t.transpose(
                            out=pj[:, 0:128], in_=zt[:, fc * 512 + r * 128: fc * 512 + r * 128 + 128],
                            identity=idf_sb[:]).then_inc(pTR.h, 1))
                        P.wait("vector", pTR, pTR.n)
                        c0.n += 1
                        P.on("vector", lambda v, fc=fc, rt=rt, pj=pj: v.tensor_tensor(
                            out=xres[:, rt * E + fc * 128: rt * E + fc * 128 + 128],
                            in0=xres[:, rt * E + fc * 128: rt * E + fc * 128 + 128],
                            in1=pj[:, 0:128], op=OP.add).then_inc(c0.h, 1))
            barrier(P)
            P.emit()

    def emit_attnb():
        c_outer = ExitStack()
        enO = c_outer.enter_context
        c12 = ExitStack()
        yT = c12.enter_context(nc.sbuf_tensor("yTC", [128, 8 * ROWS], BF16))

        with ExitStack() as st:
            en = st.enter_context
            ybf = en(nc.sbuf_tensor("ybfC", [128, E], F32))
            sqbC = en(nc.sbuf_tensor("sqbC", [128, E], F32))
            ylb = [en(nc.sbuf_tensor(f"ylbC{i}", [128, E], BF16)) for i in range(2)]
            mcol = en(nc.sbuf_tensor("mcolC", [128, 4], F32))
            ptc = [en(nc.psum_tensor(f"ptcC{i}", [128, 512], F32)) for i in range(2)]

            P = Prog(nc)
            c0, c2, c3, c4, c5 = (sm[x] for x in ("c0", "c2", "c3", "c4", "c5"))
            pTR = sm["pTR"]
            C1 = {x: sm[x].n for x in ("pTR", "c5")}
            for rt in range(NRT):
                xsl = xres[:, rt * E:(rt + 1) * E]
                yl = ylb[rt % 2]
                if rt >= 2:
                    P.wait("vector", pTR, C1["pTR"] + (rt - 1) * 8)
                emit_ln(P, mcol, xsl, ybf[:], sqbC[:], c2, c3, c4)
                c0.n += 1
                P.on("vector", lambda v, yl=yl: v.tensor_scalar(
                    out=yl[:], in0=ybf[:], scalar1=mcol[:, 3:4], scalar2=None,
                    op0=OP.mult).then_inc(c0.h, 1))
                P.wait("tensor", c0, c0.n)
                for fc in range(8):
                    gt = rt * 8 + fc
                    pj = ptc[gt % 2]
                    if gt >= 2:
                        P.wait("tensor", c5, C1["c5"] + gt - 1)
                    pTR.n += 1
                    P.on("tensor", lambda t, yl=yl, fc=fc, pj=pj: t.transpose(
                        out=pj.bitcast(BF16)[:, 0:128], in_=yl[:, fc * 128:(fc + 1) * 128],
                        identity=idb_sb[:]).then_inc(pTR.h, 1))
                    P.wait("vector", pTR, pTR.n)
                    c5.n += 1
                    P.on("vector", lambda v, pj=pj, fc=fc, rt=rt: v.tensor_copy(
                        out=yT[:, fc * ROWS + rt * 128: fc * ROWS + rt * 128 + 128],
                        in_=pj.bitcast(BF16)[:, 0:128]).then_inc(c5.h, 1))
            barrier(P)
            P.emit()

        with ExitStack() as st:
            en = st.enter_context
            rowb = en(nc.sbuf_tensor("rowbC", [128, NRT * E], BF16))
            wcb = en(nc.sbuf_tensor("wcbC", [128, 8 * E], BF16))
            stgc = [en(nc.sbuf_tensor(f"stgC{i}", [128, 512], BF16)) for i in range(2)]
            res2 = [en(nc.sbuf_tensor(f"resC{i}", [128, 512], BF16)) for i in range(2)]
            ppc = [en(nc.psum_tensor(f"ppcC{i}", [128, 512], F32)) for i in range(2)]
            ptc2 = [en(nc.psum_tensor(f"ptc2C{i}", [128, 512], F32)) for i in range(2)]

            P = Prog(nc)
            c0, c2, c5 = sm["c0"], sm["c2"], sm["c5"]
            eST, eEV, pTR, pPJ = sm["eST"], sm["eEV"], sm["pTR"], sm["pPJ"]
            C2 = {x: sm[x].n for x in ("eST", "eEV", "pTR", "c0", "c2", "c5", "pPJ")}
            WLIST = [("q", wq_b, 0, None), ("k0", wk_b0, 1, ksr), ("k1", wk_b1, 2, ksr),
                     ("v0", wv_b0, 3, vsr), ("v1", wv_b1, 4, vsr)]
            nev = 0
            ntr = 0
            res2_cons = {}
            for wi, (wn, wsrc, bidx, dstt) in enumerate(WLIST):
                if wi > 0:
                    P.wait("gpsimd", eST, eST.n)
                _dma(P, "gpsimd", wcb[:], wsrc.rearrange("(c p) n -> p c n", p=128))
                P.wait("tensor", dG, dG.n)
                for dp in range(8):
                    for rc in range(4):
                        jj = nev
                        pj = ppc[jj % 2]
                        sgt = stgc[jj % 2]
                        rs2 = res2[jj % 2]
                        if jj >= 2:
                            P.wait("tensor", eEV, C2["eEV"] + jj - 1)
                        for fc in range(8):
                            if fc == 7:
                                eST.n += 1
                            P.on("tensor", lambda t, fc=fc, dp=dp, rc=rc, pj=pj: (
                                t.matmul(out=pj[:],
                                         lhsT=wcb[:, fc * E + dp * 128: fc * E + dp * 128 + 128],
                                         rhs=yT[:, fc * ROWS + rc * 512: fc * ROWS + rc * 512 + 512],
                                         start=(fc == 0), stop=(fc == 7)).then_inc(eST.h, 1)
                                if fc == 7 else
                                t.matmul(out=pj[:],
                                         lhsT=wcb[:, fc * E + dp * 128: fc * E + dp * 128 + 128],
                                         rhs=yT[:, fc * ROWS + rc * 512: fc * ROWS + rc * 512 + 512],
                                         start=(fc == 0), stop=(fc == 7))))
                        P.wait("scalar", eST, eST.n)
                        if jj >= 2:
                            P.wait("scalar", c2, C2["c2"] + jj - 1)
                        eEV.n += 1
                        P.on("scalar", lambda s, pj=pj, sgt=sgt, bidx=bidx, dp=dp: s.activation(
                            out=sgt[:], in_=pj[:], func=AF.Identity,
                            bias=bkvc[:, bidx * 8 + dp: bidx * 8 + dp + 1]).then_inc(eEV.h, 1))
                        P.wait("vector", eEV, eEV.n)
                        if jj - 2 in res2_cons:
                            kind, cnt = res2_cons[jj - 2]
                            if kind == "ptr":
                                P.wait("vector", pTR, cnt)
                            else:
                                P.wait("vector", DS4[0], cnt)
                        c2.n += 1
                        P.on("vector", lambda v, sgt=sgt, rs2=rs2, dp=dp, rc=rc: v.tensor_tensor(
                            out=rs2[:], in0=sgt[:],
                            in1=yT[:, dp * ROWS + rc * 512: dp * ROWS + rc * 512 + 512],
                            op=OP.add).then_inc(c2.h, 1))
                        if wn == "q":
                            P.wait("sync", c2, c2.n)
                            _dma(P, "sync", qtb[2 * dp, :, rc * 512:(rc + 1) * 512], rs2[0:64, :],
                                 q=DS4[0])
                            res2_cons[jj] = ("dma", _dma(
                                P, "sync", qtb[2 * dp + 1, :, rc * 512:(rc + 1) * 512],
                                rs2[64:128, :], q=DS4[0]))
                        else:
                            P.wait("tensor", c2, c2.n)
                            for tt in range(4):
                                gt = ntr
                                pj2 = ptc2[gt % 2]
                                if gt >= 2:
                                    P.wait("tensor", c5, C2["c5"] + gt - 1)
                                pTR.n += 1
                                P.on("tensor", lambda t, rs2=rs2, tt=tt, pj2=pj2: t.transpose(
                                    out=pj2.bitcast(BF16)[:, 0:128],
                                    in_=rs2[:, tt * 128:(tt + 1) * 128],
                                    identity=idb_sb[:]).then_inc(pTR.h, 1))
                                P.wait("vector", pTR, pTR.n)
                                c5.n += 1
                                P.on("vector", lambda v, pj2=pj2, rc=rc, tt=tt, dp=dp: v.tensor_copy(
                                    out=rowb[:, (rc * 4 + tt) * E + dp * 128: (rc * 4 + tt) * E + dp * 128 + 128],
                                    in_=pj2.bitcast(BF16)[:, 0:128]).then_inc(c5.h, 1))
                                ntr += 1
                            res2_cons[jj] = ("ptr", pTR.n)
                        nev += 1
                if wn != "q":
                    k = int(wn[1])
                    P.wait("sync", c5, c5.n)
                    nrb = _dma(P, "sync", dstt[k].rearrange("(t p) d -> p t d", p=128),
                               rowb[:].rearrange("p (t d) -> p t d", d=E), q=DRB)
                    P.wait("vector", DRB, nrb)
            barrier(P)
            P.emit()

        c12.close()

        outT = enO(nc.sbuf_tensor("outTC", [128, 8 * ROWS], F32))
        with ExitStack() as st:
            en = st.enter_context
            krb1 = en(nc.sbuf_tensor("krbC", [128, 16 * E], BF16))
            vrb1 = en(nc.sbuf_tensor("vrbC", [128, 16 * E], BF16))
            krb = [krb1, krb1]
            vrb = [vrb1, vrb1]
            s01 = en(nc.sbuf_tensor("s01C", [128, E], F32))
            s1tmp = en(nc.sbuf_tensor("s1tmpC", [64, E], F32))
            stb = s1tmp[:].bitcast(BF16)[:, 0:E]
            qth1 = en(nc.sbuf_tensor("qthC0", [64, ROWS], BF16))
            qth = [qth1, qth1]
            stk = [en(nc.psum_tensor(f"stkC{i}", [64, 1024], F32)) for i in range(2)]
            pqc = [en(nc.psum_tensor(f"pqcC{i}", [64, 512], F32)) for i in range(2)]

            P = Prog(nc)
            c0, c2, c5 = sm["c0"], sm["c2"], sm["c5"]
            eST, eEV = sm["eST"], sm["eEV"]
            cc = sm["cAC"]
            C3 = {x: sm[x].n for x in ("eST", "c2", "c0")}
            for k in range(2):
                if k == 1:
                    P.wait("sync", eST, eST.n)
                _dma(P, "sync", krb[k][:].rearrange("p (t d) -> p t d", d=E),
                     ksr[k].rearrange("(t p) d -> p t d", p=128), q=DN[k])
                dtot = _dma(P, "sync", vrb[k][:].rearrange("p (t d) -> p t d", d=E),
                            vsr[k].rearrange("(t p) d -> p t d", p=128), q=DN[k])
                P.wait("tensor", DN[k], dtot)
                for h in range(H):
                    for ns in range(16):
                        if ns == 15:
                            eST.n += 1
                        P.on("tensor", lambda t, k=k, h=h, ns=ns: (
                            t.matmul(out=stk[k][:, h * 64:(h + 1) * 64],
                                     lhsT=krb[k][:, ns * E + h * 64: ns * E + h * 64 + 64],
                                     rhs=vrb[k][:, ns * E + h * 64: ns * E + h * 64 + 64],
                                     start=(ns == 0), stop=(ns == 15)).then_inc(eST.h, 1)
                            if ns == 15 else
                            t.matmul(out=stk[k][:, h * 64:(h + 1) * 64],
                                     lhsT=krb[k][:, ns * E + h * 64: ns * E + h * 64 + 64],
                                     rhs=vrb[k][:, ns * E + h * 64: ns * E + h * 64 + 64],
                                     start=(ns == 0), stop=(ns == 15))))
            P.wait("vector", eST, eST.n)
            c2.n += 2
            P.on("vector", lambda v: v.tensor_copy(out=s01[0:64, :], in_=stk[0][:]).then_inc(sm["c2"].h, 1))
            P.on("vector", lambda v: v.tensor_copy(out=s01[64:128, :], in_=stk[1][:]).then_inc(sm["c2"].h, 1))
            P.wait("gpsimd", c2, c2.n)
            _dma(P, "gpsimd", cin_t[:, :], s01[:])
            P.wait("gpsimd", dG, dG.n)
            cc.n += 1
            P.on("gpsimd", lambda g: g.collective_compute(
                "AllReduce", OP.add, replica_groups=[[0, 1, 2, 3], [4, 5, 6, 7]],
                ins=[cin_t.ap().opt()], outs=[cout_t.ap().opt()]).then_inc(cc.h, 1))
            P.wait("gpsimd", cc, cc.n)
            _dma(P, "gpsimd", s01[:], cout_t[:, :])
            P.wait("vector", dG, dG.n)
            c2.n += 1
            P.on("vector", lambda v: v.tensor_copy(out=s1tmp[:], in_=s01[64:128, :]
                                                   ).then_inc(c2.h, 1))
            P.wait("vector", c2, c2.n)
            c2.n += 1
            P.on("vector", lambda v: v.tensor_tensor(out=s01[0:64, :], in0=s01[0:64, :],
                                                     in1=s1tmp[:], op=OP.mult
                                                     ).then_inc(c2.h, 1))
            P.wait("vector", c2, c2.n)
            c2.n += 1
            P.on("vector", lambda v: v.tensor_scalar(out=stb, in0=s01[0:64, :],
                                                     scalar1=0.5 ** 24, scalar2=None,
                                                     op0=OP.mult).then_inc(c2.h, 1))
            stb_done = c2.n
            for h in range(H):
                if h >= 1:
                    P.wait("sync", c0, c0.n)
                dtot = _dma(P, "sync", qth[h % 2][:], qtb[h], q=DQ[h % 2])
                P.wait("tensor", DQ[h % 2], dtot)
                P.wait("tensor", c2, stb_done)
                for rc in range(4):
                    pj = pqc[rc % 2]
                    if h * 4 + rc >= 2:
                        P.wait("tensor", c0, C3["c0"] + h * 4 + rc - 1)
                    eST.n += 1
                    P.on("tensor", lambda t, h=h, rc=rc, pj=pj: t.matmul(
                        out=pj[:], lhsT=stb[:, h * 64:(h + 1) * 64],
                        rhs=qth[h % 2][:, rc * 512:(rc + 1) * 512],
                        start=True, stop=True).then_inc(eST.h, 1))
                    P.wait("vector", eST, eST.n)
                    c0.n += 1
                    P.on("vector", lambda v, h=h, rc=rc, pj=pj: v.tensor_copy(
                        out=outT[(h % 2) * 64:(h % 2) * 64 + 64,
                                 (h // 2) * ROWS + rc * 512:(h // 2) * ROWS + rc * 512 + 512],
                        in_=pj[:]).then_inc(c0.h, 1))
            barrier(P)
            P.emit()

        with ExitStack() as st:
            en = st.enter_context
            orow = en(nc.sbuf_tensor("orowC", [128, E], F32))
            sqbC2 = en(nc.sbuf_tensor("sqbC2", [128, E], F32))
            xh = [en(nc.sbuf_tensor(f"xhC{i}", [128, E], BF16)) for i in range(2)]
            wob = en(nc.sbuf_tensor("wobC", [128, 8 * E], BF16))
            ostg = [en(nc.sbuf_tensor(f"ostgC{i}", [128, 512], F32)) for i in range(2)]
            mcol = en(nc.sbuf_tensor("mcolC7", [128, 4], F32))
            xhT = en(nc.sbuf_tensor("xhTC", [128, 8 * ROWS], BF16))
            pt7 = [en(nc.psum_tensor(f"pt7C{i}", [128, 512], F32)) for i in range(2)]
            px7 = [en(nc.psum_tensor(f"px7C{i}", [128, 512], F32)) for i in range(2)]
            pp7 = [en(nc.psum_tensor(f"pp7C{i}", [128, 512], F32)) for i in range(2)]

            P = Prog(nc)
            c0, c2, c3, c4, c5 = (sm[x] for x in ("c0", "c2", "c3", "c4", "c5"))
            eST, eEV, pTR = sm["eST"], sm["eEV"], sm["pTR"]
            C7 = {x: sm[x].n for x in ("eST", "eEV", "pTR", "c0", "c2", "c5")}
            _dma(P, "gpsimd", wob[:].rearrange("p (c n) -> p c n", n=E), wo_b.rearrange("(c p) n -> p c n", p=128))
            ntr = 0
            evac_c2 = []
            nor = 0
            for rt in range(NRT):
                for fc in range(8):
                    pj = pt7[nor % 2]
                    if nor >= 2:
                        P.wait("tensor", c2, evac_c2[nor - 2])
                    pTR.n += 1
                    P.on("tensor", lambda t, fc=fc, rt=rt, pj=pj: t.transpose(
                        out=pj[:, 0:128],
                        in_=outT[:, fc * ROWS + rt * 128: fc * ROWS + rt * 128 + 128],
                        identity=idf_sb[:]).then_inc(pTR.h, 1))
                    P.wait("vector", pTR, pTR.n)
                    c2.n += 1
                    evac_c2.append(c2.n)
                    P.on("vector", lambda v, fc=fc, pj=pj: v.tensor_copy(
                        out=orow[:, fc * 128:(fc + 1) * 128], in_=pj[:, 0:128]).then_inc(c2.h, 1))
                    nor += 1
                xhl = xh[rt % 2]
                if rt >= 2:
                    P.wait("vector", pTR, C7["pTR"] + (rt - 1) * 16)
                emit_ln(P, mcol, orow[:], orow[:], sqbC2[:], c2, c3, c4)
                c0.n += 1
                P.on("vector", lambda v, xhl=xhl: v.tensor_scalar(
                    out=xhl[:], in0=orow[:], scalar1=mcol[:, 3:4], scalar2=None,
                    op0=OP.mult).then_inc(c0.h, 1))
                P.wait("tensor", c0, c0.n)
                for fc in range(8):
                    gx = rt * 8 + fc
                    pj = px7[gx % 2]
                    if gx >= 2:
                        P.wait("tensor", c5, C7["c5"] + gx - 1)
                    pTR.n += 1
                    P.on("tensor", lambda t, xhl=xhl, fc=fc, pj=pj: t.transpose(
                        out=pj.bitcast(BF16)[:, 0:128], in_=xhl[:, fc * 128:(fc + 1) * 128],
                        identity=idb_sb[:]).then_inc(pTR.h, 1))
                    P.wait("scalar", pTR, pTR.n)
                    c5.n += 1
                    P.on("scalar", lambda s, pj=pj, fc=fc, rt=rt: s.activation(
                        out=xhT[:, fc * ROWS + rt * 128: fc * ROWS + rt * 128 + 128],
                        in_=pj.bitcast(BF16)[:, 0:128], func=AF.Identity,
                        scale=bcols[:, BC["lng"] + fc: BC["lng"] + fc + 1],
                        bias=bcols[:, BC["lnb"] + fc: BC["lnb"] + fc + 1]).then_inc(c5.h, 1))
                    ntr += 1
            P.wait("tensor", c5, c5.n)
            P.wait("tensor", c2, c2.n)
            P.wait("tensor", dG, dG.n)
            nj = 0
            for dp in range(8):
                for rc in range(4):
                    pj = pp7[nj % 2]
                    if nj >= 2:
                        P.wait("tensor", eEV, C7["eEV"] + nj - 1)
                    for fc in range(8):
                        if fc == 7:
                            eST.n += 1
                        P.on("tensor", lambda t, fc=fc, dp=dp, rc=rc, pj=pj: (
                            t.matmul(out=pj[:],
                                     lhsT=wob[:, fc * E + dp * 128: fc * E + dp * 128 + 128],
                                     rhs=xhT[:, fc * ROWS + rc * 512: fc * ROWS + rc * 512 + 512],
                                     start=(fc == 0), stop=(fc == 7)).then_inc(eST.h, 1)
                            if fc == 7 else
                            t.matmul(out=pj[:],
                                     lhsT=wob[:, fc * E + dp * 128: fc * E + dp * 128 + 128],
                                     rhs=xhT[:, fc * ROWS + rc * 512: fc * ROWS + rc * 512 + 512],
                                     start=(fc == 0), stop=(fc == 7))))
                    P.wait("scalar", eST, eST.n)
                    if nj >= 2:
                        P.wait("scalar", c0, C7["c0"] + 16 + (nj - 1) * 4)
                    eEV.n += 1
                    osg = ostg[nj % 2]
                    P.on("scalar", lambda s, pj=pj, dp=dp, osg=osg: s.activation(
                        out=osg[:], in_=pj[:], func=AF.Identity,
                        bias=bcols[:, BC["bo_b"] + dp: BC["bo_b"] + dp + 1]).then_inc(eEV.h, 1))
                    P.wait("tensor", eEV, eEV.n)
                    for tt in range(4):
                        pj2 = pp7[(nj + 1) % 2] if False else pt7[tt % 2]
                        P.wait("tensor", c0, c0.n - 1)
                        pTR.n += 1
                        P.on("tensor", lambda t, osg=osg, tt=tt, pj2=pj2: t.transpose(
                            out=pj2[:, 0:128], in_=osg[:, tt * 128:(tt + 1) * 128],
                            identity=idf_sb[:]).then_inc(pTR.h, 1))
                        P.wait("vector", pTR, pTR.n)
                        rt = rc * 4 + tt
                        c0.n += 1
                        P.on("vector", lambda v, rt=rt, dp=dp, pj2=pj2: v.tensor_tensor(
                            out=xres[:, rt * E + dp * 128: rt * E + dp * 128 + 128],
                            in0=xres[:, rt * E + dp * 128: rt * E + dp * 128 + 128],
                            in1=pj2[:, 0:128], op=OP.add).then_inc(c0.h, 1))
                    nj += 1
            barrier(P)
            P.emit()
        c_outer.close()

    if nphase >= 4:
        emit_dequant()
        emit_ffn("F1")
    if nphase >= 5:
        emit_attnb()
    if nphase >= 6:
        emit_ffn("F2")

    P = Prog(nc)
    ov = out.rearrange("(t p) d -> p t d", p=128)
    xv = xres[:].rearrange("p (t d) -> p t d", d=E)
    for j in range(4):
        _dma(P, "sync", ov[:, j * 4:(j + 1) * 4, :], xv[:, j * 4:(j + 1) * 4, :])
    dbgsel = os.environ.get("MK_DBG", "")
    if dbgsel:
        dsrc = {
            "ksr0": ksr[0], "ksr1": ksr[1], "vsr0": vsr[0], "vsr1": vsr[1],
            "qtb": qtb.rearrange("a d (m n) -> (a d m) n", n=E),
            "cout": cout_t.ap(),
            "accs": accs,
        }[dbgsel]
        shp = dsrc.shape
        P.wait("gpsimd", dS, dS.n)
        dn = _dma(P, "gpsimd", out[0:shp[0], 0:shp[1]], dsrc)
        P.wait("sync", dG, dn)
    P.wait("sync", dS, dS.n)
    P.emit()

    outer.close()
    return nc


SCALE = 1.0 / math.sqrt(D)


_NC_CACHE = {}
LAST_RESULTS = None


def kernel(L0_attn_a_x, L0_attn_a_Wq, L0_attn_a_bq, L0_attn_a_Wk, L0_attn_a_bk,
           L0_attn_a_Wv, L0_attn_a_bv, L0_attn_a_Wo, L0_attn_a_bo,
           L0_attn_a_phase_modulators, L0_attn_a_interference_weights,
           L0_attn_b_Wq, L0_attn_b_bq, L0_attn_b_Wk, L0_attn_b_bk,
           L0_attn_b_Wv, L0_attn_b_bv, L0_attn_b_Wo, L0_attn_b_bo,
           L0_attn_b_ln_g, L0_attn_b_ln_b,
           L0_ffn_ln_gamma, L0_ffn_ln_beta, L0_ffn_fc1_q, L0_ffn_fc1_scales,
           L0_ffn_fc1_bias, L0_ffn_fc2_q, L0_ffn_fc2_scales, L0_ffn_fc2_bias):
    nphase = int(os.environ.get("MK_NPHASE", "6"))
    x = np.ascontiguousarray(np.asarray(L0_attn_a_x, dtype=np.float32))
    f32 = lambda a: np.ascontiguousarray(np.asarray(a, dtype=np.float32))
    i32 = lambda a: np.ascontiguousarray(np.asarray(a, dtype=np.int32))

    common = dict(
        wq_a=f32(L0_attn_a_Wq), wk_a=f32(L0_attn_a_Wk), wv_a=f32(L0_attn_a_Wv),
        wo_a=f32(L0_attn_a_Wo), bq_a=f32(L0_attn_a_bq), bk_a=f32(L0_attn_a_bk),
        bv_a=f32(L0_attn_a_bv), bo_a=f32(L0_attn_a_bo),
        pmt=np.ascontiguousarray(f32(L0_attn_a_phase_modulators).T),
        iw=f32(L0_attn_a_interference_weights),
        f_g=f32(L0_ffn_ln_gamma), f_b=f32(L0_ffn_ln_beta),
        w1q=i32(L0_ffn_fc1_q), s1=f32(L0_ffn_fc1_scales), fb1=f32(L0_ffn_fc1_bias),
        w2q=i32(L0_ffn_fc2_q), s2=f32(L0_ffn_fc2_scales), fb2=f32(L0_ffn_fc2_bias),
        wq_b=f32(L0_attn_b_Wq), wk_b0=f32(L0_attn_b_Wk[0]), wk_b1=f32(L0_attn_b_Wk[1]),
        wv_b0=f32(L0_attn_b_Wv[0]), wv_b1=f32(L0_attn_b_Wv[1]), wo_b=f32(L0_attn_b_Wo),
        bq_b=f32(L0_attn_b_bq), bk_b0=f32(L0_attn_b_bk[0]), bk_b1=f32(L0_attn_b_bk[1]),
        bv_b0=f32(L0_attn_b_bv[0]), bv_b1=f32(L0_attn_b_bv[1]), bo_b=f32(L0_attn_b_bo),
        lng=f32(L0_attn_b_ln_g), lnb=f32(L0_attn_b_ln_b),
        idf=np.eye(128, dtype=np.float32),
    )

    key = nphase
    if key not in _NC_CACHE:
        _NC_CACHE[key] = build(nphase)
    nc = _NC_CACHE[key]

    in_maps = []
    for c in range(8):
        b = c // 4
        off = (c % 4) * ROWS
        xq = np.roll(x[b], -off, axis=0)
        m = dict(common)
        m["xb"] = np.ascontiguousarray(xq)
        in_maps.append(m)

    res = run_bass_kernel_spmd(nc, in_maps, core_ids=list(range(8)))
    global LAST_RESULTS
    LAST_RESULTS = res.results
    shards = [np.asarray(res.results[c]["out"]) for c in range(8)]
    full = np.concatenate(shards, axis=0).reshape(B, S, E)
    return full.astype(np.float32)


# version 08
# speedup vs baseline: 1.0425x; 1.0242x over previous
import math
import os
from contextlib import ExitStack

import numpy as np
import ml_dtypes

import concourse.bass as bass
import concourse.mybir as mybir
from concourse.alu_op_type import AluOpType as OP
from concourse.bass_utils import run_bass_kernel_spmd

F32 = mybir.dt.float32
BF16 = mybir.dt.bfloat16
I32 = mybir.dt.int32
AF = mybir.ActivationFunctionType
AX = mybir.AxisListType

B, S, E = 2, 8192, 1024
H, D = 16, 64
WL = 16
ROWS = 2048
NRT = ROWS // 128
KERR = 2.7e-18 * 1e6
EPS = 1e-5
FF = 4096
NT = S // 128
NQ = 256
QCN = ROWS // NQ
TWO_PI = 2.0 * math.pi
ENGS = ("sync", "vector", "tensor", "scalar", "gpsimd")


class Sem:
    def __init__(self, handle):
        self.h = handle
        self.n = 0


class Prog:
    def __init__(self, nc):
        self.nc = nc
        self.q = {k: [] for k in ENGS}

    def on(self, eng, fn):
        self.q[eng].append(fn)

    def wait(self, eng, sem, val):
        if val > 0:
            self.q[eng].append(lambda e, s=sem.h, v=val: e.wait_ge(s, v))

    def emit(self):
        q = self.q
        with self.nc.Block() as blk:
            @blk.sync
            def _(e):
                for fn in q["sync"]:
                    fn(e)

            @blk.vector
            def _(e):
                for fn in q["vector"]:
                    fn(e)

            @blk.tensor
            def _(e):
                for fn in q["tensor"]:
                    fn(e)

            @blk.scalar
            def _(e):
                for fn in q["scalar"]:
                    fn(e)

            @blk.gpsimd
            def _(e):
                for fn in q["gpsimd"]:
                    fn(e)


def build(nphase=6):
    nc = bass.Bass("TRN2", target_bir_lowering=False)

    dri = lambda nm, shp, dt: nc.dram_tensor(nm, shp, dt, kind="ExternalInput").ap()
    xb = dri("xb", [S, E], F32)
    wq_a, wk_a, wv_a, wo_a = (dri(n, [E, E], F32) for n in ("wq_a", "wk_a", "wv_a", "wo_a"))
    bq_a, bk_a, bv_a, bo_a = (dri(n, [E], F32) for n in ("bq_a", "bk_a", "bv_a", "bo_a"))
    pmt = dri("pmt", [H, WL], F32)
    iw = dri("iw", [WL], F32)
    f_g, f_b = dri("f_g", [E], F32), dri("f_b", [E], F32)
    w1q = dri("w1q", [E, FF], I32)
    s1 = dri("s1", [8, FF], F32)
    fb1 = dri("fb1", [FF], F32)
    w2q = dri("w2q", [FF, E], I32)
    s2 = dri("s2", [32, E], F32)
    fb2 = dri("fb2", [E], F32)
    wq_b, wk_b0, wk_b1, wv_b0, wv_b1, wo_b = (
        dri(n, [E, E], F32) for n in ("wq_b", "wk_b0", "wk_b1", "wv_b0", "wv_b1", "wo_b"))
    bq_b, bk_b0, bk_b1, bv_b0, bv_b1, bo_b = (
        dri(n, [E], F32) for n in ("bq_b", "bk_b0", "bk_b1", "bv_b0", "bv_b1", "bo_b"))
    lng, lnb = dri("lng", [E], F32), dri("lnb", [E], F32)
    idf = dri("idf", [128, 128], F32)
    out = nc.dram_tensor("out", [ROWS, E], F32, kind="ExternalOutput").ap()

    kts = nc.dram_tensor("kts", [H, D, S], BF16).ap()
    vts = nc.dram_tensor("vts", [H, D, S], BF16).ap()
    qts = nc.dram_tensor("qts", [H, D, ROWS], BF16).ap()
    accs = nc.dram_tensor("accs", [E, ROWS], BF16).ap()
    flatr = nc.dram_tensor("flatr", [WL * H], F32).ap()
    w1ds = nc.dram_tensor("w1ds", [32, 128, 8, 128], BF16).ap()
    w2ds = nc.dram_tensor("w2ds", [8, 128, 32, 128], BF16).ap()
    ksr = nc.dram_tensor("ksr", [2, ROWS, E], BF16).ap()
    vsr = nc.dram_tensor("vsr", [2, ROWS, E], BF16).ap()
    qtb = nc.dram_tensor("qtb", [H, D, ROWS], BF16).ap()
    cin_t = nc.dram_tensor("cin_st", [128, E], F32)
    cout_t = nc.dram_tensor("cout_st", [128, E], F32)

    outer = ExitStack()
    names = ["dS", "dG", "c0", "c2", "c3", "c4", "c5",
             "eST", "eEV", "eXP", "eAQ", "eFN", "rRQ", "rBC", "rTM",
             "pVT", "vSB", "cAC", "pPJ", "pTR",
             "dx0", "dx1", "dx2", "dx3", "ds0", "ds1", "ds2", "ds3",
             "dk0", "dk1", "dn0", "dn1", "dq0", "dq1", "dw0", "dw1", "drb",
             "da0", "da1",
             "bS", "bV", "bT", "bA", "bG"]
    sm = {n: Sem(outer.enter_context(nc.semaphore(n))) for n in names}
    dS, dG = sm["dS"], sm["dG"]
    DX = [sm[f"dx{i}"] for i in range(4)]
    DS4 = [sm[f"ds{i}"] for i in range(4)]
    DK = [sm[f"dk{i}"] for i in range(2)]
    DN = [sm[f"dn{i}"] for i in range(2)]
    DQ = [sm[f"dq{i}"] for i in range(2)]
    DW = [sm[f"dw{i}"] for i in range(2)]
    DRB = sm["drb"]
    DA = [sm[f"da{i}"] for i in range(2)]
    SYNC_QS = [dS] + DX + DS4 + DK + DN + DQ + DW + [DRB] + DA
    BARMAP = dict(zip(ENGS, [sm["bS"], sm["bV"], sm["bT"], sm["bA"], sm["bG"]]))
    PH = [0]

    def barrier(P):
        PH[0] += 1
        tgt = PH[0]
        for q in SYNC_QS:
            P.wait("sync", q, q.n)
        P.wait("gpsimd", dG, dG.n)
        for en in ENGS:
            b = BARMAP[en]
            b.n += 1
            P.on(en, lambda e, _b=b: e.sem_inc(_b.h, 1))
        for en in ENGS:
            for b in BARMAP.values():
                P.wait(en, b, tgt)

    def _dma(P, eng, dst, src, q=None, slow=False):
        q = q if q is not None else (dG if eng == "gpsimd" else dS)
        q.n += 16
        P.on(eng, lambda e, d=dst, s=src, h=q.h, sl=slow: e.dma_start(
            out=d, in_=s, allow_slow_non_contiguous=sl).then_inc(h, 16))
        return q.n

    ones_f = outer.enter_context(nc.sbuf_tensor("ones_f", [1, 128], F32))
    ones_b = outer.enter_context(nc.sbuf_tensor("ones_b", [64, 1], BF16))
    idf_sb = outer.enter_context(nc.sbuf_tensor("idf_sb", [128, 128], F32))
    idb_sb = outer.enter_context(nc.sbuf_tensor("idb_sb", [128, 128], BF16))
    epsc = outer.enter_context(nc.sbuf_tensor("epsc", [128, 1], F32))
    bcols = outer.enter_context(nc.sbuf_tensor("bcols", [128, 8 * 11], F32))
    BCN = ["bq_a", "bk_a", "bv_a", "bo_a", "f_g", "f_b", "fb2h", "bo_b", "lng", "lnb", "bq_b"]
    BC = {n: i * 8 for i, n in enumerate(BCN)}
    fb1c = outer.enter_context(nc.sbuf_tensor("fb1c", [128, 32], F32))
    fb2c = outer.enter_context(nc.sbuf_tensor("fb2c", [128, 8], F32))
    bkvc = outer.enter_context(nc.sbuf_tensor("bkvc", [128, 5 * 8], F32))
    stackA = ExitStack()
    sccol = stackA.enter_context(nc.sbuf_tensor("sccol", [128, 512], F32))
    iwd = stackA.enter_context(nc.sbuf_tensor("iwd", [1, WL], F32))
    stackA2 = ExitStack()
    misc = stackA2.enter_context(nc.sbuf_tensor("misc", [1, 16], F32))
    arow = stackA2.enter_context(nc.sbuf_tensor("arow", [1, 512], F32))
    ncur = stackA2.enter_context(nc.sbuf_tensor("ncur", [1, 4], F32))

    P = Prog(nc)
    _dma(P, "sync", idf_sb[:], idf)
    for nm, src in [("bq_a", bq_a), ("bk_a", bk_a), ("bv_a", bv_a), ("bo_a", bo_a),
                    ("f_g", f_g), ("f_b", f_b), ("fb2h", fb2), ("bo_b", bo_b),
                    ("lng", lng), ("lnb", lnb), ("bq_b", bq_b)]:
        _dma(P, "sync", bcols[:, BC[nm]:BC[nm] + 8], src.rearrange("(c p) -> p c", p=128), slow=True)
    _dma(P, "sync", fb1c[:], fb1.rearrange("(c p) -> p c", p=128), slow=True)
    for i, src in enumerate([bq_b, bk_b0, bk_b1, bv_b0, bv_b1]):
        _dma(P, "sync", bkvc[:, i * 8:(i + 1) * 8], src.rearrange("(c p) -> p c", p=128), slow=True)
    _dma(P, "sync", iwd[:], iw.unsqueeze(0))
    P.wait("vector", dS, dS.n)
    P.on("vector", lambda v: v.memset(ones_f[:], 1.0))
    P.on("vector", lambda v: v.memset(ones_b[:], 1.0))
    P.on("vector", lambda v: v.memset(epsc[:], EPS))
    P.on("vector", lambda v: v.memset(ncur[:], 0.0))
    P.on("vector", lambda v: v.tensor_copy(out=idb_sb[:], in_=idf_sb[:]))
    P.on("vector", lambda v: v.tensor_scalar(out=iwd[:], in0=iwd[:], scalar1=1.0 / 16.0,
                                             scalar2=None, op0=OP.mult))
    P.on("vector", lambda v: v.tensor_scalar(out=fb2c[:], in0=bcols[:, BC["fb2h"]:BC["fb2h"] + 8],
                                             scalar1=0.5, scalar2=None, op0=OP.mult))
    barrier(P)
    P.emit()

    def emit_ln(P, mcol, src, center_dst, sqbuf, c2, c3, c4):
        c2.n += 1
        P.on("vector", lambda v: v.reduce_sum(out=mcol[:, 0:1], in_=src, axis=AX.X,
                                              negate=True).then_inc(c2.h, 1))
        P.wait("vector", c2, c2.n)
        c2.n += 1
        P.on("vector", lambda v: v.tensor_scalar(out=mcol[:, 0:1], in0=mcol[:, 0:1],
                                                 scalar1=1.0 / E, scalar2=None,
                                                 op0=OP.mult).then_inc(c2.h, 1))
        P.wait("vector", c2, c2.n)
        c2.n += 1
        P.on("vector", lambda v: v.tensor_scalar(out=center_dst, in0=src,
                                                 scalar1=mcol[:, 0:1], scalar2=None,
                                                 op0=OP.add).then_inc(c2.h, 1))
        P.wait("vector", c2, c2.n)
        c2.n += 1
        P.on("vector", lambda v: v.tensor_tensor(out=sqbuf, in0=center_dst, in1=center_dst,
                                                 op=OP.mult).then_inc(c2.h, 1))
        P.wait("vector", c2, c2.n)
        c3.n += 1
        P.on("vector", lambda v: v.reduce_sum(out=mcol[:, 1:2], in_=sqbuf,
                                              axis=AX.X).then_inc(c3.h, 1))
        P.wait("scalar", c3, c3.n)
        c4.n += 1
        P.on("scalar", lambda s: s.activation(out=mcol[:, 2:3], in_=mcol[:, 1:2], func=AF.Ln,
                                              scale=1.0 / E, bias=epsc[:]).then_inc(c4.h, 1))
        P.wait("scalar", c4, c4.n)
        c4.n += 1
        P.on("scalar", lambda s: s.activation(out=mcol[:, 3:4], in_=mcol[:, 2:3], func=AF.Exp,
                                              scale=-0.5).then_inc(c4.h, 1))
        P.wait("vector", c4, c4.n)

    with ExitStack() as st:
        en = st.enter_context
        wqs = en(nc.sbuf_tensor("wqs", [128, 8 * E], BF16))
        wks = en(nc.sbuf_tensor("wks", [128, 8 * E], BF16))
        wvs = en(nc.sbuf_tensor("wvs", [128, 8 * E], BF16))
        xt = [en(nc.sbuf_tensor(f"xt{i}", [128, E], F32)) for i in range(4)]
        sqb = en(nc.sbuf_tensor("sqb", [128, E], F32))
        xlb = [en(nc.sbuf_tensor(f"xlb{i}", [128, E], BF16)) for i in range(2)]
        xlnt = en(nc.sbuf_tensor("xlnt", [128, 8 * 512], BF16))
        stg = [en(nc.sbuf_tensor(f"stg{i}", [128, 512], BF16)) for i in range(4)]
        mcol = en(nc.sbuf_tensor("mcolA", [128, 4], F32))
        ptrp = [en(nc.psum_tensor(f"ptrp{i}", [128, 512], F32)) for i in range(2)]
        ppp = [en(nc.psum_tensor(f"pppA{i}", [128, 512], F32)) for i in range(2)]

        P = Prog(nc)
        c0, c2, c3, c4, c5 = (sm[x] for x in ("c0", "c2", "c3", "c4", "c5"))
        pTR, pPJ, eST, eEV = sm["pTR"], sm["pPJ"], sm["eST"], sm["eEV"]
        b0 = {x: sm[x].n for x in ("c0", "eST", "eEV", "pTR", "pPJ", "c5")}

        for wsb, wsrc in [(wqs, wq_a), (wks, wk_a), (wvs, wv_a)]:
            _dma(P, "gpsimd", wsb[:].rearrange("p (c n) -> p c n", n=E), wsrc.rearrange("(c p) n -> p c n", p=128))
        wload = dG.n

        stg_dma = {}
        chunk_est = {}
        nstg = 0
        for ch in range(16):
            own = ch < 4
            for r in range(4):
                i = ch * 4 + r
                if i >= 4:
                    P.wait("sync", c0, b0["c0"] + i - 3)
                dtot = _dma(P, "sync", xt[i % 4][:], xb[i * 128:(i + 1) * 128, :],
                            q=DX[i % 4])
                x = xt[i % 4][:]
                xl = xlb[i % 2]
                P.wait("vector", DX[i % 4], dtot)
                if i >= 2:
                    P.wait("vector", pTR, b0["pTR"] + (i - 1) * 8)
                emit_ln(P, mcol, x, x, sqb[:], c2, c3, c4)
                c0.n += 1
                P.on("vector", lambda v, x=x, xl=xl: v.tensor_scalar(
                    out=xl[:], in0=x, scalar1=mcol[:, 3:4], scalar2=None,
                    op0=OP.mult).then_inc(c0.h, 1))
                P.wait("tensor", c0, c0.n)
                if r == 0:
                    pass
                for fc in range(8):
                    gt = i * 8 + fc
                    pj = ptrp[gt % 2]
                    if gt >= 2:
                        P.wait("tensor", c5, b0["c5"] + gt - 1)
                    pTR.n += 1
                    P.on("tensor", lambda t, xl=xl, fc=fc, pj=pj: t.transpose(
                        out=pj.bitcast(BF16)[:, 0:128], in_=xl[:, fc * 128:(fc + 1) * 128],
                        identity=idb_sb[:]).then_inc(pTR.h, 1))
                    P.wait("vector", pTR, pTR.n)
                    if fc == 0 and r == 0 and ch > 0:
                        P.wait("vector", eST, chunk_est[ch - 1])
                    c5.n += 1
                    P.on("vector", lambda v, pj=pj, fc=fc, r=r: v.tensor_copy(
                        out=xlnt[:, fc * 512 + r * 128: fc * 512 + r * 128 + 128],
                        in_=pj.bitcast(BF16)[:, 0:128]).then_inc(c5.h, 1))
            xlnt_ready = c5.n
            P.wait("tensor", c5, xlnt_ready)
            if ch == 0:
                P.wait("tensor", dG, wload)
            groups = [("K", wks, BC["bk_a"], kts), ("V", wvs, BC["bv_a"], vts)]
            if own:
                groups.append(("Q", wqs, BC["bq_a"], qts))
            for gn, wsb, bcoff, dstt in groups:
                for dp in range(8):
                    jj = nstg
                    pj = ppp[jj % 2]
                    sgt = stg[jj % 4]
                    if jj >= 2:
                        P.wait("tensor", eEV, b0["eEV"] + jj - 1)
                    for fc in range(8):
                        if fc == 7:
                            eST.n += 1
                        P.on("tensor", lambda t, wsb=wsb, fc=fc, dp=dp, pj=pj: (
                            t.matmul(out=pj[:],
                                     lhsT=wsb[:, fc * E + dp * 128: fc * E + dp * 128 + 128],
                                     rhs=xlnt[:, fc * 512:(fc + 1) * 512],
                                     start=(fc == 0), stop=(fc == 7)).then_inc(eST.h, 1)
                            if fc == 7 else
                            t.matmul(out=pj[:],
                                     lhsT=wsb[:, fc * E + dp * 128: fc * E + dp * 128 + 128],
                                     rhs=xlnt[:, fc * 512:(fc + 1) * 512],
                                     start=(fc == 0), stop=(fc == 7))))
                    P.wait("scalar", eST, eST.n)
                    if jj >= 4:
                        P.wait("scalar", DS4[jj % 4], stg_dma[jj - 4])
                    eEV.n += 1
                    P.on("scalar", lambda s, pj=pj, sgt=sgt, bcoff=bcoff, dp=dp: s.activation(
                        out=sgt[:], in_=pj[:], func=AF.Identity,
                        bias=bcols[:, bcoff + dp: bcoff + dp + 1]).then_inc(eEV.h, 1))
                    P.wait("sync", eEV, eEV.n)
                    base = ch * 512
                    _dma(P, "sync", dstt[2 * dp, :, base:base + 512], sgt[0:64, :],
                         q=DS4[jj % 4])
                    stg_dma[jj] = _dma(P, "sync", dstt[2 * dp + 1, :, base:base + 512],
                                       sgt[64:128, :], q=DS4[jj % 4])
                    nstg += 1
            chunk_est[ch] = eST.n
        barrier(P)
        P.emit()

    with ExitStack() as st:
        en = st.enter_context
        nbuf = en(nc.sbuf_tensor("nbuf", [64, S], BF16))
        nps = en(nc.sbuf_tensor("nps", [1, 512], F32))
        pm_sb = en(nc.sbuf_tensor("pm_sb", [H, WL], F32))
        t16 = en(nc.sbuf_tensor("t16", [H, WL], F32))
        t16b = en(nc.sbuf_tensor("t16b", [H, WL], F32))
        pnp = [en(nc.psum_tensor(f"pnp{i}", [128, 512], F32)) for i in range(2)]
        psc = en(nc.psum_tensor("psc", [128, 512], F32))

        P = Prog(nc)
        c2, c3 = sm["c2"], sm["c3"]
        eST = sm["eST"]

        def vstep(fn):
            c2.n += 1
            P.on("vector", lambda v, f=fn: f(v).then_inc(c2.h, 1))
            P.wait("vector", c2, c2.n)

        def sstep(fn):
            P.wait("scalar", c2, c2.n)
            c3.n += 1
            P.on("scalar", lambda s, f=fn: f(s).then_inc(c3.h, 1))
            P.wait("vector", c3, c3.n)

        for src, ncol, ncols in [(kts, 1, S), (qts, 0, ROWS)]:
            for h in range(H):
                P.wait("sync", c2, c2.n)
                _dma(P, "sync", nbuf[:, 0:ncols], src[h])
                P.wait("vector", dS, dS.n)
                vstep(lambda v, n=ncols: v.tensor_tensor(
                    out=nbuf[:, 0:n], in0=nbuf[:, 0:n], in1=nbuf[:, 0:n], op=OP.mult))
                for j in range(ncols // 512):
                    pj = pnp[j % 2]
                    P.wait("tensor", c2, c2.n)
                    eST.n += 1
                    P.on("tensor", lambda t, pj=pj, j=j: t.matmul(
                        out=pj[0:1, :], lhsT=ones_b[:], rhs=nbuf[:, j * 512:(j + 1) * 512],
                        start=True, stop=True).then_inc(eST.h, 1))
                    P.wait("vector", eST, eST.n)
                    vstep(lambda v, pj=pj: v.tensor_copy(out=nps[:], in_=pj[0:1, :]))
                    vstep(lambda v: v.reduce_max(out=nps[0:1, 0:1], in_=nps[:], axis=AX.X))
                    vstep(lambda v, c=ncol: v.tensor_tensor(
                        out=ncur[0:1, c:c + 1], in0=ncur[0:1, c:c + 1],
                        in1=nps[0:1, 0:1], op=OP.max))

        P.wait("sync", c2, c2.n)
        _dma(P, "sync", pm_sb[:], pmt)
        P.wait("vector", dS, dS.n)
        MAGIC = 12582912.0
        vstep(lambda v: v.tensor_scalar(out=t16[:], in0=pm_sb[:], scalar1=1.0 / TWO_PI,
                                        scalar2=MAGIC, op0=OP.mult, op1=OP.add))
        vstep(lambda v: v.tensor_scalar(out=t16[:], in0=t16[:], scalar1=-MAGIC,
                                        scalar2=None, op0=OP.add))
        vstep(lambda v: v.scalar_tensor_tensor(out=t16[:], in0=t16[:], scalar=-TWO_PI,
                                               in1=pm_sb[:], op0=OP.mult, op1=OP.add))
        sstep(lambda s: s.activation(out=t16b[:], in_=t16[:], func=AF.Sin, scale=0.5))
        vstep(lambda v: v.tensor_tensor(out=t16[:], in0=t16b[:], in1=t16b[:], op=OP.mult))
        vstep(lambda v: v.tensor_scalar(out=t16[:], in0=t16[:], scalar1=-2.0, scalar2=1.0,
                                        op0=OP.mult, op1=OP.add))
        vstep(lambda v: v.tensor_tensor(out=t16b[:], in0=t16[:], in1=t16[:], op=OP.mult))
        vstep(lambda v: v.tensor_scalar(out=t16b[:], in0=t16b[:], scalar1=0.125,
                                        scalar2=None, op0=OP.mult))
        P.wait("sync", c2, c2.n)
        _dma(P, "sync", flatr.rearrange("(h c) -> h c", c=WL), t16b[:])
        P.wait("sync", dS, dS.n)
        _dma(P, "sync", arow[0:1, 0:256], flatr.unsqueeze(0))
        P.wait("vector", dS, dS.n)
        vstep(lambda v: v.reduce_max(out=misc[0:1, 4:5], in_=arow[0:1, 0:256], axis=AX.X))
        vstep(lambda v: v.tensor_tensor(out=misc[0:1, 5:6], in0=ncur[0:1, 0:1],
                                        in1=ncur[0:1, 1:2], op=OP.mult))
        sstep(lambda s: s.activation(out=misc[0:1, 6:7], in_=misc[0:1, 5:6], func=AF.Ln))
        P.wait("scalar", c3, c3.n)
        c3.n += 1
        P.on("scalar", lambda s: s.activation(out=misc[0:1, 2:3], in_=misc[0:1, 6:7],
                                              func=AF.Exp, scale=0.5).then_inc(c3.h, 1))
        P.wait("vector", c3, c3.n)
        vstep(lambda v: v.tensor_tensor(out=misc[0:1, 7:8], in0=misc[0:1, 4:5],
                                        in1=misc[0:1, 2:3], op=OP.mult))
        vstep(lambda v: v.tensor_tensor(out=misc[0:1, 8:9], in0=misc[0:1, 7:8],
                                        in1=misc[0:1, 7:8], op=OP.mult))
        sstep(lambda s: s.activation(out=misc[0:1, 9:10], in_=misc[0:1, 8:9], func=AF.Sin,
                                     scale=KERR / 2.0))
        vstep(lambda v: v.tensor_tensor(out=misc[0:1, 10:11], in0=misc[0:1, 9:10],
                                        in1=misc[0:1, 9:10], op=OP.mult))
        vstep(lambda v: v.tensor_scalar(out=misc[0:1, 3:4], in0=misc[0:1, 10:11],
                                        scalar1=-2.0, scalar2=1.0, op0=OP.mult, op1=OP.add))
        vstep(lambda v: v.tensor_scalar(out=arow[0:1, 0:256], in0=arow[0:1, 0:256],
                                        scalar1=misc[0:1, 3:4], scalar2=None, op0=OP.mult))
        vstep(lambda v: v.tensor_scalar(out=arow[0:1, 256:512], in0=arow[0:1, 0:256],
                                        scalar1=misc[0:1, 2:3], scalar2=-1.0,
                                        op0=OP.mult, op1=OP.mult))
        P.wait("tensor", c2, c2.n)
        eST.n += 1
        P.on("tensor", lambda t: t.matmul(out=psc[:], lhsT=ones_f[:], rhs=arow[:],
                                          start=True, stop=True).then_inc(eST.h, 1))
        P.wait("vector", eST, eST.n)
        vstep(lambda v: v.tensor_copy(out=sccol[:], in_=psc[:]))
        barrier(P)
        P.emit()

    stackA2.close()
    if nphase < 2:
        NH_RUN = 0
    else:
        NH_RUN = int(os.environ.get("MK_NH", str(H)))
    A3CUT = int(os.environ.get("MK_A3CUT", "5"))
    with ExitStack() as st:
        en = st.enter_context
        spr = en(nc.sbuf_tensor("spr", [128, 4 * 8192], F32))
        pbuf = en(nc.sbuf_tensor("pbuf", [128, 2 * 32 * NQ], BF16))
        ktb1 = en(nc.sbuf_tensor("ktb0", [64, S], BF16))
        ktb = [ktb1, ktb1]
        qtbu1 = en(nc.sbuf_tensor("qtbu0", [64, ROWS], BF16))
        qtbu = [qtbu1, qtbu1]
        vtb1 = en(nc.sbuf_tensor("vtb0", [128, S // 2], BF16))
        vtb = [vtb1, vtb1]
        vsb1 = en(nc.sbuf_tensor("vsb0", [128, NT * 65], BF16))
        vsb = [vsb1, vsb1]
        accf = [en(nc.sbuf_tensor(f"accf{i}", [64, NQ], F32)) for i in range(2)]
        rpair = en(nc.sbuf_tensor("rpair", [1, 512], F32))
        osb = en(nc.sbuf_tensor("osb", [64, NQ], F32))
        tmpf = en(nc.sbuf_tensor("tmpf", [64, NQ], F32))
        stp = [en(nc.psum_tensor(f"stp{i}", [128, 512], F32)) for i in range(3)]
        avp = [en(nc.psum_tensor(f"avp{i}", [128, 512], F32)) for i in range(3)]
        rbcp = en(nc.psum_tensor("rbcp", [128, 512], F32))
        vtpp = [en(nc.psum_tensor("vtpp0", [128, 256], BF16))] * 2

        P = Prog(nc)
        eST, eEV, eXP, eAQ, eFN = (sm[x] for x in ("eST", "eEV", "eXP", "eAQ", "eFN"))
        rRQ, rBC, rTM = sm["rRQ"], sm["rBC"], sm["rTM"]
        pVT, vSB, cAC, c2 = sm["pVT"], sm["vSB"], sm["cAC"], sm["c2"]
        A = {x: sm[x].n for x in ("eST", "eEV", "eXP", "eAQ", "eFN", "rRQ", "rBC",
                                  "rTM", "pVT", "vSB", "c2", "cAC")}
        dS_kv = {}
        acc_dma = {}
        vsb_done = {}

        P.on("vector", lambda v: v.memset(
            vsb[0].rearrange("p (t c) -> p t c", c=65)[:, :, 64:65], 1.0))

        for h in range(NH_RUN):
            if h >= 1:
                P.wait("sync", eST, A["eST"] + h * 256)
            if h >= 1 and A3CUT >= 3:
                P.wait("sync", pVT, A["pVT"] + h * (NT // 2))
            _dma(P, "sync", ktb[h % 2][:], kts[h], q=DK[h % 2])
            _dma(P, "sync", qtbu[h % 2][:], qts[h], q=DK[h % 2])
            vsrc = vts[h].rearrange("d (u two j) -> d two u j", two=2, j=128)
            _dma(P, "sync", vtb[h % 2][0:64, :].rearrange("d (u j) -> d u j", j=128),
                 vsrc[:, 0, :, :], q=DK[h % 2])
            dS_kv[h] = _dma(P, "sync",
                            vtb[h % 2][64:128, :].rearrange("d (u j) -> d u j", j=128),
                            vsrc[:, 1, :, :], q=DK[h % 2])

        def emit_fills(i, frange=None):
            h, qc = divmod(i, QCN)
            P.wait("tensor", DK[h % 2], dS_kv[h])
            for f in (frange if frange is not None else range(32)):
                gf = i * 32 + f
                pj = stp[gf % 3]
                if gf >= 3:
                    P.wait("tensor", eEV, A["eEV"] + gf - 2)
                for tl in range(2):
                    t_ = f * 2 + tl
                    if tl == 1:
                        eST.n += 1
                    P.on("tensor", lambda t, h=h, t_=t_, tl=tl, pj=pj, qc=qc: (
                        t.matmul(out=pj[:, (t_ % 2) * 256:(t_ % 2) * 256 + 256],
                                 lhsT=ktb[h % 2][:, t_ * 128:(t_ + 1) * 128],
                                 rhs=qtbu[h % 2][:, qc * NQ:(qc + 1) * NQ],
                                 start=True, stop=True).then_inc(eST.h, 1)
                        if tl == 1 else
                        t.matmul(out=pj[:, (t_ % 2) * 256:(t_ % 2) * 256 + 256],
                                 lhsT=ktb[h % 2][:, t_ * 128:(t_ + 1) * 128],
                                 rhs=qtbu[h % 2][:, qc * NQ:(qc + 1) * NQ],
                                 start=True, stop=True)))
                P.wait("vector", eST, eST.n)
                g = i * 2 + f // 16
                if g >= 4 and A3CUT >= 2:
                    gp = g - 4
                    P.wait("vector", eXP,
                           A["eXP"] + (gp // 2) * 32 + 31 + (gp % 2))
                eEV.n += 1
                sg = g % 4
                P.on("vector", lambda v, pj=pj, f=f, sg=sg: v.tensor_copy(
                    out=spr[:, sg * 8192 + (f % 16) * 512: sg * 8192 + (f % 16) * 512 + 512],
                    in_=pj[:]).then_inc(eEV.h, 1))

        def emit_vbuild(h, urange=None):
            b_ = h % 2
            P.wait("tensor", DK[h % 2], dS_kv[h])
            if h >= 1:
                if A3CUT >= 4:
                    P.wait("tensor", eAQ, A["eAQ"] + h * 256)
                if A3CUT >= 5:
                    P.wait("tensor", rTM, A["rTM"] + (h - 1) * 8 * 16)
            for u in (urange if urange is not None else range(NT // 2)):
                vv = vtpp[u % 2][:, 0:128]
                P.wait("tensor", vSB, vSB.n)
                pVT.n += 1
                P.on("tensor", lambda t, b_=b_, u=u, vv=vv: t.transpose(
                    out=vv, in_=vtb[b_][:, u * 128:(u + 1) * 128],
                    identity=idb_sb[:]).then_inc(pVT.h, 1))
                P.wait("vector", pVT, pVT.n)
                for par in range(2):
                    t_ = 2 * u + par
                    vSB.n += 1
                    P.on("vector", lambda v, b_=b_, t_=t_, vv=vv, par=par: v.tensor_copy(
                        out=vsb[b_][:, t_ * 65: t_ * 65 + 64],
                        in_=vv[:, par * 64:par * 64 + 64]).then_inc(vSB.h, 1))
            if urange is None or urange[-1] == NT // 2 - 1:
                vsb_done[h] = vSB.n

        EFN_REQ = {0: -2, 1: -2, 2: 0, 3: 0, 4: -4, 5: -4, 6: 2, 7: 2,
                   8: 4, 9: 4, 10: 6, 11: 6, 12: 8, 13: 8, 14: 10, 15: 10}

        def emit_av(i, ch):
            h, qc = divmod(i, QCN)
            if A3CUT >= 5:
                P.wait("tensor", eFN, A["eFN"] + i * 16 + EFN_REQ[ch])
            if A3CUT >= 3:
                P.wait("tensor", vSB, vsb_done[h])
            for hh in range(2):
                idx = i * 32 + ch * 2 + hh
                slot = idx % 2
                P.wait("tensor", eXP, A["eXP"] + idx + 1)
                for tl in range(32):
                    t_ = hh * 32 + tl
                    if tl == 31:
                        eAQ.n += 1
                    P.on("tensor", lambda t, h=h, ch=ch, t_=t_, tl=tl, slot=slot: (
                        t.matmul(out=avp[(ch // 2) % 3][0:65, (ch % 2) * 256:(ch % 2) * 256 + 256],
                                 lhsT=vsb[h % 2][:, t_ * 65:(t_ + 1) * 65],
                                 rhs=pbuf[:, slot * 8192 + tl * 256: slot * 8192 + tl * 256 + 256],
                                 start=(t_ == 0), stop=(t_ == 63)).then_inc(eAQ.h, 1)
                        if tl == 31 else
                        t.matmul(out=avp[(ch // 2) % 3][0:65, (ch % 2) * 256:(ch % 2) * 256 + 256],
                                 lhsT=vsb[h % 2][:, t_ * 65:(t_ + 1) * 65],
                                 rhs=pbuf[:, slot * 8192 + tl * 256: slot * 8192 + tl * 256 + 256],
                                 start=(t_ == 0), stop=(t_ == 63))))


        def emit_rbc(i, p):
            g = i * 8 + p
            P.wait("tensor", rRQ, A["rRQ"] + g + 1)
            P.wait("tensor", rTM, A["rTM"] + 2 * g)
            rBC.n += 1
            P.on("tensor", lambda t: t.matmul(
                out=rbcp[0:64, 0:512], lhsT=ones_f[0:1, 0:64], rhs=rpair[:],
                start=True, stop=True).then_inc(rBC.h, 1))

        def emit_finish(i, p):
            che, cho = 2 * p, 2 * p + 1
            pt = avp[p % 3]
            g = i * 8 + p
            if p == 0 and i > 1:
                P.wait("vector", DA[i % 2], acc_dma[i - 2])
            P.wait("vector", eAQ, A["eAQ"] + i * 32 + cho * 2 + 2)
            P.wait("vector", rBC, A["rBC"] + g)
            for chl, col in ((che, 0), (cho, 256)):
                c2.n += 1
                P.on("vector", lambda v, pt=pt, col=col: v.reciprocal(
                    out=rpair[0:1, col:col + 256],
                    in_=pt[64:65, col:col + 256]).then_inc(c2.h, 1))
                P.wait("vector", c2, c2.n)
                if chl == cho:
                    rRQ.n += 1
                P.on("vector", lambda v, chl=chl, col=col, last=(chl == cho): (
                    v.tensor_scalar(out=rpair[0:1, col:col + 256],
                                    in0=rpair[0:1, col:col + 256],
                                    scalar1=iwd[0:1, chl:chl + 1], scalar2=None,
                                    op0=OP.mult).then_inc(rRQ.h, 1)
                    if last else
                    v.tensor_scalar(out=rpair[0:1, col:col + 256],
                                    in0=rpair[0:1, col:col + 256],
                                    scalar1=iwd[0:1, chl:chl + 1], scalar2=None,
                                    op0=OP.mult)))
            P.wait("vector", rBC, A["rBC"] + g + 1)
            for chl, col in ((che, 0), (cho, 256)):
                c2.n += 1
                P.on("vector", lambda v, pt=pt, col=col: v.tensor_copy(
                    out=osb[:], in_=pt[0:64, col:col + 256]).then_inc(c2.h, 1))
                P.wait("vector", c2, c2.n)
                rTM.n += 1
                P.on("vector", lambda v, col=col: v.tensor_tensor(
                    out=tmpf[:], in0=osb[:], in1=rbcp[0:64, col:col + 256],
                    op=OP.mult).then_inc(rTM.h, 1))
                P.wait("vector", rTM, rTM.n)
                eFN.n += 1
                if chl == 0:
                    P.on("vector", lambda v, i=i: v.tensor_copy(
                        out=accf[i % 2][:], in_=tmpf[:]).then_inc(eFN.h, 1))
                else:
                    P.on("vector", lambda v, i=i: v.tensor_tensor(
                        out=accf[i % 2][:], in0=accf[i % 2][:], in1=tmpf[:],
                        op=OP.add).then_inc(eFN.h, 1))

        NITEM = NH_RUN * QCN
        if NITEM:
            if A3CUT >= 3:
                emit_vbuild(0)
            emit_fills(0)
        for i in range(NITEM):
            h, qc = divmod(i, QCN)
            for ch in range(16) if A3CUT >= 2 else []:
                for hh in range(2):
                    idx = i * 32 + ch * 2 + hh
                    slot = idx % 2
                    if ch == 0:
                        P.wait("scalar", eEV, A["eEV"] + i * 32 + (hh + 1) * 16)
                    if idx >= 2 and A3CUT >= 4:
                        P.wait("scalar", eAQ, A["eAQ"] + idx - 1)
                    eXP.n += 1
                    hc = h * 16 + ch
                    sg = (i * 2 + hh) % 4
                    P.on("scalar", lambda s, sg=sg, slot=slot, hc=hc: s.activation(
                        out=pbuf[:, slot * 8192:(slot + 1) * 8192],
                        in_=spr[:, sg * 8192:(sg + 1) * 8192], func=AF.Exp,
                        scale=sccol[:, hc:hc + 1],
                        bias=sccol[:, 256 + hc:256 + hc + 1]).then_inc(eXP.h, 1))
            if A3CUT >= 4:
                for ch in range(16):
                    emit_av(i, ch)
                    if i + 1 < NITEM:
                        emit_fills(i + 1, range(2 * ch, 2 * ch + 2))
                    if ch % 2 == 1 and A3CUT >= 5:
                        emit_rbc(i, ch // 2)
                        emit_finish(i, ch // 2)
                if i + 1 < NITEM and qc == QCN - 1 and A3CUT >= 3:
                    emit_vbuild(h + 1)
            else:
                if i + 1 < NITEM:
                    emit_fills(i + 1)
                    if qc == QCN - 1 and A3CUT >= 3:
                        emit_vbuild(h + 1)
            if A3CUT < 5:
                continue
            P.wait("gpsimd", eFN, eFN.n)
            acc_dma[i] = _dma(P, "gpsimd",
                              accs[h * 64:(h + 1) * 64, qc * NQ:(qc + 1) * NQ],
                              accf[i % 2][:], q=DA[i % 2])
        barrier(P)
        P.emit()

    stackA.close()
    xres = outer.enter_context(nc.sbuf_tensor("xres", [128, NRT * E], F32))

    a4_on = nphase >= 3 and NH_RUN == H
    if a4_on:
     with ExitStack() as st:
        en = st.enter_context
        acct = en(nc.sbuf_tensor("acct", [128, 8 * ROWS], BF16))
        wos = en(nc.sbuf_tensor("wosA", [128, 8 * E], BF16))
        otT = en(nc.sbuf_tensor("otT", [128, 8 * ROWS], F32))
        xst = [en(nc.sbuf_tensor(f"xstA{i}", [128, E], F32)) for i in range(2)]
        ppj = [en(nc.psum_tensor(f"ppjA{i}", [128, 512], F32)) for i in range(2)]

        P = Prog(nc)
        c0, c2, c3, c4 = (sm[x] for x in ("c0", "c2", "c3", "c4"))
        eST, eEV, pTR, c5 = sm["eST"], sm["eEV"], sm["pTR"], sm["c5"]
        A4 = {x: sm[x].n for x in ("eST", "eEV", "pTR", "c5", "c0")}

        _dma(P, "gpsimd", wos[:].rearrange("p (c n) -> p c n", n=E), wo_a.rearrange("(c p) n -> p c n", p=128))
        _dma(P, "sync", acct[:].rearrange("p (c n) -> p c n", n=ROWS), accs.rearrange("(c p) n -> p c n", p=128))
        ldT = dS.n
        P.wait("tensor", dS, ldT)
        P.wait("tensor", dG, dG.n)
        nj = 0
        for dp in range(8):
            for rc in range(4):
                pj = ppj[nj % 2]
                if nj >= 2:
                    P.wait("tensor", eEV, A4["eEV"] + nj - 1)
                for fc in range(8):
                    if fc == 7:
                        eST.n += 1
                    P.on("tensor", lambda t, fc=fc, dp=dp, rc=rc, pj=pj: (
                        t.matmul(out=pj[:],
                                 lhsT=wos[:, fc * E + dp * 128: fc * E + dp * 128 + 128],
                                 rhs=acct[:, fc * ROWS + rc * 512: fc * ROWS + rc * 512 + 512],
                                 start=(fc == 0), stop=(fc == 7)).then_inc(eST.h, 1)
                        if fc == 7 else
                        t.matmul(out=pj[:],
                                 lhsT=wos[:, fc * E + dp * 128: fc * E + dp * 128 + 128],
                                 rhs=acct[:, fc * ROWS + rc * 512: fc * ROWS + rc * 512 + 512],
                                 start=(fc == 0), stop=(fc == 7))))
                P.wait("scalar", eST, eST.n)
                eEV.n += 1
                P.on("scalar", lambda s, pj=pj, dp=dp, rc=rc: s.activation(
                    out=otT[:, dp * ROWS + rc * 512: dp * ROWS + rc * 512 + 512],
                    in_=pj[:], func=AF.Identity,
                    bias=bcols[:, BC["bo_a"] + dp: BC["bo_a"] + dp + 1]).then_inc(eEV.h, 1))
                nj += 1
        for rt in range(NRT):
            if rt >= 2:
                P.wait("sync", c0, A4["c0"] + (rt - 1) * 8)
            dtot = _dma(P, "sync", xst[rt % 2][:], xb[rt * 128:(rt + 1) * 128, :],
                        q=DX[rt % 2])
            P.wait("tensor", eEV, eEV.n)
            P.wait("vector", DX[rt % 2], dtot)
            for fc in range(8):
                gt = rt * 8 + fc
                pj = ppj[gt % 2]
                if gt >= 2:
                    P.wait("tensor", c0, A4["c0"] + gt - 1)
                pTR.n += 1
                P.on("tensor", lambda t, fc=fc, rt=rt, pj=pj: t.transpose(
                    out=pj[:, 0:128], in_=otT[:, fc * ROWS + rt * 128: fc * ROWS + rt * 128 + 128],
                    identity=idf_sb[:]).then_inc(pTR.h, 1))
                P.wait("vector", pTR, pTR.n)
                c0.n += 1
                P.on("vector", lambda v, fc=fc, rt=rt, pj=pj: v.tensor_tensor(
                    out=xres[:, rt * E + fc * 128: rt * E + fc * 128 + 128],
                    in0=xst[rt % 2][:, fc * 128:(fc + 1) * 128], in1=pj[:, 0:128],
                    op=OP.add).then_inc(c0.h, 1))
        barrier(P)
        P.emit()

    def emit_dequant():
        with ExitStack() as st:
            en = st.enter_context
            qw = en(nc.sbuf_tensor("qw", [128, FF], I32))
            srow = en(nc.sbuf_tensor("srow", [1, FF], F32))
            scf = en(nc.sbuf_tensor("scf", [128, FF], F32))
            tqf = en(nc.sbuf_tensor("tqf", [128, FF], F32))
            wbb = en(nc.sbuf_tensor("wbb", [128, FF], BF16))
            pbq = [en(nc.psum_tensor(f"pbq{i}", [128, 512], F32)) for i in range(2)]

            P = Prog(nc)
            c2, eST = sm["c2"], sm["eST"]
            for w_src, s_src, wd, nkc, width, nw in [
                    (w1q, s1, w1ds, 8, FF, 32), (w2q, s2, w2ds, 32, E, 8)]:
                for kc in range(nkc):
                    P.wait("sync", dG, dG.n)
                    _dma(P, "sync", qw[:, 0:width], w_src[kc * 128:(kc + 1) * 128, :])
                    _dma(P, "sync", srow[0:1, 0:width], s_src[kc:kc + 1, :])
                    dtot = dS.n
                    P.wait("tensor", dS, dtot)
                    P.wait("tensor", c2, c2.n)
                    base_c2 = c2.n
                    for j in range(width // 512):
                        pj = pbq[j % 2]
                        if j >= 2:
                            P.wait("tensor", c2, base_c2 + j - 1)
                        eST.n += 1
                        P.on("tensor", lambda t, pj=pj, j=j: t.matmul(
                            out=pj[:], lhsT=ones_f[:], rhs=srow[0:1, j * 512:(j + 1) * 512],
                            start=True, stop=True).then_inc(eST.h, 1))
                        P.wait("vector", eST, eST.n)
                        c2.n += 1
                        P.on("vector", lambda v, pj=pj, j=j: v.tensor_copy(
                            out=scf[:, j * 512:(j + 1) * 512], in_=pj[:]).then_inc(c2.h, 1))
                    P.wait("vector", dS, dtot)
                    c2.n += 1
                    P.on("vector", lambda v, width=width: v.tensor_scalar(
                        out=tqf[:, 0:width], in0=qw[:, 0:width], scalar1=-128.0,
                        scalar2=None, op0=OP.add).then_inc(c2.h, 1))
                    P.wait("vector", c2, c2.n)
                    c2.n += 1
                    P.on("vector", lambda v, width=width: v.tensor_tensor(
                        out=wbb[:, 0:width], in0=tqf[:, 0:width], in1=scf[:, 0:width],
                        op=OP.mult).then_inc(c2.h, 1))
                    P.wait("gpsimd", c2, c2.n)
                    for oc in range(nw):
                        _dma(P, "gpsimd", wd[oc, :, kc, :], wbb[:, oc * 128:(oc + 1) * 128])
            barrier(P)
            P.emit()

    def emit_ffn(tag):
        with ExitStack() as st:
            en = st.enter_context
            ybuf = en(nc.sbuf_tensor(f"ybuf{tag}", [128, 4 * E], F32))
            cb = en(nc.sbuf_tensor(f"cb{tag}", [128, E], F32))
            sqb2 = en(nc.sbuf_tensor(f"sqb2{tag}", [128, E], F32))
            hrow = en(nc.sbuf_tensor(f"hrow{tag}", [128, E], BF16))
            hT = en(nc.sbuf_tensor(f"hT{tag}", [128, 8 * 512], BF16))
            y1T = en(nc.sbuf_tensor(f"y1T{tag}", [128, 32 * 512], BF16))
            zt = en(nc.sbuf_tensor(f"zt{tag}", [128, 8 * 512], F32))
            w1sb = [en(nc.sbuf_tensor(f"w1sb{tag}{i}", [128, 8 * 128], BF16)) for i in range(2)]
            w2sb = [en(nc.sbuf_tensor(f"w2sb{tag}{i}", [128, 32 * 128], BF16)) for i in range(2)]
            mcol = en(nc.sbuf_tensor(f"mcolF{tag}", [128, 4], F32))
            ppf = [en(nc.psum_tensor(f"ppf{tag}{i}", [128, 512], F32)) for i in range(2)]

            P = Prog(nc)
            c0, c2, c3, c4, c5 = (sm[x] for x in ("c0", "c2", "c3", "c4", "c5"))
            eST, eEV, pTR, pPJ = sm["eST"], sm["eEV"], sm["pTR"], sm["pPJ"]
            FB = {x: sm[x].n for x in ("eST", "eEV", "pTR", "c5", "c0", "pPJ")}
            w1dma = {}
            w2dma = {}
            hT_ptr = {}

            for rc in range(4):
                for r in range(4):
                    rt = rc * 4 + r
                    xsl = xres[:, rt * E:(rt + 1) * E]
                    ysl = ybuf[:, r * E:(r + 1) * E]
                    emit_ln(P, mcol, xsl, ysl, sqb2[:], c2, c3, c4)
                    c2.n += 1
                    P.on("vector", lambda v, ysl=ysl: v.tensor_scalar(
                        out=ysl, in0=ysl, scalar1=mcol[:, 3:4], scalar2=None,
                        op0=OP.mult).then_inc(c2.h, 1))
                    P.wait("vector", c2, c2.n)
                    c2.n += 1
                    P.on("vector", lambda v, xsl=xsl, ysl=ysl: v.tensor_tensor(
                        out=xsl, in0=xsl, in1=ysl, op=OP.add).then_inc(c2.h, 1))
                    P.wait("vector", c2, c2.n)
                    emit_ln(P, mcol, ysl, cb[:], sqb2[:], c2, c3, c4)
                    gr = rc * 4 + r
                    if gr >= 1:
                        P.wait("vector", pTR, hT_ptr[gr - 1])
                    c0.n += 1
                    P.on("vector", lambda v: v.tensor_scalar(
                        out=hrow[:], in0=cb[:], scalar1=mcol[:, 3:4], scalar2=None,
                        op0=OP.mult).then_inc(c0.h, 1))
                    P.wait("tensor", c0, c0.n)
                    if r == 0 and rc > 0:
                        P.wait("tensor", c0, FB["c0"] + rc * 36)
                    for fc in range(8):
                        gl = r * 8 + fc
                        pj = ppf[fc % 2]
                        if gl >= 2 or r > 0:
                            P.wait("tensor", c5, c5.n - 1)
                        pTR.n += 1
                        P.on("tensor", lambda t, fc=fc, pj=pj: t.transpose(
                            out=pj.bitcast(BF16)[:, 0:128], in_=hrow[:, fc * 128:(fc + 1) * 128],
                            identity=idb_sb[:]).then_inc(pTR.h, 1))
                        P.wait("scalar", pTR, pTR.n)
                        c5.n += 1
                        P.on("scalar", lambda s, pj=pj, fc=fc, r=r: s.activation(
                            out=hT[:, fc * 512 + r * 128: fc * 512 + r * 128 + 128],
                            in_=pj.bitcast(BF16)[:, 0:128], func=AF.Identity,
                            scale=bcols[:, BC["f_g"] + fc: BC["f_g"] + fc + 1],
                            bias=bcols[:, BC["f_b"] + fc: BC["f_b"] + fc + 1]).then_inc(c5.h, 1))
                        hT_ptr[gr] = pTR.n
                P.wait("tensor", c5, c5.n)
                for oc in range(32):
                    wb = w1sb[oc % 2]
                    g1 = rc * 32 + oc
                    if g1 >= 2:
                        rcp, ocp = divmod(g1 - 2, 32)
                        P.wait("sync", eEV, FB["eEV"] + rcp * 40 + ocp + 1)
                    w1dma[(rc, oc)] = _dma(P, "sync", wb[:],
                                           w1ds[oc].rearrange("p c n -> p (c n)"),
                                           q=DW[oc % 2])
                    pj = ppf[oc % 2]
                    P.wait("tensor", DW[oc % 2], w1dma[(rc, oc)])
                    if oc >= 2:
                        P.wait("tensor", eEV, FB["eEV"] + rc * 40 + oc - 1)
                    for kc in range(8):
                        if kc == 7:
                            eST.n += 1
                        P.on("tensor", lambda t, wb=wb, kc=kc, pj=pj: (
                            t.matmul(out=pj[:], lhsT=wb[:, kc * 128:(kc + 1) * 128],
                                     rhs=hT[:, kc * 512:(kc + 1) * 512],
                                     start=(kc == 0), stop=(kc == 7)).then_inc(eST.h, 1)
                            if kc == 7 else
                            t.matmul(out=pj[:], lhsT=wb[:, kc * 128:(kc + 1) * 128],
                                     rhs=hT[:, kc * 512:(kc + 1) * 512],
                                     start=(kc == 0), stop=(kc == 7))))
                    P.wait("scalar", eST, eST.n)
                    eEV.n += 1
                    P.on("scalar", lambda s, pj=pj, oc=oc: s.activation(
                        out=y1T[:, oc * 512:(oc + 1) * 512], in_=pj[:], func=AF.Silu,
                        bias=fb1c[:, oc:oc + 1]).then_inc(eEV.h, 1))
                for oc in range(8):
                    wb = w2sb[oc % 2]
                    g2 = rc * 8 + oc
                    if g2 >= 2:
                        rcp, ocp = divmod(g2 - 2, 8)
                        P.wait("sync", eEV, FB["eEV"] + rcp * 40 + 32 + ocp + 1)
                    w2dma[(rc, oc)] = _dma(P, "sync", wb[:],
                                           w2ds[oc].rearrange("p c n -> p (c n)"),
                                           q=DW[oc % 2])
                    pj = ppf[oc % 2]
                    P.wait("tensor", DW[oc % 2], w2dma[(rc, oc)])
                    if oc >= 2:
                        P.wait("tensor", eEV, FB["eEV"] + rc * 40 + 32 + oc - 1)
                    else:
                        P.wait("tensor", eEV, FB["eEV"] + rc * 40 + 32)
                    for kc in range(32):
                        if kc == 31:
                            eST.n += 1
                        P.on("tensor", lambda t, wb=wb, kc=kc, pj=pj: (
                            t.matmul(out=pj[:], lhsT=wb[:, kc * 128:(kc + 1) * 128],
                                     rhs=y1T[:, kc * 512:(kc + 1) * 512],
                                     start=(kc == 0), stop=(kc == 31)).then_inc(eST.h, 1)
                            if kc == 31 else
                            t.matmul(out=pj[:], lhsT=wb[:, kc * 128:(kc + 1) * 128],
                                     rhs=y1T[:, kc * 512:(kc + 1) * 512],
                                     start=(kc == 0), stop=(kc == 31))))
                    P.wait("scalar", eST, eST.n)
                    eEV.n += 1
                    P.on("scalar", lambda s, pj=pj, oc=oc: s.activation(
                        out=zt[:, oc * 512:(oc + 1) * 512], in_=pj[:], func=AF.Identity,
                        scale=0.5, bias=fb2c[:, oc:oc + 1]).then_inc(eEV.h, 1))
                P.wait("tensor", eEV, eEV.n)
                for r in range(4):
                    rt = rc * 4 + r
                    for fc in range(8):
                        pj = ppf[fc % 2]
                        P.wait("tensor", c0, c0.n - 1)
                        pTR.n += 1
                        P.on("tensor", lambda t, fc=fc, r=r, pj=pj: t.transpose(
                            out=pj[:, 0:128], in_=zt[:, fc * 512 + r * 128: fc * 512 + r * 128 + 128],
                            identity=idf_sb[:]).then_inc(pTR.h, 1))
                        P.wait("vector", pTR, pTR.n)
                        c0.n += 1
                        P.on("vector", lambda v, fc=fc, rt=rt, pj=pj: v.tensor_tensor(
                            out=xres[:, rt * E + fc * 128: rt * E + fc * 128 + 128],
                            in0=xres[:, rt * E + fc * 128: rt * E + fc * 128 + 128],
                            in1=pj[:, 0:128], op=OP.add).then_inc(c0.h, 1))
            barrier(P)
            P.emit()

    def emit_attnb():
        c_outer = ExitStack()
        enO = c_outer.enter_context
        c12 = ExitStack()
        yT = c12.enter_context(nc.sbuf_tensor("yTC", [128, 8 * ROWS], BF16))

        with ExitStack() as st:
            en = st.enter_context
            ybf = en(nc.sbuf_tensor("ybfC", [128, E], F32))
            sqbC = en(nc.sbuf_tensor("sqbC", [128, E], F32))
            ylb = [en(nc.sbuf_tensor(f"ylbC{i}", [128, E], BF16)) for i in range(2)]
            mcol = en(nc.sbuf_tensor("mcolC", [128, 4], F32))
            ptc = [en(nc.psum_tensor(f"ptcC{i}", [128, 512], F32)) for i in range(2)]

            P = Prog(nc)
            c0, c2, c3, c4, c5 = (sm[x] for x in ("c0", "c2", "c3", "c4", "c5"))
            pTR = sm["pTR"]
            C1 = {x: sm[x].n for x in ("pTR", "c5")}
            for rt in range(NRT):
                xsl = xres[:, rt * E:(rt + 1) * E]
                yl = ylb[rt % 2]
                if rt >= 2:
                    P.wait("vector", pTR, C1["pTR"] + (rt - 1) * 8)
                emit_ln(P, mcol, xsl, ybf[:], sqbC[:], c2, c3, c4)
                c0.n += 1
                P.on("vector", lambda v, yl=yl: v.tensor_scalar(
                    out=yl[:], in0=ybf[:], scalar1=mcol[:, 3:4], scalar2=None,
                    op0=OP.mult).then_inc(c0.h, 1))
                P.wait("tensor", c0, c0.n)
                for fc in range(8):
                    gt = rt * 8 + fc
                    pj = ptc[gt % 2]
                    if gt >= 2:
                        P.wait("tensor", c5, C1["c5"] + gt - 1)
                    pTR.n += 1
                    P.on("tensor", lambda t, yl=yl, fc=fc, pj=pj: t.transpose(
                        out=pj.bitcast(BF16)[:, 0:128], in_=yl[:, fc * 128:(fc + 1) * 128],
                        identity=idb_sb[:]).then_inc(pTR.h, 1))
                    P.wait("vector", pTR, pTR.n)
                    c5.n += 1
                    P.on("vector", lambda v, pj=pj, fc=fc, rt=rt: v.tensor_copy(
                        out=yT[:, fc * ROWS + rt * 128: fc * ROWS + rt * 128 + 128],
                        in_=pj.bitcast(BF16)[:, 0:128]).then_inc(c5.h, 1))
            barrier(P)
            P.emit()

        with ExitStack() as st:
            en = st.enter_context
            rowb = en(nc.sbuf_tensor("rowbC", [128, NRT * E], BF16))
            wcb = en(nc.sbuf_tensor("wcbC", [128, 8 * E], BF16))
            stgc = [en(nc.sbuf_tensor(f"stgC{i}", [128, 512], BF16)) for i in range(2)]
            res2 = [en(nc.sbuf_tensor(f"resC{i}", [128, 512], BF16)) for i in range(2)]
            ppc = [en(nc.psum_tensor(f"ppcC{i}", [128, 512], F32)) for i in range(2)]
            ptc2 = [en(nc.psum_tensor(f"ptc2C{i}", [128, 512], F32)) for i in range(2)]

            P = Prog(nc)
            c0, c2, c5 = sm["c0"], sm["c2"], sm["c5"]
            eST, eEV, pTR, pPJ = sm["eST"], sm["eEV"], sm["pTR"], sm["pPJ"]
            C2 = {x: sm[x].n for x in ("eST", "eEV", "pTR", "c0", "c2", "c5", "pPJ")}
            WLIST = [("q", wq_b, 0, None), ("k0", wk_b0, 1, ksr), ("k1", wk_b1, 2, ksr),
                     ("v0", wv_b0, 3, vsr), ("v1", wv_b1, 4, vsr)]
            nev = 0
            ntr = 0
            res2_cons = {}
            for wi, (wn, wsrc, bidx, dstt) in enumerate(WLIST):
                if wi > 0:
                    P.wait("gpsimd", eST, eST.n)
                _dma(P, "gpsimd", wcb[:], wsrc.rearrange("(c p) n -> p c n", p=128))
                P.wait("tensor", dG, dG.n)
                for dp in range(8):
                    for rc in range(4):
                        jj = nev
                        pj = ppc[jj % 2]
                        sgt = stgc[jj % 2]
                        rs2 = res2[jj % 2]
                        if jj >= 2:
                            P.wait("tensor", eEV, C2["eEV"] + jj - 1)
                        for fc in range(8):
                            if fc == 7:
                                eST.n += 1
                            P.on("tensor", lambda t, fc=fc, dp=dp, rc=rc, pj=pj: (
                                t.matmul(out=pj[:],
                                         lhsT=wcb[:, fc * E + dp * 128: fc * E + dp * 128 + 128],
                                         rhs=yT[:, fc * ROWS + rc * 512: fc * ROWS + rc * 512 + 512],
                                         start=(fc == 0), stop=(fc == 7)).then_inc(eST.h, 1)
                                if fc == 7 else
                                t.matmul(out=pj[:],
                                         lhsT=wcb[:, fc * E + dp * 128: fc * E + dp * 128 + 128],
                                         rhs=yT[:, fc * ROWS + rc * 512: fc * ROWS + rc * 512 + 512],
                                         start=(fc == 0), stop=(fc == 7))))
                        P.wait("scalar", eST, eST.n)
                        if jj >= 2:
                            P.wait("scalar", c2, C2["c2"] + jj - 1)
                        eEV.n += 1
                        P.on("scalar", lambda s, pj=pj, sgt=sgt, bidx=bidx, dp=dp: s.activation(
                            out=sgt[:], in_=pj[:], func=AF.Identity,
                            bias=bkvc[:, bidx * 8 + dp: bidx * 8 + dp + 1]).then_inc(eEV.h, 1))
                        P.wait("vector", eEV, eEV.n)
                        if jj - 2 in res2_cons:
                            kind, cnt = res2_cons[jj - 2]
                            if kind == "ptr":
                                P.wait("vector", pTR, cnt)
                            else:
                                P.wait("vector", DS4[0], cnt)
                        c2.n += 1
                        P.on("vector", lambda v, sgt=sgt, rs2=rs2, dp=dp, rc=rc: v.tensor_tensor(
                            out=rs2[:], in0=sgt[:],
                            in1=yT[:, dp * ROWS + rc * 512: dp * ROWS + rc * 512 + 512],
                            op=OP.add).then_inc(c2.h, 1))
                        if wn == "q":
                            P.wait("sync", c2, c2.n)
                            _dma(P, "sync", qtb[2 * dp, :, rc * 512:(rc + 1) * 512], rs2[0:64, :],
                                 q=DS4[0])
                            res2_cons[jj] = ("dma", _dma(
                                P, "sync", qtb[2 * dp + 1, :, rc * 512:(rc + 1) * 512],
                                rs2[64:128, :], q=DS4[0]))
                        else:
                            P.wait("tensor", c2, c2.n)
                            for tt in range(4):
                                gt = ntr
                                pj2 = ptc2[gt % 2]
                                if gt >= 2:
                                    P.wait("tensor", c5, C2["c5"] + gt - 1)
                                pTR.n += 1
                                P.on("tensor", lambda t, rs2=rs2, tt=tt, pj2=pj2: t.transpose(
                                    out=pj2.bitcast(BF16)[:, 0:128],
                                    in_=rs2[:, tt * 128:(tt + 1) * 128],
                                    identity=idb_sb[:]).then_inc(pTR.h, 1))
                                P.wait("vector", pTR, pTR.n)
                                c5.n += 1
                                P.on("vector", lambda v, pj2=pj2, rc=rc, tt=tt, dp=dp: v.tensor_copy(
                                    out=rowb[:, (rc * 4 + tt) * E + dp * 128: (rc * 4 + tt) * E + dp * 128 + 128],
                                    in_=pj2.bitcast(BF16)[:, 0:128]).then_inc(c5.h, 1))
                                ntr += 1
                            res2_cons[jj] = ("ptr", pTR.n)
                        nev += 1
                if wn != "q":
                    k = int(wn[1])
                    P.wait("sync", c5, c5.n)
                    nrb = _dma(P, "sync", dstt[k].rearrange("(t p) d -> p t d", p=128),
                               rowb[:].rearrange("p (t d) -> p t d", d=E), q=DRB)
                    P.wait("vector", DRB, nrb)
            barrier(P)
            P.emit()

        c12.close()

        outT = enO(nc.sbuf_tensor("outTC", [128, 8 * ROWS], F32))
        with ExitStack() as st:
            en = st.enter_context
            krb1 = en(nc.sbuf_tensor("krbC", [128, 16 * E], BF16))
            vrb1 = en(nc.sbuf_tensor("vrbC", [128, 16 * E], BF16))
            krb = [krb1, krb1]
            vrb = [vrb1, vrb1]
            s01 = en(nc.sbuf_tensor("s01C", [128, E], F32))
            s1tmp = en(nc.sbuf_tensor("s1tmpC", [64, E], F32))
            stb = s1tmp[:].bitcast(BF16)[:, 0:E]
            qth1 = en(nc.sbuf_tensor("qthC0", [64, ROWS], BF16))
            qth = [qth1, qth1]
            stk = [en(nc.psum_tensor(f"stkC{i}", [64, 1024], F32)) for i in range(2)]
            pqc = [en(nc.psum_tensor(f"pqcC{i}", [64, 512], F32)) for i in range(2)]

            P = Prog(nc)
            c0, c2, c5 = sm["c0"], sm["c2"], sm["c5"]
            eST, eEV = sm["eST"], sm["eEV"]
            cc = sm["cAC"]
            C3 = {x: sm[x].n for x in ("eST", "c2", "c0")}
            for k in range(2):
                if k == 1:
                    P.wait("sync", eST, eST.n)
                _dma(P, "sync", krb[k][:].rearrange("p (t d) -> p t d", d=E),
                     ksr[k].rearrange("(t p) d -> p t d", p=128), q=DN[k])
                dtot = _dma(P, "sync", vrb[k][:].rearrange("p (t d) -> p t d", d=E),
                            vsr[k].rearrange("(t p) d -> p t d", p=128), q=DN[k])
                P.wait("tensor", DN[k], dtot)
                for h in range(H):
                    for ns in range(16):
                        if ns == 15:
                            eST.n += 1
                        P.on("tensor", lambda t, k=k, h=h, ns=ns: (
                            t.matmul(out=stk[k][:, h * 64:(h + 1) * 64],
                                     lhsT=krb[k][:, ns * E + h * 64: ns * E + h * 64 + 64],
                                     rhs=vrb[k][:, ns * E + h * 64: ns * E + h * 64 + 64],
                                     start=(ns == 0), stop=(ns == 15)).then_inc(eST.h, 1)
                            if ns == 15 else
                            t.matmul(out=stk[k][:, h * 64:(h + 1) * 64],
                                     lhsT=krb[k][:, ns * E + h * 64: ns * E + h * 64 + 64],
                                     rhs=vrb[k][:, ns * E + h * 64: ns * E + h * 64 + 64],
                                     start=(ns == 0), stop=(ns == 15))))
            P.wait("vector", eST, eST.n)
            c2.n += 2
            P.on("vector", lambda v: v.tensor_copy(out=s01[0:64, :], in_=stk[0][:]).then_inc(sm["c2"].h, 1))
            P.on("vector", lambda v: v.tensor_copy(out=s01[64:128, :], in_=stk[1][:]).then_inc(sm["c2"].h, 1))
            P.wait("gpsimd", c2, c2.n)
            _dma(P, "gpsimd", cin_t[:, :], s01[:])
            P.wait("gpsimd", dG, dG.n)
            cc.n += 1
            P.on("gpsimd", lambda g: g.collective_compute(
                "AllReduce", OP.add, replica_groups=[[0, 1, 2, 3], [4, 5, 6, 7]],
                ins=[cin_t.ap().opt()], outs=[cout_t.ap().opt()]).then_inc(cc.h, 1))
            P.wait("gpsimd", cc, cc.n)
            _dma(P, "gpsimd", s01[:], cout_t[:, :])
            P.wait("vector", dG, dG.n)
            c2.n += 1
            P.on("vector", lambda v: v.tensor_copy(out=s1tmp[:], in_=s01[64:128, :]
                                                   ).then_inc(c2.h, 1))
            P.wait("vector", c2, c2.n)
            c2.n += 1
            P.on("vector", lambda v: v.tensor_tensor(out=s01[0:64, :], in0=s01[0:64, :],
                                                     in1=s1tmp[:], op=OP.mult
                                                     ).then_inc(c2.h, 1))
            P.wait("vector", c2, c2.n)
            c2.n += 1
            P.on("vector", lambda v: v.tensor_scalar(out=stb, in0=s01[0:64, :],
                                                     scalar1=0.5 ** 24, scalar2=None,
                                                     op0=OP.mult).then_inc(c2.h, 1))
            stb_done = c2.n
            for h in range(H):
                if h >= 1:
                    P.wait("sync", c0, c0.n)
                dtot = _dma(P, "sync", qth[h % 2][:], qtb[h], q=DQ[h % 2])
                P.wait("tensor", DQ[h % 2], dtot)
                P.wait("tensor", c2, stb_done)
                for rc in range(4):
                    pj = pqc[rc % 2]
                    if h * 4 + rc >= 2:
                        P.wait("tensor", c0, C3["c0"] + h * 4 + rc - 1)
                    eST.n += 1
                    P.on("tensor", lambda t, h=h, rc=rc, pj=pj: t.matmul(
                        out=pj[:], lhsT=stb[:, h * 64:(h + 1) * 64],
                        rhs=qth[h % 2][:, rc * 512:(rc + 1) * 512],
                        start=True, stop=True).then_inc(eST.h, 1))
                    P.wait("vector", eST, eST.n)
                    c0.n += 1
                    P.on("vector", lambda v, h=h, rc=rc, pj=pj: v.tensor_copy(
                        out=outT[(h % 2) * 64:(h % 2) * 64 + 64,
                                 (h // 2) * ROWS + rc * 512:(h // 2) * ROWS + rc * 512 + 512],
                        in_=pj[:]).then_inc(c0.h, 1))
            barrier(P)
            P.emit()

        with ExitStack() as st:
            en = st.enter_context
            orow = en(nc.sbuf_tensor("orowC", [128, E], F32))
            sqbC2 = en(nc.sbuf_tensor("sqbC2", [128, E], F32))
            xh = [en(nc.sbuf_tensor(f"xhC{i}", [128, E], BF16)) for i in range(2)]
            wob = en(nc.sbuf_tensor("wobC", [128, 8 * E], BF16))
            ostg = [en(nc.sbuf_tensor(f"ostgC{i}", [128, 512], F32)) for i in range(2)]
            mcol = en(nc.sbuf_tensor("mcolC7", [128, 4], F32))
            xhT = en(nc.sbuf_tensor("xhTC", [128, 8 * ROWS], BF16))
            pt7 = [en(nc.psum_tensor(f"pt7C{i}", [128, 512], F32)) for i in range(2)]
            px7 = [en(nc.psum_tensor(f"px7C{i}", [128, 512], F32)) for i in range(2)]
            pp7 = [en(nc.psum_tensor(f"pp7C{i}", [128, 512], F32)) for i in range(2)]

            P = Prog(nc)
            c0, c2, c3, c4, c5 = (sm[x] for x in ("c0", "c2", "c3", "c4", "c5"))
            eST, eEV, pTR = sm["eST"], sm["eEV"], sm["pTR"]
            C7 = {x: sm[x].n for x in ("eST", "eEV", "pTR", "c0", "c2", "c5")}
            _dma(P, "gpsimd", wob[:].rearrange("p (c n) -> p c n", n=E), wo_b.rearrange("(c p) n -> p c n", p=128))
            ntr = 0
            evac_c2 = []
            nor = 0
            for rt in range(NRT):
                for fc in range(8):
                    pj = pt7[nor % 2]
                    if nor >= 2:
                        P.wait("tensor", c2, evac_c2[nor - 2])
                    pTR.n += 1
                    P.on("tensor", lambda t, fc=fc, rt=rt, pj=pj: t.transpose(
                        out=pj[:, 0:128],
                        in_=outT[:, fc * ROWS + rt * 128: fc * ROWS + rt * 128 + 128],
                        identity=idf_sb[:]).then_inc(pTR.h, 1))
                    P.wait("vector", pTR, pTR.n)
                    c2.n += 1
                    evac_c2.append(c2.n)
                    P.on("vector", lambda v, fc=fc, pj=pj: v.tensor_copy(
                        out=orow[:, fc * 128:(fc + 1) * 128], in_=pj[:, 0:128]).then_inc(c2.h, 1))
                    nor += 1
                xhl = xh[rt % 2]
                if rt >= 2:
                    P.wait("vector", pTR, C7["pTR"] + (rt - 1) * 16)
                emit_ln(P, mcol, orow[:], orow[:], sqbC2[:], c2, c3, c4)
                c0.n += 1
                P.on("vector", lambda v, xhl=xhl: v.tensor_scalar(
                    out=xhl[:], in0=orow[:], scalar1=mcol[:, 3:4], scalar2=None,
                    op0=OP.mult).then_inc(c0.h, 1))
                P.wait("tensor", c0, c0.n)
                for fc in range(8):
                    gx = rt * 8 + fc
                    pj = px7[gx % 2]
                    if gx >= 2:
                        P.wait("tensor", c5, C7["c5"] + gx - 1)
                    pTR.n += 1
                    P.on("tensor", lambda t, xhl=xhl, fc=fc, pj=pj: t.transpose(
                        out=pj.bitcast(BF16)[:, 0:128], in_=xhl[:, fc * 128:(fc + 1) * 128],
                        identity=idb_sb[:]).then_inc(pTR.h, 1))
                    P.wait("scalar", pTR, pTR.n)
                    c5.n += 1
                    P.on("scalar", lambda s, pj=pj, fc=fc, rt=rt: s.activation(
                        out=xhT[:, fc * ROWS + rt * 128: fc * ROWS + rt * 128 + 128],
                        in_=pj.bitcast(BF16)[:, 0:128], func=AF.Identity,
                        scale=bcols[:, BC["lng"] + fc: BC["lng"] + fc + 1],
                        bias=bcols[:, BC["lnb"] + fc: BC["lnb"] + fc + 1]).then_inc(c5.h, 1))
                    ntr += 1
            P.wait("tensor", c5, c5.n)
            P.wait("tensor", c2, c2.n)
            P.wait("tensor", dG, dG.n)
            nj = 0
            for dp in range(8):
                for rc in range(4):
                    pj = pp7[nj % 2]
                    if nj >= 2:
                        P.wait("tensor", eEV, C7["eEV"] + nj - 1)
                    for fc in range(8):
                        if fc == 7:
                            eST.n += 1
                        P.on("tensor", lambda t, fc=fc, dp=dp, rc=rc, pj=pj: (
                            t.matmul(out=pj[:],
                                     lhsT=wob[:, fc * E + dp * 128: fc * E + dp * 128 + 128],
                                     rhs=xhT[:, fc * ROWS + rc * 512: fc * ROWS + rc * 512 + 512],
                                     start=(fc == 0), stop=(fc == 7)).then_inc(eST.h, 1)
                            if fc == 7 else
                            t.matmul(out=pj[:],
                                     lhsT=wob[:, fc * E + dp * 128: fc * E + dp * 128 + 128],
                                     rhs=xhT[:, fc * ROWS + rc * 512: fc * ROWS + rc * 512 + 512],
                                     start=(fc == 0), stop=(fc == 7))))
                    P.wait("scalar", eST, eST.n)
                    if nj >= 2:
                        P.wait("scalar", c0, C7["c0"] + 16 + (nj - 1) * 4)
                    eEV.n += 1
                    osg = ostg[nj % 2]
                    P.on("scalar", lambda s, pj=pj, dp=dp, osg=osg: s.activation(
                        out=osg[:], in_=pj[:], func=AF.Identity,
                        bias=bcols[:, BC["bo_b"] + dp: BC["bo_b"] + dp + 1]).then_inc(eEV.h, 1))
                    P.wait("tensor", eEV, eEV.n)
                    for tt in range(4):
                        pj2 = pp7[(nj + 1) % 2] if False else pt7[tt % 2]
                        P.wait("tensor", c0, c0.n - 1)
                        pTR.n += 1
                        P.on("tensor", lambda t, osg=osg, tt=tt, pj2=pj2: t.transpose(
                            out=pj2[:, 0:128], in_=osg[:, tt * 128:(tt + 1) * 128],
                            identity=idf_sb[:]).then_inc(pTR.h, 1))
                        P.wait("vector", pTR, pTR.n)
                        rt = rc * 4 + tt
                        c0.n += 1
                        P.on("vector", lambda v, rt=rt, dp=dp, pj2=pj2: v.tensor_tensor(
                            out=xres[:, rt * E + dp * 128: rt * E + dp * 128 + 128],
                            in0=xres[:, rt * E + dp * 128: rt * E + dp * 128 + 128],
                            in1=pj2[:, 0:128], op=OP.add).then_inc(c0.h, 1))
                    nj += 1
            barrier(P)
            P.emit()
        c_outer.close()

    if nphase >= 4:
        emit_dequant()
        emit_ffn("F1")
    if nphase >= 5:
        emit_attnb()
    if nphase >= 6:
        emit_ffn("F2")

    P = Prog(nc)
    if not a4_on:
        c0 = sm["c0"]
        c0.n += 1
        P.on("vector", lambda v: v.memset(xres[:], 0.0).then_inc(c0.h, 1))
        P.wait("sync", c0, c0.n)
    ov = out.rearrange("(t p) d -> p t d", p=128)
    xv = xres[:].rearrange("p (t d) -> p t d", d=E)
    for j in range(4):
        _dma(P, "sync", ov[:, j * 4:(j + 1) * 4, :], xv[:, j * 4:(j + 1) * 4, :])
    dbgsel = os.environ.get("MK_DBG", "")
    if dbgsel:
        dsrc = {
            "ksr0": ksr[0], "ksr1": ksr[1], "vsr0": vsr[0], "vsr1": vsr[1],
            "qtb": qtb.rearrange("a d (m n) -> (a d m) n", n=E),
            "cout": cout_t.ap(),
            "accs": accs,
        }[dbgsel]
        shp = dsrc.shape
        P.wait("gpsimd", dS, dS.n)
        dn = _dma(P, "gpsimd", out[0:shp[0], 0:shp[1]], dsrc)
        P.wait("sync", dG, dn)
    P.wait("sync", dS, dS.n)
    P.emit()

    outer.close()
    return nc


SCALE = 1.0 / math.sqrt(D)


_NC_CACHE = {}
LAST_RESULTS = None


def kernel(L0_attn_a_x, L0_attn_a_Wq, L0_attn_a_bq, L0_attn_a_Wk, L0_attn_a_bk,
           L0_attn_a_Wv, L0_attn_a_bv, L0_attn_a_Wo, L0_attn_a_bo,
           L0_attn_a_phase_modulators, L0_attn_a_interference_weights,
           L0_attn_b_Wq, L0_attn_b_bq, L0_attn_b_Wk, L0_attn_b_bk,
           L0_attn_b_Wv, L0_attn_b_bv, L0_attn_b_Wo, L0_attn_b_bo,
           L0_attn_b_ln_g, L0_attn_b_ln_b,
           L0_ffn_ln_gamma, L0_ffn_ln_beta, L0_ffn_fc1_q, L0_ffn_fc1_scales,
           L0_ffn_fc1_bias, L0_ffn_fc2_q, L0_ffn_fc2_scales, L0_ffn_fc2_bias):
    nphase = int(os.environ.get("MK_NPHASE", "6"))
    x = np.ascontiguousarray(np.asarray(L0_attn_a_x, dtype=np.float32))
    f32 = lambda a: np.ascontiguousarray(np.asarray(a, dtype=np.float32))
    i32 = lambda a: np.ascontiguousarray(np.asarray(a, dtype=np.int32))

    common = dict(
        wq_a=f32(L0_attn_a_Wq), wk_a=f32(L0_attn_a_Wk), wv_a=f32(L0_attn_a_Wv),
        wo_a=f32(L0_attn_a_Wo), bq_a=f32(L0_attn_a_bq), bk_a=f32(L0_attn_a_bk),
        bv_a=f32(L0_attn_a_bv), bo_a=f32(L0_attn_a_bo),
        pmt=np.ascontiguousarray(f32(L0_attn_a_phase_modulators).T),
        iw=f32(L0_attn_a_interference_weights),
        f_g=f32(L0_ffn_ln_gamma), f_b=f32(L0_ffn_ln_beta),
        w1q=i32(L0_ffn_fc1_q), s1=f32(L0_ffn_fc1_scales), fb1=f32(L0_ffn_fc1_bias),
        w2q=i32(L0_ffn_fc2_q), s2=f32(L0_ffn_fc2_scales), fb2=f32(L0_ffn_fc2_bias),
        wq_b=f32(L0_attn_b_Wq), wk_b0=f32(L0_attn_b_Wk[0]), wk_b1=f32(L0_attn_b_Wk[1]),
        wv_b0=f32(L0_attn_b_Wv[0]), wv_b1=f32(L0_attn_b_Wv[1]), wo_b=f32(L0_attn_b_Wo),
        bq_b=f32(L0_attn_b_bq), bk_b0=f32(L0_attn_b_bk[0]), bk_b1=f32(L0_attn_b_bk[1]),
        bv_b0=f32(L0_attn_b_bv[0]), bv_b1=f32(L0_attn_b_bv[1]), bo_b=f32(L0_attn_b_bo),
        lng=f32(L0_attn_b_ln_g), lnb=f32(L0_attn_b_ln_b),
        idf=np.eye(128, dtype=np.float32),
    )

    key = nphase
    if key not in _NC_CACHE:
        _NC_CACHE[key] = build(nphase)
    nc = _NC_CACHE[key]

    in_maps = []
    for c in range(8):
        b = c // 4
        off = (c % 4) * ROWS
        xq = np.roll(x[b], -off, axis=0)
        m = dict(common)
        m["xb"] = np.ascontiguousarray(xq)
        in_maps.append(m)

    res = run_bass_kernel_spmd(nc, in_maps, core_ids=list(range(8)))
    global LAST_RESULTS
    LAST_RESULTS = res.results
    shards = [np.asarray(res.results[c]["out"]) for c in range(8)]
    full = np.concatenate(shards, axis=0).reshape(B, S, E)
    return full.astype(np.float32)


# version 09
# speedup vs baseline: 1.0446x; 1.0020x over previous
import math
import os
from contextlib import ExitStack

import numpy as np
import ml_dtypes

import concourse.bass as bass
import concourse.mybir as mybir
from concourse.alu_op_type import AluOpType as OP
from concourse.bass_utils import run_bass_kernel_spmd

F32 = mybir.dt.float32
BF16 = mybir.dt.bfloat16
I32 = mybir.dt.int32
AF = mybir.ActivationFunctionType
AX = mybir.AxisListType

B, S, E = 2, 8192, 1024
H, D = 16, 64
WL = 16
ROWS = 2048
NRT = ROWS // 128
KERR = 2.7e-18 * 1e6
EPS = 1e-5
FF = 4096
NT = S // 128
NQ = 256
QCN = ROWS // NQ
TWO_PI = 2.0 * math.pi
ENGS = ("sync", "vector", "tensor", "scalar", "gpsimd")


class Sem:
    def __init__(self, handle):
        self.h = handle
        self.n = 0


class Prog:
    def __init__(self, nc):
        self.nc = nc
        self.q = {k: [] for k in ENGS}

    def on(self, eng, fn):
        self.q[eng].append(fn)

    def wait(self, eng, sem, val):
        if val > 0:
            self.q[eng].append(lambda e, s=sem.h, v=val: e.wait_ge(s, v))

    def emit(self):
        q = self.q
        with self.nc.Block() as blk:
            @blk.sync
            def _(e):
                for fn in q["sync"]:
                    fn(e)

            @blk.vector
            def _(e):
                for fn in q["vector"]:
                    fn(e)

            @blk.tensor
            def _(e):
                for fn in q["tensor"]:
                    fn(e)

            @blk.scalar
            def _(e):
                for fn in q["scalar"]:
                    fn(e)

            @blk.gpsimd
            def _(e):
                for fn in q["gpsimd"]:
                    fn(e)


def build(nphase=6):
    nc = bass.Bass("TRN2", target_bir_lowering=False)

    dri = lambda nm, shp, dt: nc.dram_tensor(nm, shp, dt, kind="ExternalInput").ap()
    xb = dri("xb", [S, E], F32)
    wq_a, wk_a, wv_a, wo_a = (dri(n, [E, E], F32) for n in ("wq_a", "wk_a", "wv_a", "wo_a"))
    bq_a, bk_a, bv_a, bo_a = (dri(n, [E], F32) for n in ("bq_a", "bk_a", "bv_a", "bo_a"))
    pmt = dri("pmt", [H, WL], F32)
    iw = dri("iw", [WL], F32)
    f_g, f_b = dri("f_g", [E], F32), dri("f_b", [E], F32)
    w1q = dri("w1q", [E, FF], I32)
    s1 = dri("s1", [8, FF], F32)
    fb1 = dri("fb1", [FF], F32)
    w2q = dri("w2q", [FF, E], I32)
    s2 = dri("s2", [32, E], F32)
    fb2 = dri("fb2", [E], F32)
    wq_b, wk_b0, wk_b1, wv_b0, wv_b1, wo_b = (
        dri(n, [E, E], F32) for n in ("wq_b", "wk_b0", "wk_b1", "wv_b0", "wv_b1", "wo_b"))
    bq_b, bk_b0, bk_b1, bv_b0, bv_b1, bo_b = (
        dri(n, [E], F32) for n in ("bq_b", "bk_b0", "bk_b1", "bv_b0", "bv_b1", "bo_b"))
    lng, lnb = dri("lng", [E], F32), dri("lnb", [E], F32)
    idf = dri("idf", [128, 128], F32)
    out = nc.dram_tensor("out", [ROWS, E], F32, kind="ExternalOutput").ap()

    kts = nc.dram_tensor("kts", [H, D, S], BF16).ap()
    vts = nc.dram_tensor("vts", [H, D, S], BF16).ap()
    qts = nc.dram_tensor("qts", [H, D, ROWS], BF16).ap()
    accs = nc.dram_tensor("accs", [E, ROWS], BF16).ap()
    flatr = nc.dram_tensor("flatr", [WL * H], F32).ap()
    w1ds = nc.dram_tensor("w1ds", [32, 128, 8, 128], BF16).ap()
    w2ds = nc.dram_tensor("w2ds", [8, 128, 32, 128], BF16).ap()
    ksr = nc.dram_tensor("ksr", [2, ROWS, E], BF16).ap()
    vsr = nc.dram_tensor("vsr", [2, ROWS, E], BF16).ap()
    qtb = nc.dram_tensor("qtb", [H, D, ROWS], BF16).ap()
    cin_t = nc.dram_tensor("cin_st", [128, E], F32)
    cout_t = nc.dram_tensor("cout_st", [128, E], F32)

    outer = ExitStack()
    names = ["dS", "dG", "c0", "c2", "c3", "c4", "c5",
             "eST", "eEV", "eXP", "eAQ", "eFN", "rRQ", "rBC", "rTM",
             "pVT", "vSB", "cAC", "pPJ", "pTR",
             "dx0", "dx1", "dx2", "dx3", "ds0", "ds1", "ds2", "ds3",
             "dk0", "dk1", "dn0", "dn1", "dq0", "dq1", "dw0", "dw1", "drb",
             "da0", "da1",
             "bS", "bV", "bT", "bA", "bG"]
    sm = {n: Sem(outer.enter_context(nc.semaphore(n))) for n in names}
    dS, dG = sm["dS"], sm["dG"]
    DX = [sm[f"dx{i}"] for i in range(4)]
    DS4 = [sm[f"ds{i}"] for i in range(4)]
    DK = [sm[f"dk{i}"] for i in range(2)]
    DN = [sm[f"dn{i}"] for i in range(2)]
    DQ = [sm[f"dq{i}"] for i in range(2)]
    DW = [sm[f"dw{i}"] for i in range(2)]
    DRB = sm["drb"]
    DA = [sm[f"da{i}"] for i in range(2)]
    SYNC_QS = [dS] + DX + DS4 + DK + DN + DQ + DW + [DRB] + DA
    BARMAP = dict(zip(ENGS, [sm["bS"], sm["bV"], sm["bT"], sm["bA"], sm["bG"]]))
    PH = [0]

    def barrier(P):
        PH[0] += 1
        tgt = PH[0]
        for q in SYNC_QS:
            P.wait("sync", q, q.n)
        P.wait("gpsimd", dG, dG.n)
        for en in ENGS:
            b = BARMAP[en]
            b.n += 1
            P.on(en, lambda e, _b=b: e.sem_inc(_b.h, 1))
        for en in ENGS:
            for b in BARMAP.values():
                P.wait(en, b, tgt)

    def _dma(P, eng, dst, src, q=None, slow=False):
        q = q if q is not None else (dG if eng == "gpsimd" else dS)
        q.n += 16
        P.on(eng, lambda e, d=dst, s=src, h=q.h, sl=slow: e.dma_start(
            out=d, in_=s, allow_slow_non_contiguous=sl).then_inc(h, 16))
        return q.n

    ones_f = outer.enter_context(nc.sbuf_tensor("ones_f", [1, 128], F32))
    ones_b = outer.enter_context(nc.sbuf_tensor("ones_b", [64, 1], BF16))
    idf_sb = outer.enter_context(nc.sbuf_tensor("idf_sb", [128, 128], F32))
    idb_sb = outer.enter_context(nc.sbuf_tensor("idb_sb", [128, 128], BF16))
    epsc = outer.enter_context(nc.sbuf_tensor("epsc", [128, 1], F32))
    bcols = outer.enter_context(nc.sbuf_tensor("bcols", [128, 8 * 11], F32))
    BCN = ["bq_a", "bk_a", "bv_a", "bo_a", "f_g", "f_b", "fb2h", "bo_b", "lng", "lnb", "bq_b"]
    BC = {n: i * 8 for i, n in enumerate(BCN)}
    fb1c = outer.enter_context(nc.sbuf_tensor("fb1c", [128, 32], F32))
    fb2c = outer.enter_context(nc.sbuf_tensor("fb2c", [128, 8], F32))
    bkvc = outer.enter_context(nc.sbuf_tensor("bkvc", [128, 5 * 8], F32))
    stackA = ExitStack()
    sccol = stackA.enter_context(nc.sbuf_tensor("sccol", [128, 512], F32))
    iwd = stackA.enter_context(nc.sbuf_tensor("iwd", [1, WL], F32))
    stackA2 = ExitStack()
    misc = stackA2.enter_context(nc.sbuf_tensor("misc", [1, 16], F32))
    arow = stackA2.enter_context(nc.sbuf_tensor("arow", [1, 512], F32))
    ncur = stackA2.enter_context(nc.sbuf_tensor("ncur", [1, 4], F32))

    P = Prog(nc)
    _dma(P, "sync", idf_sb[:], idf)
    for nm, src in [("bq_a", bq_a), ("bk_a", bk_a), ("bv_a", bv_a), ("bo_a", bo_a),
                    ("f_g", f_g), ("f_b", f_b), ("fb2h", fb2), ("bo_b", bo_b),
                    ("lng", lng), ("lnb", lnb), ("bq_b", bq_b)]:
        _dma(P, "sync", bcols[:, BC[nm]:BC[nm] + 8], src.rearrange("(c p) -> p c", p=128), slow=True)
    _dma(P, "sync", fb1c[:], fb1.rearrange("(c p) -> p c", p=128), slow=True)
    for i, src in enumerate([bq_b, bk_b0, bk_b1, bv_b0, bv_b1]):
        _dma(P, "sync", bkvc[:, i * 8:(i + 1) * 8], src.rearrange("(c p) -> p c", p=128), slow=True)
    _dma(P, "sync", iwd[:], iw.unsqueeze(0))
    P.wait("vector", dS, dS.n)
    P.on("vector", lambda v: v.memset(ones_f[:], 1.0))
    P.on("vector", lambda v: v.memset(ones_b[:], 1.0))
    P.on("vector", lambda v: v.memset(epsc[:], EPS))
    P.on("vector", lambda v: v.memset(ncur[:], 0.0))
    P.on("vector", lambda v: v.tensor_copy(out=idb_sb[:], in_=idf_sb[:]))
    P.on("vector", lambda v: v.tensor_scalar(out=iwd[:], in0=iwd[:], scalar1=1.0 / 16.0,
                                             scalar2=None, op0=OP.mult))
    P.on("vector", lambda v: v.tensor_scalar(out=fb2c[:], in0=bcols[:, BC["fb2h"]:BC["fb2h"] + 8],
                                             scalar1=0.5, scalar2=None, op0=OP.mult))
    barrier(P)
    P.emit()

    def emit_ln(P, mcol, src, center_dst, sqbuf, c2, c3, c4):
        c2.n += 1
        P.on("vector", lambda v: v.reduce_sum(out=mcol[:, 0:1], in_=src, axis=AX.X,
                                              negate=True).then_inc(c2.h, 1))
        P.wait("vector", c2, c2.n)
        c2.n += 1
        P.on("vector", lambda v: v.tensor_scalar(out=mcol[:, 0:1], in0=mcol[:, 0:1],
                                                 scalar1=1.0 / E, scalar2=None,
                                                 op0=OP.mult).then_inc(c2.h, 1))
        P.wait("vector", c2, c2.n)
        c2.n += 1
        P.on("vector", lambda v: v.tensor_scalar(out=center_dst, in0=src,
                                                 scalar1=mcol[:, 0:1], scalar2=None,
                                                 op0=OP.add).then_inc(c2.h, 1))
        P.wait("vector", c2, c2.n)
        c2.n += 1
        P.on("vector", lambda v: v.tensor_tensor(out=sqbuf, in0=center_dst, in1=center_dst,
                                                 op=OP.mult).then_inc(c2.h, 1))
        P.wait("vector", c2, c2.n)
        c3.n += 1
        P.on("vector", lambda v: v.reduce_sum(out=mcol[:, 1:2], in_=sqbuf,
                                              axis=AX.X).then_inc(c3.h, 1))
        P.wait("scalar", c3, c3.n)
        c4.n += 1
        P.on("scalar", lambda s: s.activation(out=mcol[:, 2:3], in_=mcol[:, 1:2], func=AF.Ln,
                                              scale=1.0 / E, bias=epsc[:]).then_inc(c4.h, 1))
        P.wait("scalar", c4, c4.n)
        c4.n += 1
        P.on("scalar", lambda s: s.activation(out=mcol[:, 3:4], in_=mcol[:, 2:3], func=AF.Exp,
                                              scale=-0.5).then_inc(c4.h, 1))
        P.wait("vector", c4, c4.n)

    with ExitStack() as st:
        en = st.enter_context
        wqs = en(nc.sbuf_tensor("wqs", [128, 8 * E], BF16))
        wks = en(nc.sbuf_tensor("wks", [128, 8 * E], BF16))
        wvs = en(nc.sbuf_tensor("wvs", [128, 8 * E], BF16))
        xt = [en(nc.sbuf_tensor(f"xt{i}", [128, E], F32)) for i in range(4)]
        sqb = en(nc.sbuf_tensor("sqb", [128, E], F32))
        xlb = [en(nc.sbuf_tensor(f"xlb{i}", [128, E], BF16)) for i in range(2)]
        xlnt = en(nc.sbuf_tensor("xlnt", [128, 8 * 512], BF16))
        stg = [en(nc.sbuf_tensor(f"stg{i}", [128, 512], BF16)) for i in range(4)]
        mcol = en(nc.sbuf_tensor("mcolA", [128, 4], F32))
        ptrp = [en(nc.psum_tensor(f"ptrp{i}", [128, 512], F32)) for i in range(2)]
        ppp = [en(nc.psum_tensor(f"pppA{i}", [128, 512], F32)) for i in range(2)]

        P = Prog(nc)
        c0, c2, c3, c4, c5 = (sm[x] for x in ("c0", "c2", "c3", "c4", "c5"))
        pTR, pPJ, eST, eEV = sm["pTR"], sm["pPJ"], sm["eST"], sm["eEV"]
        b0 = {x: sm[x].n for x in ("c0", "eST", "eEV", "pTR", "pPJ", "c5")}

        for wsb, wsrc in [(wqs, wq_a), (wks, wk_a), (wvs, wv_a)]:
            _dma(P, "gpsimd", wsb[:].rearrange("p (c n) -> p c n", n=E), wsrc.rearrange("(c p) n -> p c n", p=128))
        wload = dG.n

        stg_dma = {}
        chunk_est = {}
        nstg = 0
        for ch in range(16):
            own = ch < 4
            for r in range(4):
                i = ch * 4 + r
                if i >= 4:
                    P.wait("sync", c0, b0["c0"] + i - 3)
                dtot = _dma(P, "sync", xt[i % 4][:], xb[i * 128:(i + 1) * 128, :],
                            q=DX[i % 4])
                x = xt[i % 4][:]
                xl = xlb[i % 2]
                P.wait("vector", DX[i % 4], dtot)
                if i >= 2:
                    P.wait("vector", pTR, b0["pTR"] + (i - 1) * 8)
                emit_ln(P, mcol, x, x, sqb[:], c2, c3, c4)
                c0.n += 1
                P.on("vector", lambda v, x=x, xl=xl: v.tensor_scalar(
                    out=xl[:], in0=x, scalar1=mcol[:, 3:4], scalar2=None,
                    op0=OP.mult).then_inc(c0.h, 1))
                P.wait("tensor", c0, c0.n)
                if r == 0:
                    pass
                for fc in range(8):
                    gt = i * 8 + fc
                    pj = ptrp[gt % 2]
                    if gt >= 2:
                        P.wait("tensor", c5, b0["c5"] + gt - 1)
                    pTR.n += 1
                    P.on("tensor", lambda t, xl=xl, fc=fc, pj=pj: t.transpose(
                        out=pj.bitcast(BF16)[:, 0:128], in_=xl[:, fc * 128:(fc + 1) * 128],
                        identity=idb_sb[:]).then_inc(pTR.h, 1))
                    P.wait("vector", pTR, pTR.n)
                    if fc == 0 and r == 0 and ch > 0:
                        P.wait("vector", eST, chunk_est[ch - 1])
                    c5.n += 1
                    P.on("vector", lambda v, pj=pj, fc=fc, r=r: v.tensor_copy(
                        out=xlnt[:, fc * 512 + r * 128: fc * 512 + r * 128 + 128],
                        in_=pj.bitcast(BF16)[:, 0:128]).then_inc(c5.h, 1))
            xlnt_ready = c5.n
            P.wait("tensor", c5, xlnt_ready)
            if ch == 0:
                P.wait("tensor", dG, wload)
            groups = [("K", wks, BC["bk_a"], kts), ("V", wvs, BC["bv_a"], vts)]
            if own:
                groups.append(("Q", wqs, BC["bq_a"], qts))
            for gn, wsb, bcoff, dstt in groups:
                for dp in range(8):
                    jj = nstg
                    pj = ppp[jj % 2]
                    sgt = stg[jj % 4]
                    if jj >= 2:
                        P.wait("tensor", eEV, b0["eEV"] + jj - 1)
                    for fc in range(8):
                        if fc == 7:
                            eST.n += 1
                        P.on("tensor", lambda t, wsb=wsb, fc=fc, dp=dp, pj=pj: (
                            t.matmul(out=pj[:],
                                     lhsT=wsb[:, fc * E + dp * 128: fc * E + dp * 128 + 128],
                                     rhs=xlnt[:, fc * 512:(fc + 1) * 512],
                                     start=(fc == 0), stop=(fc == 7)).then_inc(eST.h, 1)
                            if fc == 7 else
                            t.matmul(out=pj[:],
                                     lhsT=wsb[:, fc * E + dp * 128: fc * E + dp * 128 + 128],
                                     rhs=xlnt[:, fc * 512:(fc + 1) * 512],
                                     start=(fc == 0), stop=(fc == 7))))
                    P.wait("scalar", eST, eST.n)
                    if jj >= 4:
                        P.wait("scalar", DS4[jj % 4], stg_dma[jj - 4])
                    eEV.n += 1
                    P.on("scalar", lambda s, pj=pj, sgt=sgt, bcoff=bcoff, dp=dp: s.activation(
                        out=sgt[:], in_=pj[:], func=AF.Identity,
                        bias=bcols[:, bcoff + dp: bcoff + dp + 1]).then_inc(eEV.h, 1))
                    P.wait("sync", eEV, eEV.n)
                    base = ch * 512
                    _dma(P, "sync", dstt[2 * dp, :, base:base + 512], sgt[0:64, :],
                         q=DS4[jj % 4])
                    stg_dma[jj] = _dma(P, "sync", dstt[2 * dp + 1, :, base:base + 512],
                                       sgt[64:128, :], q=DS4[jj % 4])
                    nstg += 1
            chunk_est[ch] = eST.n
        barrier(P)
        P.emit()

    with ExitStack() as st:
        en = st.enter_context
        nbuf = en(nc.sbuf_tensor("nbuf", [64, S], BF16))
        nps = en(nc.sbuf_tensor("nps", [1, 512], F32))
        pm_sb = en(nc.sbuf_tensor("pm_sb", [H, WL], F32))
        t16 = en(nc.sbuf_tensor("t16", [H, WL], F32))
        t16b = en(nc.sbuf_tensor("t16b", [H, WL], F32))
        pnp = [en(nc.psum_tensor(f"pnp{i}", [128, 512], F32)) for i in range(2)]
        psc = en(nc.psum_tensor("psc", [128, 512], F32))

        P = Prog(nc)
        c2, c3 = sm["c2"], sm["c3"]
        eST = sm["eST"]

        def vstep(fn):
            c2.n += 1
            P.on("vector", lambda v, f=fn: f(v).then_inc(c2.h, 1))
            P.wait("vector", c2, c2.n)

        def sstep(fn):
            P.wait("scalar", c2, c2.n)
            c3.n += 1
            P.on("scalar", lambda s, f=fn: f(s).then_inc(c3.h, 1))
            P.wait("vector", c3, c3.n)

        for src, ncol, ncols in [(kts, 1, S), (qts, 0, ROWS)]:
            for h in range(H):
                P.wait("sync", c2, c2.n)
                _dma(P, "sync", nbuf[:, 0:ncols], src[h])
                P.wait("vector", dS, dS.n)
                vstep(lambda v, n=ncols: v.tensor_tensor(
                    out=nbuf[:, 0:n], in0=nbuf[:, 0:n], in1=nbuf[:, 0:n], op=OP.mult))
                for j in range(ncols // 512):
                    pj = pnp[j % 2]
                    P.wait("tensor", c2, c2.n)
                    eST.n += 1
                    P.on("tensor", lambda t, pj=pj, j=j: t.matmul(
                        out=pj[0:1, :], lhsT=ones_b[:], rhs=nbuf[:, j * 512:(j + 1) * 512],
                        start=True, stop=True).then_inc(eST.h, 1))
                    P.wait("vector", eST, eST.n)
                    vstep(lambda v, pj=pj: v.tensor_copy(out=nps[:], in_=pj[0:1, :]))
                    vstep(lambda v: v.reduce_max(out=nps[0:1, 0:1], in_=nps[:], axis=AX.X))
                    vstep(lambda v, c=ncol: v.tensor_tensor(
                        out=ncur[0:1, c:c + 1], in0=ncur[0:1, c:c + 1],
                        in1=nps[0:1, 0:1], op=OP.max))

        P.wait("sync", c2, c2.n)
        _dma(P, "sync", pm_sb[:], pmt)
        P.wait("vector", dS, dS.n)
        MAGIC = 12582912.0
        vstep(lambda v: v.tensor_scalar(out=t16[:], in0=pm_sb[:], scalar1=1.0 / TWO_PI,
                                        scalar2=MAGIC, op0=OP.mult, op1=OP.add))
        vstep(lambda v: v.tensor_scalar(out=t16[:], in0=t16[:], scalar1=-MAGIC,
                                        scalar2=None, op0=OP.add))
        vstep(lambda v: v.scalar_tensor_tensor(out=t16[:], in0=t16[:], scalar=-TWO_PI,
                                               in1=pm_sb[:], op0=OP.mult, op1=OP.add))
        sstep(lambda s: s.activation(out=t16b[:], in_=t16[:], func=AF.Sin, scale=0.5))
        vstep(lambda v: v.tensor_tensor(out=t16[:], in0=t16b[:], in1=t16b[:], op=OP.mult))
        vstep(lambda v: v.tensor_scalar(out=t16[:], in0=t16[:], scalar1=-2.0, scalar2=1.0,
                                        op0=OP.mult, op1=OP.add))
        vstep(lambda v: v.tensor_tensor(out=t16b[:], in0=t16[:], in1=t16[:], op=OP.mult))
        vstep(lambda v: v.tensor_scalar(out=t16b[:], in0=t16b[:], scalar1=0.125,
                                        scalar2=None, op0=OP.mult))
        P.wait("sync", c2, c2.n)
        _dma(P, "sync", flatr.rearrange("(h c) -> h c", c=WL), t16b[:])
        P.wait("sync", dS, dS.n)
        _dma(P, "sync", arow[0:1, 0:256], flatr.unsqueeze(0))
        P.wait("vector", dS, dS.n)
        vstep(lambda v: v.reduce_max(out=misc[0:1, 4:5], in_=arow[0:1, 0:256], axis=AX.X))
        vstep(lambda v: v.tensor_tensor(out=misc[0:1, 5:6], in0=ncur[0:1, 0:1],
                                        in1=ncur[0:1, 1:2], op=OP.mult))
        sstep(lambda s: s.activation(out=misc[0:1, 6:7], in_=misc[0:1, 5:6], func=AF.Ln))
        P.wait("scalar", c3, c3.n)
        c3.n += 1
        P.on("scalar", lambda s: s.activation(out=misc[0:1, 2:3], in_=misc[0:1, 6:7],
                                              func=AF.Exp, scale=0.5).then_inc(c3.h, 1))
        P.wait("vector", c3, c3.n)
        vstep(lambda v: v.tensor_tensor(out=misc[0:1, 7:8], in0=misc[0:1, 4:5],
                                        in1=misc[0:1, 2:3], op=OP.mult))
        vstep(lambda v: v.tensor_tensor(out=misc[0:1, 8:9], in0=misc[0:1, 7:8],
                                        in1=misc[0:1, 7:8], op=OP.mult))
        sstep(lambda s: s.activation(out=misc[0:1, 9:10], in_=misc[0:1, 8:9], func=AF.Sin,
                                     scale=KERR / 2.0))
        vstep(lambda v: v.tensor_tensor(out=misc[0:1, 10:11], in0=misc[0:1, 9:10],
                                        in1=misc[0:1, 9:10], op=OP.mult))
        vstep(lambda v: v.tensor_scalar(out=misc[0:1, 3:4], in0=misc[0:1, 10:11],
                                        scalar1=-2.0, scalar2=1.0, op0=OP.mult, op1=OP.add))
        vstep(lambda v: v.tensor_scalar(out=arow[0:1, 0:256], in0=arow[0:1, 0:256],
                                        scalar1=misc[0:1, 3:4], scalar2=None, op0=OP.mult))
        vstep(lambda v: v.tensor_scalar(out=arow[0:1, 256:512], in0=arow[0:1, 0:256],
                                        scalar1=misc[0:1, 2:3], scalar2=-1.0,
                                        op0=OP.mult, op1=OP.mult))
        P.wait("tensor", c2, c2.n)
        eST.n += 1
        P.on("tensor", lambda t: t.matmul(out=psc[:], lhsT=ones_f[:], rhs=arow[:],
                                          start=True, stop=True).then_inc(eST.h, 1))
        P.wait("vector", eST, eST.n)
        vstep(lambda v: v.tensor_copy(out=sccol[:], in_=psc[:]))
        barrier(P)
        P.emit()

    stackA2.close()
    if nphase < 2:
        NH_RUN = 0
    else:
        NH_RUN = int(os.environ.get("MK_NH", str(H)))
    A3CUT = int(os.environ.get("MK_A3CUT", "5"))
    with ExitStack() as st:
        en = st.enter_context
        spr = en(nc.sbuf_tensor("spr", [128, 4 * 8192], F32))
        pbuf = en(nc.sbuf_tensor("pbuf", [128, 2 * 32 * NQ], BF16))
        ktb1 = en(nc.sbuf_tensor("ktb0", [64, S], BF16))
        ktb = [ktb1, ktb1]
        qtbu1 = en(nc.sbuf_tensor("qtbu0", [64, ROWS], BF16))
        qtbu = [qtbu1, qtbu1]
        vtb1 = en(nc.sbuf_tensor("vtb0", [128, S // 2], BF16))
        vtb = [vtb1, vtb1]
        vsb1 = en(nc.sbuf_tensor("vsb0", [128, NT * 65], BF16))
        vsb = [vsb1, vsb1]
        accf = [en(nc.sbuf_tensor(f"accf{i}", [64, NQ], F32)) for i in range(2)]
        rpair = en(nc.sbuf_tensor("rpair", [1, 512], F32))
        osb = en(nc.sbuf_tensor("osb", [64, NQ], F32))
        tmpf = en(nc.sbuf_tensor("tmpf", [64, NQ], F32))
        stp = [en(nc.psum_tensor(f"stp{i}", [128, 1024], F32)) for i in range(2)]
        avp = [en(nc.psum_tensor(f"avp{i}", [128, 512], F32)) for i in range(2)]
        rbcp = en(nc.psum_tensor("rbcp", [128, 512], F32))
        vtpp = [en(nc.psum_tensor("vtpp0", [128, 256], BF16))] * 2

        P = Prog(nc)
        eST, eEV, eXP, eAQ, eFN = (sm[x] for x in ("eST", "eEV", "eXP", "eAQ", "eFN"))
        rRQ, rBC, rTM = sm["rRQ"], sm["rBC"], sm["rTM"]
        pVT, vSB, cAC, c2 = sm["pVT"], sm["vSB"], sm["cAC"], sm["c2"]
        A = {x: sm[x].n for x in ("eST", "eEV", "eXP", "eAQ", "eFN", "rRQ", "rBC",
                                  "rTM", "pVT", "vSB", "c2", "cAC")}
        dS_kv = {}
        acc_dma = {}
        vsb_done = {}

        P.on("vector", lambda v: v.memset(
            vsb[0].rearrange("p (t c) -> p t c", c=65)[:, :, 64:65], 1.0))

        for h in range(NH_RUN):
            if h >= 1:
                P.wait("sync", eST, A["eST"] + h * 128)
            if h >= 1 and A3CUT >= 3:
                P.wait("sync", pVT, A["pVT"] + h * (NT // 2))
            _dma(P, "sync", ktb[h % 2][:], kts[h], q=DK[h % 2])
            _dma(P, "sync", qtbu[h % 2][:], qts[h], q=DK[h % 2])
            vsrc = vts[h].rearrange("d (u two j) -> d two u j", two=2, j=128)
            _dma(P, "sync", vtb[h % 2][0:64, :].rearrange("d (u j) -> d u j", j=128),
                 vsrc[:, 0, :, :], q=DK[h % 2])
            dS_kv[h] = _dma(P, "sync",
                            vtb[h % 2][64:128, :].rearrange("d (u j) -> d u j", j=128),
                            vsrc[:, 1, :, :], q=DK[h % 2])

        def emit_fills(i, trange=None):
            h, qc = divmod(i, QCN)
            P.wait("tensor", DK[h % 2], dS_kv[h])
            for tf in (trange if trange is not None else range(16)):
                gt = i * 16 + tf
                pj = stp[gt % 2]
                if gt >= 2:
                    P.wait("tensor", eEV, A["eEV"] + gt - 1)
                for sub in range(4):
                    t_ = tf * 4 + sub
                    if sub == 3:
                        eST.n += 1
                    P.on("tensor", lambda t, h=h, t_=t_, sub=sub, pj=pj, qc=qc: (
                        t.matmul(out=pj[:, (t_ % 4) * 256:(t_ % 4) * 256 + 256],
                                 lhsT=ktb[h % 2][:, t_ * 128:(t_ + 1) * 128],
                                 rhs=qtbu[h % 2][:, qc * NQ:(qc + 1) * NQ],
                                 start=True, stop=True).then_inc(eST.h, 1)
                        if sub == 3 else
                        t.matmul(out=pj[:, (t_ % 4) * 256:(t_ % 4) * 256 + 256],
                                 lhsT=ktb[h % 2][:, t_ * 128:(t_ + 1) * 128],
                                 rhs=qtbu[h % 2][:, qc * NQ:(qc + 1) * NQ],
                                 start=True, stop=True)))
                P.wait("vector", eST, eST.n)
                g = i * 2 + tf // 8
                if g >= 4 and A3CUT >= 2:
                    gp = g - 4
                    P.wait("vector", eXP,
                           A["eXP"] + (gp // 2) * 32 + 31 + (gp % 2))
                eEV.n += 1
                sg = g % 4
                P.on("vector", lambda v, pj=pj, tf=tf, sg=sg: v.tensor_copy(
                    out=spr[:, sg * 8192 + (tf % 8) * 1024: sg * 8192 + (tf % 8) * 1024 + 1024],
                    in_=pj[:]).then_inc(eEV.h, 1))

        def emit_vbuild(h, urange=None):
            b_ = h % 2
            P.wait("tensor", DK[h % 2], dS_kv[h])
            if h >= 1:
                if A3CUT >= 4:
                    P.wait("tensor", eAQ, A["eAQ"] + h * 256)
                if A3CUT >= 5:
                    P.wait("tensor", rTM, A["rTM"] + (h - 1) * 8 * 16)
            for u in (urange if urange is not None else range(NT // 2)):
                vv = vtpp[u % 2][:, 0:128]
                P.wait("tensor", vSB, vSB.n)
                pVT.n += 1
                P.on("tensor", lambda t, b_=b_, u=u, vv=vv: t.transpose(
                    out=vv, in_=vtb[b_][:, u * 128:(u + 1) * 128],
                    identity=idb_sb[:]).then_inc(pVT.h, 1))
                P.wait("vector", pVT, pVT.n)
                for par in range(2):
                    t_ = 2 * u + par
                    vSB.n += 1
                    P.on("vector", lambda v, b_=b_, t_=t_, vv=vv, par=par: v.tensor_copy(
                        out=vsb[b_][:, t_ * 65: t_ * 65 + 64],
                        in_=vv[:, par * 64:par * 64 + 64]).then_inc(vSB.h, 1))
            if urange is None or urange[-1] == NT // 2 - 1:
                vsb_done[h] = vSB.n

        EFN_REQ = {ch: 2 * (ch // 2) - 2 for ch in range(16)}

        def emit_av(i, ch):
            h, qc = divmod(i, QCN)
            if A3CUT >= 5:
                P.wait("tensor", eFN, A["eFN"] + i * 16 + EFN_REQ[ch])
            if A3CUT >= 3:
                P.wait("tensor", vSB, vsb_done[h])
            for hh in range(2):
                idx = i * 32 + ch * 2 + hh
                slot = idx % 2
                P.wait("tensor", eXP, A["eXP"] + idx + 1)
                for tl in range(32):
                    t_ = hh * 32 + tl
                    if tl == 31:
                        eAQ.n += 1
                    P.on("tensor", lambda t, h=h, ch=ch, t_=t_, tl=tl, slot=slot: (
                        t.matmul(out=avp[(ch // 2) % 2][0:65, (ch % 2) * 256:(ch % 2) * 256 + 256],
                                 lhsT=vsb[h % 2][:, t_ * 65:(t_ + 1) * 65],
                                 rhs=pbuf[:, slot * 8192 + tl * 256: slot * 8192 + tl * 256 + 256],
                                 start=(t_ == 0), stop=(t_ == 63)).then_inc(eAQ.h, 1)
                        if tl == 31 else
                        t.matmul(out=avp[(ch // 2) % 2][0:65, (ch % 2) * 256:(ch % 2) * 256 + 256],
                                 lhsT=vsb[h % 2][:, t_ * 65:(t_ + 1) * 65],
                                 rhs=pbuf[:, slot * 8192 + tl * 256: slot * 8192 + tl * 256 + 256],
                                 start=(t_ == 0), stop=(t_ == 63))))


        def emit_rbc(i, p):
            g = i * 8 + p
            P.wait("tensor", rRQ, A["rRQ"] + g + 1)
            P.wait("tensor", rTM, A["rTM"] + 2 * g)
            rBC.n += 1
            P.on("tensor", lambda t: t.matmul(
                out=rbcp[0:64, 0:512], lhsT=ones_f[0:1, 0:64], rhs=rpair[:],
                start=True, stop=True).then_inc(rBC.h, 1))

        def emit_finish(i, p):
            che, cho = 2 * p, 2 * p + 1
            pt = avp[p % 2]
            g = i * 8 + p
            if p == 0 and i > 1:
                P.wait("vector", DA[i % 2], acc_dma[i - 2])
            P.wait("vector", eAQ, A["eAQ"] + i * 32 + cho * 2 + 2)
            P.wait("vector", rBC, A["rBC"] + g)
            for chl, col in ((che, 0), (cho, 256)):
                c2.n += 1
                P.on("vector", lambda v, pt=pt, col=col: v.reciprocal(
                    out=rpair[0:1, col:col + 256],
                    in_=pt[64:65, col:col + 256]).then_inc(c2.h, 1))
                P.wait("vector", c2, c2.n)
                if chl == cho:
                    rRQ.n += 1
                P.on("vector", lambda v, chl=chl, col=col, last=(chl == cho): (
                    v.tensor_scalar(out=rpair[0:1, col:col + 256],
                                    in0=rpair[0:1, col:col + 256],
                                    scalar1=iwd[0:1, chl:chl + 1], scalar2=None,
                                    op0=OP.mult).then_inc(rRQ.h, 1)
                    if last else
                    v.tensor_scalar(out=rpair[0:1, col:col + 256],
                                    in0=rpair[0:1, col:col + 256],
                                    scalar1=iwd[0:1, chl:chl + 1], scalar2=None,
                                    op0=OP.mult)))
            P.wait("vector", rBC, A["rBC"] + g + 1)
            for chl, col in ((che, 0), (cho, 256)):
                c2.n += 1
                P.on("vector", lambda v, pt=pt, col=col: v.tensor_copy(
                    out=osb[:], in_=pt[0:64, col:col + 256]).then_inc(c2.h, 1))
                P.wait("vector", c2, c2.n)
                rTM.n += 1
                P.on("vector", lambda v, col=col: v.tensor_tensor(
                    out=tmpf[:], in0=osb[:], in1=rbcp[0:64, col:col + 256],
                    op=OP.mult).then_inc(rTM.h, 1))
                P.wait("vector", rTM, rTM.n)
                eFN.n += 1
                if chl == 0:
                    P.on("vector", lambda v, i=i: v.tensor_copy(
                        out=accf[i % 2][:], in_=tmpf[:]).then_inc(eFN.h, 1))
                else:
                    P.on("vector", lambda v, i=i: v.tensor_tensor(
                        out=accf[i % 2][:], in0=accf[i % 2][:], in1=tmpf[:],
                        op=OP.add).then_inc(eFN.h, 1))

        NITEM = NH_RUN * QCN
        if NITEM:
            if A3CUT >= 3:
                emit_vbuild(0)
            emit_fills(0)
        for i in range(NITEM):
            h, qc = divmod(i, QCN)
            for ch in range(16) if A3CUT >= 2 else []:
                for hh in range(2):
                    idx = i * 32 + ch * 2 + hh
                    slot = idx % 2
                    if ch == 0:
                        P.wait("scalar", eEV, A["eEV"] + i * 16 + (hh + 1) * 8)
                    if idx >= 2 and A3CUT >= 4:
                        P.wait("scalar", eAQ, A["eAQ"] + idx - 1)
                    eXP.n += 1
                    hc = h * 16 + ch
                    sg = (i * 2 + hh) % 4
                    P.on("scalar", lambda s, sg=sg, slot=slot, hc=hc: s.activation(
                        out=pbuf[:, slot * 8192:(slot + 1) * 8192],
                        in_=spr[:, sg * 8192:(sg + 1) * 8192], func=AF.Exp,
                        scale=sccol[:, hc:hc + 1],
                        bias=sccol[:, 256 + hc:256 + hc + 1]).then_inc(eXP.h, 1))
            if A3CUT >= 4:
                for ch in range(16):
                    emit_av(i, ch)
                    if i + 1 < NITEM:
                        emit_fills(i + 1, [ch])
                    if ch % 2 == 1 and A3CUT >= 5:
                        emit_rbc(i, ch // 2)
                        emit_finish(i, ch // 2)
                if i + 1 < NITEM and qc == QCN - 1 and A3CUT >= 3:
                    emit_vbuild(h + 1)
            else:
                if i + 1 < NITEM:
                    emit_fills(i + 1)
                    if qc == QCN - 1 and A3CUT >= 3:
                        emit_vbuild(h + 1)
            if A3CUT < 5:
                continue
            P.wait("gpsimd", eFN, eFN.n)
            acc_dma[i] = _dma(P, "gpsimd",
                              accs[h * 64:(h + 1) * 64, qc * NQ:(qc + 1) * NQ],
                              accf[i % 2][:], q=DA[i % 2])
        barrier(P)
        P.emit()

    stackA.close()
    xres = outer.enter_context(nc.sbuf_tensor("xres", [128, NRT * E], F32))

    a4_on = nphase >= 3 and NH_RUN == H
    if a4_on:
     with ExitStack() as st:
        en = st.enter_context
        acct = en(nc.sbuf_tensor("acct", [128, 8 * ROWS], BF16))
        wos = en(nc.sbuf_tensor("wosA", [128, 8 * E], BF16))
        otT = en(nc.sbuf_tensor("otT", [128, 8 * ROWS], F32))
        xst = [en(nc.sbuf_tensor(f"xstA{i}", [128, E], F32)) for i in range(2)]
        ppj = [en(nc.psum_tensor(f"ppjA{i}", [128, 512], F32)) for i in range(2)]

        P = Prog(nc)
        c0, c2, c3, c4 = (sm[x] for x in ("c0", "c2", "c3", "c4"))
        eST, eEV, pTR, c5 = sm["eST"], sm["eEV"], sm["pTR"], sm["c5"]
        A4 = {x: sm[x].n for x in ("eST", "eEV", "pTR", "c5", "c0")}

        _dma(P, "gpsimd", wos[:].rearrange("p (c n) -> p c n", n=E), wo_a.rearrange("(c p) n -> p c n", p=128))
        _dma(P, "sync", acct[:].rearrange("p (c n) -> p c n", n=ROWS), accs.rearrange("(c p) n -> p c n", p=128))
        ldT = dS.n
        P.wait("tensor", dS, ldT)
        P.wait("tensor", dG, dG.n)
        nj = 0
        for dp in range(8):
            for rc in range(4):
                pj = ppj[nj % 2]
                if nj >= 2:
                    P.wait("tensor", eEV, A4["eEV"] + nj - 1)
                for fc in range(8):
                    if fc == 7:
                        eST.n += 1
                    P.on("tensor", lambda t, fc=fc, dp=dp, rc=rc, pj=pj: (
                        t.matmul(out=pj[:],
                                 lhsT=wos[:, fc * E + dp * 128: fc * E + dp * 128 + 128],
                                 rhs=acct[:, fc * ROWS + rc * 512: fc * ROWS + rc * 512 + 512],
                                 start=(fc == 0), stop=(fc == 7)).then_inc(eST.h, 1)
                        if fc == 7 else
                        t.matmul(out=pj[:],
                                 lhsT=wos[:, fc * E + dp * 128: fc * E + dp * 128 + 128],
                                 rhs=acct[:, fc * ROWS + rc * 512: fc * ROWS + rc * 512 + 512],
                                 start=(fc == 0), stop=(fc == 7))))
                P.wait("scalar", eST, eST.n)
                eEV.n += 1
                P.on("scalar", lambda s, pj=pj, dp=dp, rc=rc: s.activation(
                    out=otT[:, dp * ROWS + rc * 512: dp * ROWS + rc * 512 + 512],
                    in_=pj[:], func=AF.Identity,
                    bias=bcols[:, BC["bo_a"] + dp: BC["bo_a"] + dp + 1]).then_inc(eEV.h, 1))
                nj += 1
        for rt in range(NRT):
            if rt >= 2:
                P.wait("sync", c0, A4["c0"] + (rt - 1) * 8)
            dtot = _dma(P, "sync", xst[rt % 2][:], xb[rt * 128:(rt + 1) * 128, :],
                        q=DX[rt % 2])
            P.wait("tensor", eEV, eEV.n)
            P.wait("vector", DX[rt % 2], dtot)
            for fc in range(8):
                gt = rt * 8 + fc
                pj = ppj[gt % 2]
                if gt >= 2:
                    P.wait("tensor", c0, A4["c0"] + gt - 1)
                pTR.n += 1
                P.on("tensor", lambda t, fc=fc, rt=rt, pj=pj: t.transpose(
                    out=pj[:, 0:128], in_=otT[:, fc * ROWS + rt * 128: fc * ROWS + rt * 128 + 128],
                    identity=idf_sb[:]).then_inc(pTR.h, 1))
                P.wait("vector", pTR, pTR.n)
                c0.n += 1
                P.on("vector", lambda v, fc=fc, rt=rt, pj=pj: v.tensor_tensor(
                    out=xres[:, rt * E + fc * 128: rt * E + fc * 128 + 128],
                    in0=xst[rt % 2][:, fc * 128:(fc + 1) * 128], in1=pj[:, 0:128],
                    op=OP.add).then_inc(c0.h, 1))
        barrier(P)
        P.emit()

    def emit_dequant():
        with ExitStack() as st:
            en = st.enter_context
            qw = en(nc.sbuf_tensor("qw", [128, FF], I32))
            srow = en(nc.sbuf_tensor("srow", [1, FF], F32))
            scf = en(nc.sbuf_tensor("scf", [128, FF], F32))
            tqf = en(nc.sbuf_tensor("tqf", [128, FF], F32))
            wbb = en(nc.sbuf_tensor("wbb", [128, FF], BF16))
            pbq = [en(nc.psum_tensor(f"pbq{i}", [128, 512], F32)) for i in range(2)]

            P = Prog(nc)
            c2, eST = sm["c2"], sm["eST"]
            for w_src, s_src, wd, nkc, width, nw in [
                    (w1q, s1, w1ds, 8, FF, 32), (w2q, s2, w2ds, 32, E, 8)]:
                for kc in range(nkc):
                    P.wait("sync", dG, dG.n)
                    _dma(P, "sync", qw[:, 0:width], w_src[kc * 128:(kc + 1) * 128, :])
                    _dma(P, "sync", srow[0:1, 0:width], s_src[kc:kc + 1, :])
                    dtot = dS.n
                    P.wait("tensor", dS, dtot)
                    P.wait("tensor", c2, c2.n)
                    base_c2 = c2.n
                    for j in range(width // 512):
                        pj = pbq[j % 2]
                        if j >= 2:
                            P.wait("tensor", c2, base_c2 + j - 1)
                        eST.n += 1
                        P.on("tensor", lambda t, pj=pj, j=j: t.matmul(
                            out=pj[:], lhsT=ones_f[:], rhs=srow[0:1, j * 512:(j + 1) * 512],
                            start=True, stop=True).then_inc(eST.h, 1))
                        P.wait("vector", eST, eST.n)
                        c2.n += 1
                        P.on("vector", lambda v, pj=pj, j=j: v.tensor_copy(
                            out=scf[:, j * 512:(j + 1) * 512], in_=pj[:]).then_inc(c2.h, 1))
                    P.wait("vector", dS, dtot)
                    c2.n += 1
                    P.on("vector", lambda v, width=width: v.tensor_scalar(
                        out=tqf[:, 0:width], in0=qw[:, 0:width], scalar1=-128.0,
                        scalar2=None, op0=OP.add).then_inc(c2.h, 1))
                    P.wait("vector", c2, c2.n)
                    c2.n += 1
                    P.on("vector", lambda v, width=width: v.tensor_tensor(
                        out=wbb[:, 0:width], in0=tqf[:, 0:width], in1=scf[:, 0:width],
                        op=OP.mult).then_inc(c2.h, 1))
                    P.wait("gpsimd", c2, c2.n)
                    for oc in range(nw):
                        _dma(P, "gpsimd", wd[oc, :, kc, :], wbb[:, oc * 128:(oc + 1) * 128])
            barrier(P)
            P.emit()

    def emit_ffn(tag):
        with ExitStack() as st:
            en = st.enter_context
            ybuf = en(nc.sbuf_tensor(f"ybuf{tag}", [128, 4 * E], F32))
            cb = en(nc.sbuf_tensor(f"cb{tag}", [128, E], F32))
            sqb2 = en(nc.sbuf_tensor(f"sqb2{tag}", [128, E], F32))
            hrow = en(nc.sbuf_tensor(f"hrow{tag}", [128, E], BF16))
            hT = en(nc.sbuf_tensor(f"hT{tag}", [128, 8 * 512], BF16))
            y1T = en(nc.sbuf_tensor(f"y1T{tag}", [128, 32 * 512], BF16))
            zt = en(nc.sbuf_tensor(f"zt{tag}", [128, 8 * 512], F32))
            w1sb = [en(nc.sbuf_tensor(f"w1sb{tag}{i}", [128, 8 * 128], BF16)) for i in range(2)]
            w2sb = [en(nc.sbuf_tensor(f"w2sb{tag}{i}", [128, 32 * 128], BF16)) for i in range(2)]
            mcol = en(nc.sbuf_tensor(f"mcolF{tag}", [128, 4], F32))
            ppf = [en(nc.psum_tensor(f"ppf{tag}{i}", [128, 512], F32)) for i in range(2)]

            P = Prog(nc)
            c0, c2, c3, c4, c5 = (sm[x] for x in ("c0", "c2", "c3", "c4", "c5"))
            eST, eEV, pTR, pPJ = sm["eST"], sm["eEV"], sm["pTR"], sm["pPJ"]
            FB = {x: sm[x].n for x in ("eST", "eEV", "pTR", "c5", "c0", "pPJ")}
            w1dma = {}
            w2dma = {}
            hT_ptr = {}

            for rc in range(4):
                for r in range(4):
                    rt = rc * 4 + r
                    xsl = xres[:, rt * E:(rt + 1) * E]
                    ysl = ybuf[:, r * E:(r + 1) * E]
                    emit_ln(P, mcol, xsl, ysl, sqb2[:], c2, c3, c4)
                    c2.n += 1
                    P.on("vector", lambda v, ysl=ysl: v.tensor_scalar(
                        out=ysl, in0=ysl, scalar1=mcol[:, 3:4], scalar2=None,
                        op0=OP.mult).then_inc(c2.h, 1))
                    P.wait("vector", c2, c2.n)
                    c2.n += 1
                    P.on("vector", lambda v, xsl=xsl, ysl=ysl: v.tensor_tensor(
                        out=xsl, in0=xsl, in1=ysl, op=OP.add).then_inc(c2.h, 1))
                    P.wait("vector", c2, c2.n)
                    emit_ln(P, mcol, ysl, cb[:], sqb2[:], c2, c3, c4)
                    gr = rc * 4 + r
                    if gr >= 1:
                        P.wait("vector", pTR, hT_ptr[gr - 1])
                    c0.n += 1
                    P.on("vector", lambda v: v.tensor_scalar(
                        out=hrow[:], in0=cb[:], scalar1=mcol[:, 3:4], scalar2=None,
                        op0=OP.mult).then_inc(c0.h, 1))
                    P.wait("tensor", c0, c0.n)
                    if r == 0 and rc > 0:
                        P.wait("tensor", c0, FB["c0"] + rc * 36)
                    for fc in range(8):
                        gl = r * 8 + fc
                        pj = ppf[fc % 2]
                        if gl >= 2 or r > 0:
                            P.wait("tensor", c5, c5.n - 1)
                        pTR.n += 1
                        P.on("tensor", lambda t, fc=fc, pj=pj: t.transpose(
                            out=pj.bitcast(BF16)[:, 0:128], in_=hrow[:, fc * 128:(fc + 1) * 128],
                            identity=idb_sb[:]).then_inc(pTR.h, 1))
                        P.wait("scalar", pTR, pTR.n)
                        c5.n += 1
                        P.on("scalar", lambda s, pj=pj, fc=fc, r=r: s.activation(
                            out=hT[:, fc * 512 + r * 128: fc * 512 + r * 128 + 128],
                            in_=pj.bitcast(BF16)[:, 0:128], func=AF.Identity,
                            scale=bcols[:, BC["f_g"] + fc: BC["f_g"] + fc + 1],
                            bias=bcols[:, BC["f_b"] + fc: BC["f_b"] + fc + 1]).then_inc(c5.h, 1))
                        hT_ptr[gr] = pTR.n
                P.wait("tensor", c5, c5.n)
                for oc in range(32):
                    wb = w1sb[oc % 2]
                    g1 = rc * 32 + oc
                    if g1 >= 2:
                        rcp, ocp = divmod(g1 - 2, 32)
                        P.wait("sync", eEV, FB["eEV"] + rcp * 40 + ocp + 1)
                    w1dma[(rc, oc)] = _dma(P, "sync", wb[:],
                                           w1ds[oc].rearrange("p c n -> p (c n)"),
                                           q=DW[oc % 2])
                    pj = ppf[oc % 2]
                    P.wait("tensor", DW[oc % 2], w1dma[(rc, oc)])
                    if oc >= 2:
                        P.wait("tensor", eEV, FB["eEV"] + rc * 40 + oc - 1)
                    for kc in range(8):
                        if kc == 7:
                            eST.n += 1
                        P.on("tensor", lambda t, wb=wb, kc=kc, pj=pj: (
                            t.matmul(out=pj[:], lhsT=wb[:, kc * 128:(kc + 1) * 128],
                                     rhs=hT[:, kc * 512:(kc + 1) * 512],
                                     start=(kc == 0), stop=(kc == 7)).then_inc(eST.h, 1)
                            if kc == 7 else
                            t.matmul(out=pj[:], lhsT=wb[:, kc * 128:(kc + 1) * 128],
                                     rhs=hT[:, kc * 512:(kc + 1) * 512],
                                     start=(kc == 0), stop=(kc == 7))))
                    P.wait("scalar", eST, eST.n)
                    eEV.n += 1
                    P.on("scalar", lambda s, pj=pj, oc=oc: s.activation(
                        out=y1T[:, oc * 512:(oc + 1) * 512], in_=pj[:], func=AF.Silu,
                        bias=fb1c[:, oc:oc + 1]).then_inc(eEV.h, 1))
                for oc in range(8):
                    wb = w2sb[oc % 2]
                    g2 = rc * 8 + oc
                    if g2 >= 2:
                        rcp, ocp = divmod(g2 - 2, 8)
                        P.wait("sync", eEV, FB["eEV"] + rcp * 40 + 32 + ocp + 1)
                    w2dma[(rc, oc)] = _dma(P, "sync", wb[:],
                                           w2ds[oc].rearrange("p c n -> p (c n)"),
                                           q=DW[oc % 2])
                    pj = ppf[oc % 2]
                    P.wait("tensor", DW[oc % 2], w2dma[(rc, oc)])
                    if oc >= 2:
                        P.wait("tensor", eEV, FB["eEV"] + rc * 40 + 32 + oc - 1)
                    else:
                        P.wait("tensor", eEV, FB["eEV"] + rc * 40 + 32)
                    for kc in range(32):
                        if kc == 31:
                            eST.n += 1
                        P.on("tensor", lambda t, wb=wb, kc=kc, pj=pj: (
                            t.matmul(out=pj[:], lhsT=wb[:, kc * 128:(kc + 1) * 128],
                                     rhs=y1T[:, kc * 512:(kc + 1) * 512],
                                     start=(kc == 0), stop=(kc == 31)).then_inc(eST.h, 1)
                            if kc == 31 else
                            t.matmul(out=pj[:], lhsT=wb[:, kc * 128:(kc + 1) * 128],
                                     rhs=y1T[:, kc * 512:(kc + 1) * 512],
                                     start=(kc == 0), stop=(kc == 31))))
                    P.wait("scalar", eST, eST.n)
                    eEV.n += 1
                    P.on("scalar", lambda s, pj=pj, oc=oc: s.activation(
                        out=zt[:, oc * 512:(oc + 1) * 512], in_=pj[:], func=AF.Identity,
                        scale=0.5, bias=fb2c[:, oc:oc + 1]).then_inc(eEV.h, 1))
                P.wait("tensor", eEV, eEV.n)
                for r in range(4):
                    rt = rc * 4 + r
                    for fc in range(8):
                        pj = ppf[fc % 2]
                        P.wait("tensor", c0, c0.n - 1)
                        pTR.n += 1
                        P.on("tensor", lambda t, fc=fc, r=r, pj=pj: t.transpose(
                            out=pj[:, 0:128], in_=zt[:, fc * 512 + r * 128: fc * 512 + r * 128 + 128],
                            identity=idf_sb[:]).then_inc(pTR.h, 1))
                        P.wait("vector", pTR, pTR.n)
                        c0.n += 1
                        P.on("vector", lambda v, fc=fc, rt=rt, pj=pj: v.tensor_tensor(
                            out=xres[:, rt * E + fc * 128: rt * E + fc * 128 + 128],
                            in0=xres[:, rt * E + fc * 128: rt * E + fc * 128 + 128],
                            in1=pj[:, 0:128], op=OP.add).then_inc(c0.h, 1))
            barrier(P)
            P.emit()

    def emit_attnb():
        c_outer = ExitStack()
        enO = c_outer.enter_context
        c12 = ExitStack()
        yT = c12.enter_context(nc.sbuf_tensor("yTC", [128, 8 * ROWS], BF16))

        with ExitStack() as st:
            en = st.enter_context
            ybf = en(nc.sbuf_tensor("ybfC", [128, E], F32))
            sqbC = en(nc.sbuf_tensor("sqbC", [128, E], F32))
            ylb = [en(nc.sbuf_tensor(f"ylbC{i}", [128, E], BF16)) for i in range(2)]
            mcol = en(nc.sbuf_tensor("mcolC", [128, 4], F32))
            ptc = [en(nc.psum_tensor(f"ptcC{i}", [128, 512], F32)) for i in range(2)]

            P = Prog(nc)
            c0, c2, c3, c4, c5 = (sm[x] for x in ("c0", "c2", "c3", "c4", "c5"))
            pTR = sm["pTR"]
            C1 = {x: sm[x].n for x in ("pTR", "c5")}
            for rt in range(NRT):
                xsl = xres[:, rt * E:(rt + 1) * E]
                yl = ylb[rt % 2]
                if rt >= 2:
                    P.wait("vector", pTR, C1["pTR"] + (rt - 1) * 8)
                emit_ln(P, mcol, xsl, ybf[:], sqbC[:], c2, c3, c4)
                c0.n += 1
                P.on("vector", lambda v, yl=yl: v.tensor_scalar(
                    out=yl[:], in0=ybf[:], scalar1=mcol[:, 3:4], scalar2=None,
                    op0=OP.mult).then_inc(c0.h, 1))
                P.wait("tensor", c0, c0.n)
                for fc in range(8):
                    gt = rt * 8 + fc
                    pj = ptc[gt % 2]
                    if gt >= 2:
                        P.wait("tensor", c5, C1["c5"] + gt - 1)
                    pTR.n += 1
                    P.on("tensor", lambda t, yl=yl, fc=fc, pj=pj: t.transpose(
                        out=pj.bitcast(BF16)[:, 0:128], in_=yl[:, fc * 128:(fc + 1) * 128],
                        identity=idb_sb[:]).then_inc(pTR.h, 1))
                    P.wait("vector", pTR, pTR.n)
                    c5.n += 1
                    P.on("vector", lambda v, pj=pj, fc=fc, rt=rt: v.tensor_copy(
                        out=yT[:, fc * ROWS + rt * 128: fc * ROWS + rt * 128 + 128],
                        in_=pj.bitcast(BF16)[:, 0:128]).then_inc(c5.h, 1))
            barrier(P)
            P.emit()

        with ExitStack() as st:
            en = st.enter_context
            rowb = en(nc.sbuf_tensor("rowbC", [128, NRT * E], BF16))
            wcb = en(nc.sbuf_tensor("wcbC", [128, 8 * E], BF16))
            stgc = [en(nc.sbuf_tensor(f"stgC{i}", [128, 512], BF16)) for i in range(2)]
            res2 = [en(nc.sbuf_tensor(f"resC{i}", [128, 512], BF16)) for i in range(2)]
            ppc = [en(nc.psum_tensor(f"ppcC{i}", [128, 512], F32)) for i in range(2)]
            ptc2 = [en(nc.psum_tensor(f"ptc2C{i}", [128, 512], F32)) for i in range(2)]

            P = Prog(nc)
            c0, c2, c5 = sm["c0"], sm["c2"], sm["c5"]
            eST, eEV, pTR, pPJ = sm["eST"], sm["eEV"], sm["pTR"], sm["pPJ"]
            C2 = {x: sm[x].n for x in ("eST", "eEV", "pTR", "c0", "c2", "c5", "pPJ")}
            WLIST = [("q", wq_b, 0, None), ("k0", wk_b0, 1, ksr), ("k1", wk_b1, 2, ksr),
                     ("v0", wv_b0, 3, vsr), ("v1", wv_b1, 4, vsr)]
            nev = 0
            ntr = 0
            res2_cons = {}
            for wi, (wn, wsrc, bidx, dstt) in enumerate(WLIST):
                if wi > 0:
                    P.wait("gpsimd", eST, eST.n)
                _dma(P, "gpsimd", wcb[:], wsrc.rearrange("(c p) n -> p c n", p=128))
                P.wait("tensor", dG, dG.n)
                for dp in range(8):
                    for rc in range(4):
                        jj = nev
                        pj = ppc[jj % 2]
                        sgt = stgc[jj % 2]
                        rs2 = res2[jj % 2]
                        if jj >= 2:
                            P.wait("tensor", eEV, C2["eEV"] + jj - 1)
                        for fc in range(8):
                            if fc == 7:
                                eST.n += 1
                            P.on("tensor", lambda t, fc=fc, dp=dp, rc=rc, pj=pj: (
                                t.matmul(out=pj[:],
                                         lhsT=wcb[:, fc * E + dp * 128: fc * E + dp * 128 + 128],
                                         rhs=yT[:, fc * ROWS + rc * 512: fc * ROWS + rc * 512 + 512],
                                         start=(fc == 0), stop=(fc == 7)).then_inc(eST.h, 1)
                                if fc == 7 else
                                t.matmul(out=pj[:],
                                         lhsT=wcb[:, fc * E + dp * 128: fc * E + dp * 128 + 128],
                                         rhs=yT[:, fc * ROWS + rc * 512: fc * ROWS + rc * 512 + 512],
                                         start=(fc == 0), stop=(fc == 7))))
                        P.wait("scalar", eST, eST.n)
                        if jj >= 2:
                            P.wait("scalar", c2, C2["c2"] + jj - 1)
                        eEV.n += 1
                        P.on("scalar", lambda s, pj=pj, sgt=sgt, bidx=bidx, dp=dp: s.activation(
                            out=sgt[:], in_=pj[:], func=AF.Identity,
                            bias=bkvc[:, bidx * 8 + dp: bidx * 8 + dp + 1]).then_inc(eEV.h, 1))
                        P.wait("vector", eEV, eEV.n)
                        if jj - 2 in res2_cons:
                            kind, cnt = res2_cons[jj - 2]
                            if kind == "ptr":
                                P.wait("vector", pTR, cnt)
                            else:
                                P.wait("vector", DS4[0], cnt)
                        c2.n += 1
                        P.on("vector", lambda v, sgt=sgt, rs2=rs2, dp=dp, rc=rc: v.tensor_tensor(
                            out=rs2[:], in0=sgt[:],
                            in1=yT[:, dp * ROWS + rc * 512: dp * ROWS + rc * 512 + 512],
                            op=OP.add).then_inc(c2.h, 1))
                        if wn == "q":
                            P.wait("sync", c2, c2.n)
                            _dma(P, "sync", qtb[2 * dp, :, rc * 512:(rc + 1) * 512], rs2[0:64, :],
                                 q=DS4[0])
                            res2_cons[jj] = ("dma", _dma(
                                P, "sync", qtb[2 * dp + 1, :, rc * 512:(rc + 1) * 512],
                                rs2[64:128, :], q=DS4[0]))
                        else:
                            P.wait("tensor", c2, c2.n)
                            for tt in range(4):
                                gt = ntr
                                pj2 = ptc2[gt % 2]
                                if gt >= 2:
                                    P.wait("tensor", c5, C2["c5"] + gt - 1)
                                pTR.n += 1
                                P.on("tensor", lambda t, rs2=rs2, tt=tt, pj2=pj2: t.transpose(
                                    out=pj2.bitcast(BF16)[:, 0:128],
                                    in_=rs2[:, tt * 128:(tt + 1) * 128],
                                    identity=idb_sb[:]).then_inc(pTR.h, 1))
                                P.wait("vector", pTR, pTR.n)
                                c5.n += 1
                                P.on("vector", lambda v, pj2=pj2, rc=rc, tt=tt, dp=dp: v.tensor_copy(
                                    out=rowb[:, (rc * 4 + tt) * E + dp * 128: (rc * 4 + tt) * E + dp * 128 + 128],
                                    in_=pj2.bitcast(BF16)[:, 0:128]).then_inc(c5.h, 1))
                                ntr += 1
                            res2_cons[jj] = ("ptr", pTR.n)
                        nev += 1
                if wn != "q":
                    k = int(wn[1])
                    P.wait("sync", c5, c5.n)
                    nrb = _dma(P, "sync", dstt[k].rearrange("(t p) d -> p t d", p=128),
                               rowb[:].rearrange("p (t d) -> p t d", d=E), q=DRB)
                    P.wait("vector", DRB, nrb)
            barrier(P)
            P.emit()

        c12.close()

        outT = enO(nc.sbuf_tensor("outTC", [128, 8 * ROWS], F32))
        with ExitStack() as st:
            en = st.enter_context
            krb1 = en(nc.sbuf_tensor("krbC", [128, 16 * E], BF16))
            vrb1 = en(nc.sbuf_tensor("vrbC", [128, 16 * E], BF16))
            krb = [krb1, krb1]
            vrb = [vrb1, vrb1]
            s01 = en(nc.sbuf_tensor("s01C", [128, E], F32))
            s1tmp = en(nc.sbuf_tensor("s1tmpC", [64, E], F32))
            stb = s1tmp[:].bitcast(BF16)[:, 0:E]
            qth1 = en(nc.sbuf_tensor("qthC0", [64, ROWS], BF16))
            qth = [qth1, qth1]
            stk = [en(nc.psum_tensor(f"stkC{i}", [64, 1024], F32)) for i in range(2)]
            pqc = [en(nc.psum_tensor(f"pqcC{i}", [64, 512], F32)) for i in range(2)]

            P = Prog(nc)
            c0, c2, c5 = sm["c0"], sm["c2"], sm["c5"]
            eST, eEV = sm["eST"], sm["eEV"]
            cc = sm["cAC"]
            C3 = {x: sm[x].n for x in ("eST", "c2", "c0")}
            for k in range(2):
                if k == 1:
                    P.wait("sync", eST, eST.n)
                _dma(P, "sync", krb[k][:].rearrange("p (t d) -> p t d", d=E),
                     ksr[k].rearrange("(t p) d -> p t d", p=128), q=DN[k])
                dtot = _dma(P, "sync", vrb[k][:].rearrange("p (t d) -> p t d", d=E),
                            vsr[k].rearrange("(t p) d -> p t d", p=128), q=DN[k])
                P.wait("tensor", DN[k], dtot)
                for h in range(H):
                    for ns in range(16):
                        if ns == 15:
                            eST.n += 1
                        P.on("tensor", lambda t, k=k, h=h, ns=ns: (
                            t.matmul(out=stk[k][:, h * 64:(h + 1) * 64],
                                     lhsT=krb[k][:, ns * E + h * 64: ns * E + h * 64 + 64],
                                     rhs=vrb[k][:, ns * E + h * 64: ns * E + h * 64 + 64],
                                     start=(ns == 0), stop=(ns == 15)).then_inc(eST.h, 1)
                            if ns == 15 else
                            t.matmul(out=stk[k][:, h * 64:(h + 1) * 64],
                                     lhsT=krb[k][:, ns * E + h * 64: ns * E + h * 64 + 64],
                                     rhs=vrb[k][:, ns * E + h * 64: ns * E + h * 64 + 64],
                                     start=(ns == 0), stop=(ns == 15))))
            P.wait("vector", eST, eST.n)
            c2.n += 2
            P.on("vector", lambda v: v.tensor_copy(out=s01[0:64, :], in_=stk[0][:]).then_inc(sm["c2"].h, 1))
            P.on("vector", lambda v: v.tensor_copy(out=s01[64:128, :], in_=stk[1][:]).then_inc(sm["c2"].h, 1))
            P.wait("gpsimd", c2, c2.n)
            _dma(P, "gpsimd", cin_t[:, :], s01[:])
            P.wait("gpsimd", dG, dG.n)
            cc.n += 1
            P.on("gpsimd", lambda g: g.collective_compute(
                "AllReduce", OP.add, replica_groups=[[0, 1, 2, 3], [4, 5, 6, 7]],
                ins=[cin_t.ap().opt()], outs=[cout_t.ap().opt()]).then_inc(cc.h, 1))
            P.wait("gpsimd", cc, cc.n)
            _dma(P, "gpsimd", s01[:], cout_t[:, :])
            P.wait("vector", dG, dG.n)
            c2.n += 1
            P.on("vector", lambda v: v.tensor_copy(out=s1tmp[:], in_=s01[64:128, :]
                                                   ).then_inc(c2.h, 1))
            P.wait("vector", c2, c2.n)
            c2.n += 1
            P.on("vector", lambda v: v.tensor_tensor(out=s01[0:64, :], in0=s01[0:64, :],
                                                     in1=s1tmp[:], op=OP.mult
                                                     ).then_inc(c2.h, 1))
            P.wait("vector", c2, c2.n)
            c2.n += 1
            P.on("vector", lambda v: v.tensor_scalar(out=stb, in0=s01[0:64, :],
                                                     scalar1=0.5 ** 24, scalar2=None,
                                                     op0=OP.mult).then_inc(c2.h, 1))
            stb_done = c2.n
            for h in range(H):
                if h >= 1:
                    P.wait("sync", c0, c0.n)
                dtot = _dma(P, "sync", qth[h % 2][:], qtb[h], q=DQ[h % 2])
                P.wait("tensor", DQ[h % 2], dtot)
                P.wait("tensor", c2, stb_done)
                for rc in range(4):
                    pj = pqc[rc % 2]
                    if h * 4 + rc >= 2:
                        P.wait("tensor", c0, C3["c0"] + h * 4 + rc - 1)
                    eST.n += 1
                    P.on("tensor", lambda t, h=h, rc=rc, pj=pj: t.matmul(
                        out=pj[:], lhsT=stb[:, h * 64:(h + 1) * 64],
                        rhs=qth[h % 2][:, rc * 512:(rc + 1) * 512],
                        start=True, stop=True).then_inc(eST.h, 1))
                    P.wait("vector", eST, eST.n)
                    c0.n += 1
                    P.on("vector", lambda v, h=h, rc=rc, pj=pj: v.tensor_copy(
                        out=outT[(h % 2) * 64:(h % 2) * 64 + 64,
                                 (h // 2) * ROWS + rc * 512:(h // 2) * ROWS + rc * 512 + 512],
                        in_=pj[:]).then_inc(c0.h, 1))
            barrier(P)
            P.emit()

        with ExitStack() as st:
            en = st.enter_context
            orow = en(nc.sbuf_tensor("orowC", [128, E], F32))
            sqbC2 = en(nc.sbuf_tensor("sqbC2", [128, E], F32))
            xh = [en(nc.sbuf_tensor(f"xhC{i}", [128, E], BF16)) for i in range(2)]
            wob = en(nc.sbuf_tensor("wobC", [128, 8 * E], BF16))
            ostg = [en(nc.sbuf_tensor(f"ostgC{i}", [128, 512], F32)) for i in range(2)]
            mcol = en(nc.sbuf_tensor("mcolC7", [128, 4], F32))
            xhT = en(nc.sbuf_tensor("xhTC", [128, 8 * ROWS], BF16))
            pt7 = [en(nc.psum_tensor(f"pt7C{i}", [128, 512], F32)) for i in range(2)]
            px7 = [en(nc.psum_tensor(f"px7C{i}", [128, 512], F32)) for i in range(2)]
            pp7 = [en(nc.psum_tensor(f"pp7C{i}", [128, 512], F32)) for i in range(2)]

            P = Prog(nc)
            c0, c2, c3, c4, c5 = (sm[x] for x in ("c0", "c2", "c3", "c4", "c5"))
            eST, eEV, pTR = sm["eST"], sm["eEV"], sm["pTR"]
            C7 = {x: sm[x].n for x in ("eST", "eEV", "pTR", "c0", "c2", "c5")}
            _dma(P, "gpsimd", wob[:].rearrange("p (c n) -> p c n", n=E), wo_b.rearrange("(c p) n -> p c n", p=128))
            ntr = 0
            evac_c2 = []
            nor = 0
            for rt in range(NRT):
                for fc in range(8):
                    pj = pt7[nor % 2]
                    if nor >= 2:
                        P.wait("tensor", c2, evac_c2[nor - 2])
                    pTR.n += 1
                    P.on("tensor", lambda t, fc=fc, rt=rt, pj=pj: t.transpose(
                        out=pj[:, 0:128],
                        in_=outT[:, fc * ROWS + rt * 128: fc * ROWS + rt * 128 + 128],
                        identity=idf_sb[:]).then_inc(pTR.h, 1))
                    P.wait("vector", pTR, pTR.n)
                    c2.n += 1
                    evac_c2.append(c2.n)
                    P.on("vector", lambda v, fc=fc, pj=pj: v.tensor_copy(
                        out=orow[:, fc * 128:(fc + 1) * 128], in_=pj[:, 0:128]).then_inc(c2.h, 1))
                    nor += 1
                xhl = xh[rt % 2]
                if rt >= 2:
                    P.wait("vector", pTR, C7["pTR"] + (rt - 1) * 16)
                emit_ln(P, mcol, orow[:], orow[:], sqbC2[:], c2, c3, c4)
                c0.n += 1
                P.on("vector", lambda v, xhl=xhl: v.tensor_scalar(
                    out=xhl[:], in0=orow[:], scalar1=mcol[:, 3:4], scalar2=None,
                    op0=OP.mult).then_inc(c0.h, 1))
                P.wait("tensor", c0, c0.n)
                for fc in range(8):
                    gx = rt * 8 + fc
                    pj = px7[gx % 2]
                    if gx >= 2:
                        P.wait("tensor", c5, C7["c5"] + gx - 1)
                    pTR.n += 1
                    P.on("tensor", lambda t, xhl=xhl, fc=fc, pj=pj: t.transpose(
                        out=pj.bitcast(BF16)[:, 0:128], in_=xhl[:, fc * 128:(fc + 1) * 128],
                        identity=idb_sb[:]).then_inc(pTR.h, 1))
                    P.wait("scalar", pTR, pTR.n)
                    c5.n += 1
                    P.on("scalar", lambda s, pj=pj, fc=fc, rt=rt: s.activation(
                        out=xhT[:, fc * ROWS + rt * 128: fc * ROWS + rt * 128 + 128],
                        in_=pj.bitcast(BF16)[:, 0:128], func=AF.Identity,
                        scale=bcols[:, BC["lng"] + fc: BC["lng"] + fc + 1],
                        bias=bcols[:, BC["lnb"] + fc: BC["lnb"] + fc + 1]).then_inc(c5.h, 1))
                    ntr += 1
            P.wait("tensor", c5, c5.n)
            P.wait("tensor", c2, c2.n)
            P.wait("tensor", dG, dG.n)
            nj = 0
            for dp in range(8):
                for rc in range(4):
                    pj = pp7[nj % 2]
                    if nj >= 2:
                        P.wait("tensor", eEV, C7["eEV"] + nj - 1)
                    for fc in range(8):
                        if fc == 7:
                            eST.n += 1
                        P.on("tensor", lambda t, fc=fc, dp=dp, rc=rc, pj=pj: (
                            t.matmul(out=pj[:],
                                     lhsT=wob[:, fc * E + dp * 128: fc * E + dp * 128 + 128],
                                     rhs=xhT[:, fc * ROWS + rc * 512: fc * ROWS + rc * 512 + 512],
                                     start=(fc == 0), stop=(fc == 7)).then_inc(eST.h, 1)
                            if fc == 7 else
                            t.matmul(out=pj[:],
                                     lhsT=wob[:, fc * E + dp * 128: fc * E + dp * 128 + 128],
                                     rhs=xhT[:, fc * ROWS + rc * 512: fc * ROWS + rc * 512 + 512],
                                     start=(fc == 0), stop=(fc == 7))))
                    P.wait("scalar", eST, eST.n)
                    if nj >= 2:
                        P.wait("scalar", c0, C7["c0"] + 16 + (nj - 1) * 4)
                    eEV.n += 1
                    osg = ostg[nj % 2]
                    P.on("scalar", lambda s, pj=pj, dp=dp, osg=osg: s.activation(
                        out=osg[:], in_=pj[:], func=AF.Identity,
                        bias=bcols[:, BC["bo_b"] + dp: BC["bo_b"] + dp + 1]).then_inc(eEV.h, 1))
                    P.wait("tensor", eEV, eEV.n)
                    for tt in range(4):
                        pj2 = pp7[(nj + 1) % 2] if False else pt7[tt % 2]
                        P.wait("tensor", c0, c0.n - 1)
                        pTR.n += 1
                        P.on("tensor", lambda t, osg=osg, tt=tt, pj2=pj2: t.transpose(
                            out=pj2[:, 0:128], in_=osg[:, tt * 128:(tt + 1) * 128],
                            identity=idf_sb[:]).then_inc(pTR.h, 1))
                        P.wait("vector", pTR, pTR.n)
                        rt = rc * 4 + tt
                        c0.n += 1
                        P.on("vector", lambda v, rt=rt, dp=dp, pj2=pj2: v.tensor_tensor(
                            out=xres[:, rt * E + dp * 128: rt * E + dp * 128 + 128],
                            in0=xres[:, rt * E + dp * 128: rt * E + dp * 128 + 128],
                            in1=pj2[:, 0:128], op=OP.add).then_inc(c0.h, 1))
                    nj += 1
            barrier(P)
            P.emit()
        c_outer.close()

    if nphase >= 4:
        emit_dequant()
        emit_ffn("F1")
    if nphase >= 5:
        emit_attnb()
    if nphase >= 6:
        emit_ffn("F2")

    P = Prog(nc)
    if not a4_on:
        c0 = sm["c0"]
        c0.n += 1
        P.on("vector", lambda v: v.memset(xres[:], 0.0).then_inc(c0.h, 1))
        P.wait("sync", c0, c0.n)
    ov = out.rearrange("(t p) d -> p t d", p=128)
    xv = xres[:].rearrange("p (t d) -> p t d", d=E)
    for j in range(4):
        _dma(P, "sync", ov[:, j * 4:(j + 1) * 4, :], xv[:, j * 4:(j + 1) * 4, :])
    dbgsel = os.environ.get("MK_DBG", "")
    if dbgsel:
        dsrc = {
            "ksr0": ksr[0], "ksr1": ksr[1], "vsr0": vsr[0], "vsr1": vsr[1],
            "qtb": qtb.rearrange("a d (m n) -> (a d m) n", n=E),
            "cout": cout_t.ap(),
            "accs": accs,
        }[dbgsel]
        shp = dsrc.shape
        P.wait("gpsimd", dS, dS.n)
        dn = _dma(P, "gpsimd", out[0:shp[0], 0:shp[1]], dsrc)
        P.wait("sync", dG, dn)
    P.wait("sync", dS, dS.n)
    P.emit()

    outer.close()
    return nc


SCALE = 1.0 / math.sqrt(D)


_NC_CACHE = {}
LAST_RESULTS = None


def kernel(L0_attn_a_x, L0_attn_a_Wq, L0_attn_a_bq, L0_attn_a_Wk, L0_attn_a_bk,
           L0_attn_a_Wv, L0_attn_a_bv, L0_attn_a_Wo, L0_attn_a_bo,
           L0_attn_a_phase_modulators, L0_attn_a_interference_weights,
           L0_attn_b_Wq, L0_attn_b_bq, L0_attn_b_Wk, L0_attn_b_bk,
           L0_attn_b_Wv, L0_attn_b_bv, L0_attn_b_Wo, L0_attn_b_bo,
           L0_attn_b_ln_g, L0_attn_b_ln_b,
           L0_ffn_ln_gamma, L0_ffn_ln_beta, L0_ffn_fc1_q, L0_ffn_fc1_scales,
           L0_ffn_fc1_bias, L0_ffn_fc2_q, L0_ffn_fc2_scales, L0_ffn_fc2_bias):
    nphase = int(os.environ.get("MK_NPHASE", "6"))
    x = np.ascontiguousarray(np.asarray(L0_attn_a_x, dtype=np.float32))
    f32 = lambda a: np.ascontiguousarray(np.asarray(a, dtype=np.float32))
    i32 = lambda a: np.ascontiguousarray(np.asarray(a, dtype=np.int32))

    common = dict(
        wq_a=f32(L0_attn_a_Wq), wk_a=f32(L0_attn_a_Wk), wv_a=f32(L0_attn_a_Wv),
        wo_a=f32(L0_attn_a_Wo), bq_a=f32(L0_attn_a_bq), bk_a=f32(L0_attn_a_bk),
        bv_a=f32(L0_attn_a_bv), bo_a=f32(L0_attn_a_bo),
        pmt=np.ascontiguousarray(f32(L0_attn_a_phase_modulators).T),
        iw=f32(L0_attn_a_interference_weights),
        f_g=f32(L0_ffn_ln_gamma), f_b=f32(L0_ffn_ln_beta),
        w1q=i32(L0_ffn_fc1_q), s1=f32(L0_ffn_fc1_scales), fb1=f32(L0_ffn_fc1_bias),
        w2q=i32(L0_ffn_fc2_q), s2=f32(L0_ffn_fc2_scales), fb2=f32(L0_ffn_fc2_bias),
        wq_b=f32(L0_attn_b_Wq), wk_b0=f32(L0_attn_b_Wk[0]), wk_b1=f32(L0_attn_b_Wk[1]),
        wv_b0=f32(L0_attn_b_Wv[0]), wv_b1=f32(L0_attn_b_Wv[1]), wo_b=f32(L0_attn_b_Wo),
        bq_b=f32(L0_attn_b_bq), bk_b0=f32(L0_attn_b_bk[0]), bk_b1=f32(L0_attn_b_bk[1]),
        bv_b0=f32(L0_attn_b_bv[0]), bv_b1=f32(L0_attn_b_bv[1]), bo_b=f32(L0_attn_b_bo),
        lng=f32(L0_attn_b_ln_g), lnb=f32(L0_attn_b_ln_b),
        idf=np.eye(128, dtype=np.float32),
    )

    key = nphase
    if key not in _NC_CACHE:
        _NC_CACHE[key] = build(nphase)
    nc = _NC_CACHE[key]

    in_maps = []
    for c in range(8):
        b = c // 4
        off = (c % 4) * ROWS
        xq = np.roll(x[b], -off, axis=0)
        m = dict(common)
        m["xb"] = np.ascontiguousarray(xq)
        in_maps.append(m)

    res = run_bass_kernel_spmd(nc, in_maps, core_ids=list(range(8)))
    global LAST_RESULTS
    LAST_RESULTS = res.results
    shards = [np.asarray(res.results[c]["out"]) for c in range(8)]
    full = np.concatenate(shards, axis=0).reshape(B, S, E)
    return full.astype(np.float32)


# version 10 (final)
# speedup vs baseline: 1.1632x; 1.1135x over previous
import math
import os
from contextlib import ExitStack

import numpy as np
import ml_dtypes

import concourse.bass as bass
import concourse.mybir as mybir
from concourse.alu_op_type import AluOpType as OP
from concourse.bass_utils import run_bass_kernel_spmd

F32 = mybir.dt.float32
BF16 = mybir.dt.bfloat16
I32 = mybir.dt.int32
AF = mybir.ActivationFunctionType
AX = mybir.AxisListType

B, S, E = 2, 8192, 1024
H, D = 16, 64
WL = 16
ROWS = 2048
NRT = ROWS // 128
KERR = 2.7e-18 * 1e6
EPS = 1e-5
FF = 4096
NT = S // 128
NQ = 256
QCN = ROWS // NQ
TWO_PI = 2.0 * math.pi
ENGS = ("sync", "vector", "tensor", "scalar", "gpsimd")


class Sem:
    def __init__(self, handle):
        self.h = handle
        self.n = 0


class Prog:
    def __init__(self, nc):
        self.nc = nc
        self.q = {k: [] for k in ENGS}

    def on(self, eng, fn):
        self.q[eng].append(fn)

    def wait(self, eng, sem, val):
        if val > 0:
            self.q[eng].append(lambda e, s=sem.h, v=val: e.wait_ge(s, v))

    def emit(self):
        q = self.q
        with self.nc.Block() as blk:
            @blk.sync
            def _(e):
                for fn in q["sync"]:
                    fn(e)

            @blk.vector
            def _(e):
                for fn in q["vector"]:
                    fn(e)

            @blk.tensor
            def _(e):
                for fn in q["tensor"]:
                    fn(e)

            @blk.scalar
            def _(e):
                for fn in q["scalar"]:
                    fn(e)

            @blk.gpsimd
            def _(e):
                for fn in q["gpsimd"]:
                    fn(e)


def build(nphase=6):
    nc = bass.Bass("TRN2", target_bir_lowering=False)

    dri = lambda nm, shp, dt: nc.dram_tensor(nm, shp, dt, kind="ExternalInput").ap()
    xb = dri("xb", [S, E], F32)
    wq_a, wk_a, wv_a, wo_a = (dri(n, [E, E], F32) for n in ("wq_a", "wk_a", "wv_a", "wo_a"))
    bq_a, bk_a, bv_a, bo_a = (dri(n, [E], F32) for n in ("bq_a", "bk_a", "bv_a", "bo_a"))
    pmt = dri("pmt", [H, WL], F32)
    iw = dri("iw", [WL], F32)
    f_g, f_b = dri("f_g", [E], F32), dri("f_b", [E], F32)
    w1q = dri("w1q", [E, FF], I32)
    s1 = dri("s1", [8, FF], F32)
    fb1 = dri("fb1", [FF], F32)
    w2q = dri("w2q", [FF, E], I32)
    s2 = dri("s2", [32, E], F32)
    fb2 = dri("fb2", [E], F32)
    wq_b, wk_b0, wk_b1, wv_b0, wv_b1, wo_b = (
        dri(n, [E, E], F32) for n in ("wq_b", "wk_b0", "wk_b1", "wv_b0", "wv_b1", "wo_b"))
    bq_b, bk_b0, bk_b1, bv_b0, bv_b1, bo_b = (
        dri(n, [E], F32) for n in ("bq_b", "bk_b0", "bk_b1", "bv_b0", "bv_b1", "bo_b"))
    lng, lnb = dri("lng", [E], F32), dri("lnb", [E], F32)
    idf = dri("idf", [128, 128], F32)
    out = nc.dram_tensor("out", [ROWS, E], F32, kind="ExternalOutput").ap()

    kts = nc.dram_tensor("kts", [H, D, S], BF16).ap()
    vts = nc.dram_tensor("vts", [H, D, S], BF16).ap()
    qts = nc.dram_tensor("qts", [H, D, ROWS], BF16).ap()
    accs = nc.dram_tensor("accs", [E, ROWS], BF16).ap()
    flatr = nc.dram_tensor("flatr", [WL * H], F32).ap()
    w1ds = nc.dram_tensor("w1ds", [32, 128, 8, 128], BF16).ap()
    w2ds = nc.dram_tensor("w2ds", [8, 128, 32, 128], BF16).ap()
    ksr = nc.dram_tensor("ksr", [2, ROWS, E], BF16).ap()
    vsr = nc.dram_tensor("vsr", [2, ROWS, E], BF16).ap()
    qtb = nc.dram_tensor("qtb", [H, D, ROWS], BF16).ap()
    cin_t = nc.dram_tensor("cin_st", [128, E], F32)
    cout_t = nc.dram_tensor("cout_st", [128, E], F32)

    outer = ExitStack()
    names = ["dS", "dG", "c0", "c2", "c3", "c4", "c5",
             "eST", "eEV", "eXP", "eAQ", "eFN", "rRQ", "rBC", "rTM",
             "pVT", "vSB", "cAC", "pPJ", "pTR",
             "dx0", "dx1", "dx2", "dx3", "ds0", "ds1", "ds2", "ds3",
             "dk0", "dk1", "dn0", "dn1", "dq0", "dq1", "dw0", "dw1", "drb",
             "da0", "da1",
             "bS", "bV", "bT", "bA", "bG"]
    sm = {n: Sem(outer.enter_context(nc.semaphore(n))) for n in names}
    dS, dG = sm["dS"], sm["dG"]
    DX = [sm[f"dx{i}"] for i in range(4)]
    DS4 = [sm[f"ds{i}"] for i in range(4)]
    DK = [sm[f"dk{i}"] for i in range(2)]
    DN = [sm[f"dn{i}"] for i in range(2)]
    DQ = [sm[f"dq{i}"] for i in range(2)]
    DW = [sm[f"dw{i}"] for i in range(2)]
    DRB = sm["drb"]
    DA = [sm[f"da{i}"] for i in range(2)]
    SYNC_QS = [dS] + DX + DS4 + DK + DN + DQ + DW + [DRB] + DA
    BARMAP = dict(zip(ENGS, [sm["bS"], sm["bV"], sm["bT"], sm["bA"], sm["bG"]]))
    PH = [0]

    def barrier(P):
        PH[0] += 1
        tgt = PH[0]
        for q in SYNC_QS:
            P.wait("sync", q, q.n)
        P.wait("gpsimd", dG, dG.n)
        for en in ENGS:
            b = BARMAP[en]
            b.n += 1
            P.on(en, lambda e, _b=b: e.sem_inc(_b.h, 1))
        for en in ENGS:
            for b in BARMAP.values():
                P.wait(en, b, tgt)

    def _dma(P, eng, dst, src, q=None, slow=False):
        q = q if q is not None else (dG if eng == "gpsimd" else dS)
        q.n += 16
        P.on(eng, lambda e, d=dst, s=src, h=q.h, sl=slow: e.dma_start(
            out=d, in_=s, allow_slow_non_contiguous=sl).then_inc(h, 16))
        return q.n

    ones_f = outer.enter_context(nc.sbuf_tensor("ones_f", [1, 128], F32))
    ones_b = outer.enter_context(nc.sbuf_tensor("ones_b", [64, 1], BF16))
    idb_sb = outer.enter_context(nc.sbuf_tensor("idb_sb", [128, 128], BF16))
    epsc = outer.enter_context(nc.sbuf_tensor("epsc", [128, 1], F32))
    bcols = outer.enter_context(nc.sbuf_tensor("bcols", [128, 8 * 11], F32))
    BCN = ["bq_a", "bk_a", "bv_a", "bo_a", "f_g", "f_b", "fb2h", "bo_b", "lng", "lnb", "bq_b"]
    BC = {n: i * 8 for i, n in enumerate(BCN)}
    stackA = ExitStack()
    sccol = stackA.enter_context(nc.sbuf_tensor("sccol", [128, 1024], F32))
    iwd = stackA.enter_context(nc.sbuf_tensor("iwd", [1, WL], F32))
    stackA2 = ExitStack()
    misc = stackA2.enter_context(nc.sbuf_tensor("misc", [1, 16], F32))
    arow = stackA2.enter_context(nc.sbuf_tensor("arow", [1, 512], F32))
    ncur = stackA2.enter_context(nc.sbuf_tensor("ncur", [1, 4], F32))

    P = Prog(nc)
    _dma(P, "gpsimd", idb_sb[:], idf)
    for nm, src in [("bq_a", bq_a), ("bk_a", bk_a), ("bv_a", bv_a), ("bo_a", bo_a),
                    ("f_g", f_g), ("f_b", f_b), ("fb2h", fb2), ("bo_b", bo_b),
                    ("lng", lng), ("lnb", lnb), ("bq_b", bq_b)]:
        _dma(P, "sync", bcols[:, BC[nm]:BC[nm] + 8], src.rearrange("(c p) -> p c", p=128), slow=True)
    _dma(P, "sync", iwd[:], iw.unsqueeze(0))
    P.wait("vector", dS, dS.n)
    P.on("vector", lambda v: v.memset(ones_f[:], 1.0))
    P.on("vector", lambda v: v.memset(ones_b[:], 1.0))
    P.on("vector", lambda v: v.memset(epsc[:], EPS))
    P.on("vector", lambda v: v.memset(ncur[:], 0.0))
    P.on("vector", lambda v: v.tensor_scalar(out=iwd[:], in0=iwd[:], scalar1=1.0 / 16.0,
                                             scalar2=None, op0=OP.mult))
    barrier(P)
    P.emit()

    def emit_ln(P, mcol, src, center_dst, sqbuf, c2, c3, c4):
        c2.n += 1
        P.on("vector", lambda v: v.reduce_sum(out=mcol[:, 0:1], in_=src, axis=AX.X,
                                              negate=True).then_inc(c2.h, 1))
        P.wait("vector", c2, c2.n)
        c2.n += 1
        P.on("vector", lambda v: v.tensor_scalar(out=mcol[:, 0:1], in0=mcol[:, 0:1],
                                                 scalar1=1.0 / E, scalar2=None,
                                                 op0=OP.mult).then_inc(c2.h, 1))
        P.wait("vector", c2, c2.n)
        c2.n += 1
        P.on("vector", lambda v: v.tensor_scalar(out=center_dst, in0=src,
                                                 scalar1=mcol[:, 0:1], scalar2=None,
                                                 op0=OP.add).then_inc(c2.h, 1))
        P.wait("vector", c2, c2.n)
        c2.n += 1
        P.on("vector", lambda v: v.tensor_tensor(out=sqbuf, in0=center_dst, in1=center_dst,
                                                 op=OP.mult).then_inc(c2.h, 1))
        P.wait("vector", c2, c2.n)
        c3.n += 1
        P.on("vector", lambda v: v.reduce_sum(out=mcol[:, 1:2], in_=sqbuf,
                                              axis=AX.X).then_inc(c3.h, 1))
        P.wait("scalar", c3, c3.n)
        c4.n += 1
        P.on("scalar", lambda s: s.activation(out=mcol[:, 2:3], in_=mcol[:, 1:2], func=AF.Ln,
                                              scale=1.0 / E, bias=epsc[:]).then_inc(c4.h, 1))
        P.wait("scalar", c4, c4.n)
        c4.n += 1
        P.on("scalar", lambda s: s.activation(out=mcol[:, 3:4], in_=mcol[:, 2:3], func=AF.Exp,
                                              scale=-0.5).then_inc(c4.h, 1))
        P.wait("vector", c4, c4.n)

    with ExitStack() as st:
        en = st.enter_context
        wqs = en(nc.sbuf_tensor("wqs", [128, 8 * E], BF16))
        wks = en(nc.sbuf_tensor("wks", [128, 8 * E], BF16))
        wvs = en(nc.sbuf_tensor("wvs", [128, 8 * E], BF16))
        xt = [en(nc.sbuf_tensor(f"xt{i}", [128, E], F32)) for i in range(4)]
        sqb = en(nc.sbuf_tensor("sqb", [128, E], F32))
        xlb = [en(nc.sbuf_tensor(f"xlb{i}", [128, E], BF16)) for i in range(2)]
        xlnt = en(nc.sbuf_tensor("xlnt", [128, 8 * 512], BF16))
        stg = [en(nc.sbuf_tensor(f"stg{i}", [128, 512], BF16)) for i in range(4)]
        mcol = en(nc.sbuf_tensor("mcolA", [128, 4], F32))
        ptrp = [en(nc.psum_tensor(f"ptrp{i}", [128, 512], F32)) for i in range(2)]
        ppp = [en(nc.psum_tensor(f"pppA{i}", [128, 512], F32)) for i in range(2)]

        P = Prog(nc)
        c0, c2, c3, c4, c5 = (sm[x] for x in ("c0", "c2", "c3", "c4", "c5"))
        pTR, pPJ, eST, eEV = sm["pTR"], sm["pPJ"], sm["eST"], sm["eEV"]
        b0 = {x: sm[x].n for x in ("c0", "eST", "eEV", "pTR", "pPJ", "c5")}

        for wsb, wsrc in [(wqs, wq_a), (wks, wk_a), (wvs, wv_a)]:
            _dma(P, "gpsimd", wsb[:].rearrange("p (c n) -> p c n", n=E), wsrc.rearrange("(c p) n -> p c n", p=128))
        wload = dG.n

        stg_dma = {}
        chunk_est = {}
        nstg = 0
        for ch in range(16):
            own = ch < 4
            for r in range(4):
                i = ch * 4 + r
                if i >= 4:
                    P.wait("sync", c0, b0["c0"] + i - 3)
                dtot = _dma(P, "sync", xt[i % 4][:], xb[i * 128:(i + 1) * 128, :],
                            q=DX[i % 4])
                x = xt[i % 4][:]
                xl = xlb[i % 2]
                P.wait("vector", DX[i % 4], dtot)
                if i >= 2:
                    P.wait("vector", pTR, b0["pTR"] + (i - 1) * 8)
                emit_ln(P, mcol, x, x, sqb[:], c2, c3, c4)
                c0.n += 1
                P.on("vector", lambda v, x=x, xl=xl: v.tensor_scalar(
                    out=xl[:], in0=x, scalar1=mcol[:, 3:4], scalar2=None,
                    op0=OP.mult).then_inc(c0.h, 1))
                P.wait("tensor", c0, c0.n)
                if r == 0:
                    pass
                for fc in range(8):
                    gt = i * 8 + fc
                    pj = ptrp[gt % 2]
                    if gt >= 2:
                        P.wait("tensor", c5, b0["c5"] + gt - 1)
                    pTR.n += 1
                    P.on("tensor", lambda t, xl=xl, fc=fc, pj=pj: t.transpose(
                        out=pj.bitcast(BF16)[:, 0:128], in_=xl[:, fc * 128:(fc + 1) * 128],
                        identity=idb_sb[:]).then_inc(pTR.h, 1))
                    P.wait("vector", pTR, pTR.n)
                    if fc == 0 and r == 0 and ch > 0:
                        P.wait("vector", eST, chunk_est[ch - 1])
                    c5.n += 1
                    P.on("vector", lambda v, pj=pj, fc=fc, r=r: v.tensor_copy(
                        out=xlnt[:, fc * 512 + r * 128: fc * 512 + r * 128 + 128],
                        in_=pj.bitcast(BF16)[:, 0:128]).then_inc(c5.h, 1))
            xlnt_ready = c5.n
            P.wait("tensor", c5, xlnt_ready)
            if ch == 0:
                P.wait("tensor", dG, wload)
            groups = [("K", wks, BC["bk_a"], kts), ("V", wvs, BC["bv_a"], vts)]
            if own:
                groups.append(("Q", wqs, BC["bq_a"], qts))
            for gn, wsb, bcoff, dstt in groups:
                for dp in range(8):
                    jj = nstg
                    pj = ppp[jj % 2]
                    sgt = stg[jj % 4]
                    if jj >= 2:
                        P.wait("tensor", eEV, b0["eEV"] + jj - 1)
                    for fc in range(8):
                        if fc == 7:
                            eST.n += 1
                        P.on("tensor", lambda t, wsb=wsb, fc=fc, dp=dp, pj=pj: (
                            t.matmul(out=pj[:],
                                     lhsT=wsb[:, fc * E + dp * 128: fc * E + dp * 128 + 128],
                                     rhs=xlnt[:, fc * 512:(fc + 1) * 512],
                                     start=(fc == 0), stop=(fc == 7)).then_inc(eST.h, 1)
                            if fc == 7 else
                            t.matmul(out=pj[:],
                                     lhsT=wsb[:, fc * E + dp * 128: fc * E + dp * 128 + 128],
                                     rhs=xlnt[:, fc * 512:(fc + 1) * 512],
                                     start=(fc == 0), stop=(fc == 7))))
                    P.wait("scalar", eST, eST.n)
                    if jj >= 4:
                        P.wait("scalar", DS4[jj % 4], stg_dma[jj - 4])
                    eEV.n += 1
                    P.on("scalar", lambda s, pj=pj, sgt=sgt, bcoff=bcoff, dp=dp: s.activation(
                        out=sgt[:], in_=pj[:], func=AF.Identity,
                        bias=bcols[:, bcoff + dp: bcoff + dp + 1]).then_inc(eEV.h, 1))
                    P.wait("sync", eEV, eEV.n)
                    base = ch * 512
                    _dma(P, "sync", dstt[2 * dp, :, base:base + 512], sgt[0:64, :],
                         q=DS4[jj % 4])
                    stg_dma[jj] = _dma(P, "sync", dstt[2 * dp + 1, :, base:base + 512],
                                       sgt[64:128, :], q=DS4[jj % 4])
                    nstg += 1
            chunk_est[ch] = eST.n
        barrier(P)
        P.emit()

    with ExitStack() as st:
        en = st.enter_context
        nbuf = en(nc.sbuf_tensor("nbuf", [64, S], BF16))
        nps = en(nc.sbuf_tensor("nps", [1, 512], F32))
        pm_sb = en(nc.sbuf_tensor("pm_sb", [H, WL], F32))
        t16 = en(nc.sbuf_tensor("t16", [H, WL], F32))
        t16b = en(nc.sbuf_tensor("t16b", [H, WL], F32))
        pnp = [en(nc.psum_tensor(f"pnp{i}", [128, 512], F32)) for i in range(2)]
        psc = en(nc.psum_tensor("psc", [128, 512], F32))

        P = Prog(nc)
        c2, c3 = sm["c2"], sm["c3"]
        eST = sm["eST"]

        def vstep(fn):
            c2.n += 1
            P.on("vector", lambda v, f=fn: f(v).then_inc(c2.h, 1))
            P.wait("vector", c2, c2.n)

        def sstep(fn):
            P.wait("scalar", c2, c2.n)
            c3.n += 1
            P.on("scalar", lambda s, f=fn: f(s).then_inc(c3.h, 1))
            P.wait("vector", c3, c3.n)

        for src, ncol, ncols in [(kts, 1, S), (qts, 0, ROWS)]:
            for h in range(H):
                P.wait("sync", c2, c2.n)
                _dma(P, "sync", nbuf[:, 0:ncols], src[h])
                P.wait("vector", dS, dS.n)
                vstep(lambda v, n=ncols: v.tensor_tensor(
                    out=nbuf[:, 0:n], in0=nbuf[:, 0:n], in1=nbuf[:, 0:n], op=OP.mult))
                for j in range(ncols // 512):
                    pj = pnp[j % 2]
                    P.wait("tensor", c2, c2.n)
                    eST.n += 1
                    P.on("tensor", lambda t, pj=pj, j=j: t.matmul(
                        out=pj[0:1, :], lhsT=ones_b[:], rhs=nbuf[:, j * 512:(j + 1) * 512],
                        start=True, stop=True).then_inc(eST.h, 1))
                    P.wait("vector", eST, eST.n)
                    vstep(lambda v, pj=pj: v.tensor_copy(out=nps[:], in_=pj[0:1, :]))
                    vstep(lambda v: v.reduce_max(out=nps[0:1, 0:1], in_=nps[:], axis=AX.X))
                    vstep(lambda v, c=ncol: v.tensor_tensor(
                        out=ncur[0:1, c:c + 1], in0=ncur[0:1, c:c + 1],
                        in1=nps[0:1, 0:1], op=OP.max))

        P.wait("sync", c2, c2.n)
        _dma(P, "sync", pm_sb[:], pmt)
        P.wait("vector", dS, dS.n)
        MAGIC = 12582912.0
        vstep(lambda v: v.tensor_scalar(out=t16[:], in0=pm_sb[:], scalar1=1.0 / TWO_PI,
                                        scalar2=MAGIC, op0=OP.mult, op1=OP.add))
        vstep(lambda v: v.tensor_scalar(out=t16[:], in0=t16[:], scalar1=-MAGIC,
                                        scalar2=None, op0=OP.add))
        vstep(lambda v: v.scalar_tensor_tensor(out=t16[:], in0=t16[:], scalar=-TWO_PI,
                                               in1=pm_sb[:], op0=OP.mult, op1=OP.add))
        sstep(lambda s: s.activation(out=t16b[:], in_=t16[:], func=AF.Sin, scale=0.5))
        vstep(lambda v: v.tensor_tensor(out=t16[:], in0=t16b[:], in1=t16b[:], op=OP.mult))
        vstep(lambda v: v.tensor_scalar(out=t16[:], in0=t16[:], scalar1=-2.0, scalar2=1.0,
                                        op0=OP.mult, op1=OP.add))
        vstep(lambda v: v.tensor_tensor(out=t16b[:], in0=t16[:], in1=t16[:], op=OP.mult))
        vstep(lambda v: v.tensor_scalar(out=t16b[:], in0=t16b[:], scalar1=0.125,
                                        scalar2=None, op0=OP.mult))
        P.wait("sync", c2, c2.n)
        _dma(P, "sync", flatr.rearrange("(h c) -> h c", c=WL), t16b[:])
        P.wait("sync", dS, dS.n)
        _dma(P, "sync", arow[0:1, 0:256], flatr.unsqueeze(0))
        P.wait("vector", dS, dS.n)
        vstep(lambda v: v.reduce_max(out=misc[0:1, 4:5], in_=arow[0:1, 0:256], axis=AX.X))
        vstep(lambda v: v.tensor_tensor(out=misc[0:1, 5:6], in0=ncur[0:1, 0:1],
                                        in1=ncur[0:1, 1:2], op=OP.mult))
        sstep(lambda s: s.activation(out=misc[0:1, 6:7], in_=misc[0:1, 5:6], func=AF.Ln))
        P.wait("scalar", c3, c3.n)
        c3.n += 1
        P.on("scalar", lambda s: s.activation(out=misc[0:1, 2:3], in_=misc[0:1, 6:7],
                                              func=AF.Exp, scale=0.5).then_inc(c3.h, 1))
        P.wait("vector", c3, c3.n)
        vstep(lambda v: v.tensor_tensor(out=misc[0:1, 7:8], in0=misc[0:1, 4:5],
                                        in1=misc[0:1, 2:3], op=OP.mult))
        vstep(lambda v: v.tensor_tensor(out=misc[0:1, 8:9], in0=misc[0:1, 7:8],
                                        in1=misc[0:1, 7:8], op=OP.mult))
        sstep(lambda s: s.activation(out=misc[0:1, 9:10], in_=misc[0:1, 8:9], func=AF.Sin,
                                     scale=KERR / 2.0))
        vstep(lambda v: v.tensor_tensor(out=misc[0:1, 10:11], in0=misc[0:1, 9:10],
                                        in1=misc[0:1, 9:10], op=OP.mult))
        vstep(lambda v: v.tensor_scalar(out=misc[0:1, 3:4], in0=misc[0:1, 10:11],
                                        scalar1=-2.0, scalar2=1.0, op0=OP.mult, op1=OP.add))
        vstep(lambda v: v.tensor_scalar(out=arow[0:1, 0:256], in0=arow[0:1, 0:256],
                                        scalar1=misc[0:1, 3:4], scalar2=None, op0=OP.mult))
        vstep(lambda v: v.tensor_scalar(out=arow[0:1, 256:512], in0=arow[0:1, 0:256],
                                        scalar1=misc[0:1, 2:3], scalar2=-1.0,
                                        op0=OP.mult, op1=OP.mult))
        P.wait("tensor", c2, c2.n)
        eST.n += 1
        P.on("tensor", lambda t: t.matmul(out=psc[:], lhsT=ones_f[:], rhs=arow[:],
                                          start=True, stop=True).then_inc(eST.h, 1))
        P.wait("vector", eST, eST.n)
        vstep(lambda v: v.tensor_copy(out=sccol[:, 0:512], in_=psc[:]))
        barrier(P)
        P.emit()

    stackA2.close()
    if nphase < 2:
        NH_RUN = 0
    else:
        NH_RUN = int(os.environ.get("MK_NH", str(H)))
    A3CUT = int(os.environ.get("MK_A3CUT", "5"))
    with ExitStack() as st:
        en = st.enter_context
        spr = en(nc.sbuf_tensor("spr", [128, 4 * 8192], F32))
        pbuf = en(nc.sbuf_tensor("pbuf", [128, 3 * 32 * NQ], BF16))
        ktb1 = en(nc.sbuf_tensor("ktb0", [64, S], BF16))
        ktb = [ktb1, ktb1]
        vtb = en(nc.sbuf_tensor("vtb0", [128, 512], BF16))
        vsb1 = en(nc.sbuf_tensor("vsb0", [128, NT * 65 + 2 * NQ], BF16))
        vsb = [vsb1, vsb1]
        stp = [en(nc.psum_tensor(f"stp{i}", [128, 1024], F32)) for i in range(2)]
        avp = [en(nc.psum_tensor(f"avp{i}", [128, 512], F32)) for i in range(2)]
        rbcp = en(nc.psum_tensor("rbcp", [128, 512], F32))
        vtpp = [en(nc.psum_tensor("vtpp0", [128, 256], BF16))] * 2

        P = Prog(nc)
        eST, eEV, eXP, eAQ, eFN = (sm[x] for x in ("eST", "eEV", "eXP", "eAQ", "eFN"))
        rRQ, rBC, rTM = sm["rRQ"], sm["rBC"], sm["rTM"]
        pVT, vSB, cAC, c2 = sm["pVT"], sm["vSB"], sm["cAC"], sm["c2"]
        A = {x: sm[x].n for x in ("eST", "eEV", "eXP", "eAQ", "eFN", "rRQ", "rBC",
                                  "rTM", "pVT", "vSB", "c2", "cAC")}
        dS_kv = {}
        dS_vt = {}
        acc_dma = {}
        vsb_done = {}

        P.on("vector", lambda v: v.memset(
            vsb[0][:, 0:NT * 65].rearrange("p (t c) -> p t c", c=65)[:, :, 64:65], 1.0))

        for h in range(NH_RUN):
            if h >= 1:
                P.wait("sync", eST, A["eST"] + h * 128)
            if h >= 1 and A3CUT >= 3:
                P.wait("sync", pVT, A["pVT"] + h * (NT // 2))
            v1 = _dma(P, "sync", ktb[h % 2][:, 0:S // 2], kts[h][:, 0:S // 2],
                      q=DK[0])
            v2 = _dma(P, "sync", ktb[h % 2][:, S // 2:S], kts[h][:, S // 2:S],
                      q=DK[1])
            vsrc = vts[h].rearrange("d (u two j) -> d two u j", two=2, j=128)
            dS_vt[h] = []
            for kk in range(8):
                qa, qb = ((DX[1], DX[2]), (DS4[0], DS4[1]))[kk % 2]
                if kk > 0:
                    P.wait("sync", pVT, A["pVT"] + h * 32 + 4 * kk)
                va = _dma(P, "sync", vtb[0:64, :].rearrange("d (u j) -> d u j", j=128),
                          vsrc[:, 0, 4 * kk:4 * kk + 4, :], q=qa)
                vb = _dma(P, "sync",
                          vtb[64:128, :].rearrange("d (u j) -> d u j", j=128),
                          vsrc[:, 1, 4 * kk:4 * kk + 4, :], q=qb)
                dS_vt[h].append([(qa, va), (qb, vb)])
            dS_kv[h] = [(DK[0], v1), (DK[1], v2)]

        def emit_fills(i, trange=None):
            h, qc = divmod(i, QCN)
            for sq, vv_ in dS_kv[h]:
                P.wait("tensor", sq, vv_)
            P.wait("tensor", DQ[0], qld[i])
            for tf in (trange if trange is not None else range(16)):
                gt = i * 16 + tf
                pj = stp[gt % 2]
                if gt >= 2:
                    P.wait("tensor", eEV, A["eEV"] + gt - 1)
                for sub in range(4):
                    t_ = tf * 4 + sub
                    if sub == 3:
                        eST.n += 1
                    P.on("tensor", lambda t, h=h, t_=t_, sub=sub, pj=pj, i=i: (
                        t.matmul(out=pj[:, (t_ % 4) * 256:(t_ % 4) * 256 + 256],
                                 lhsT=ktb[h % 2][:, t_ * 128:(t_ + 1) * 128],
                                 rhs=vsb1[0:64, NT * 65:NT * 65 + NQ],
                                 start=True, stop=True).then_inc(eST.h, 1)
                        if sub == 3 else
                        t.matmul(out=pj[:, (t_ % 4) * 256:(t_ % 4) * 256 + 256],
                                 lhsT=ktb[h % 2][:, t_ * 128:(t_ + 1) * 128],
                                 rhs=vsb1[0:64, NT * 65:NT * 65 + NQ],
                                 start=True, stop=True)))
                P.wait("vector", eST, eST.n)
                g = i * 2 + tf // 8
                if g >= 4 and A3CUT >= 2:
                    gp = g - 4
                    P.wait("vector", eXP,
                           A["eXP"] + (gp // 2) * 32 + 31 + (gp % 2))
                eEV.n += 1
                sg = g % 4
                P.on("vector", lambda v, pj=pj, tf=tf, sg=sg: v.tensor_copy(
                    out=spr[:, sg * 8192 + (tf % 8) * 1024: sg * 8192 + (tf % 8) * 1024 + 1024],
                    in_=pj[:]).then_inc(eEV.h, 1))

        def emit_vbuild(h, urange=None):
            if h >= 1:
                if A3CUT >= 4:
                    P.wait("tensor", eAQ, A["eAQ"] + h * 256)
                if A3CUT >= 5:
                    P.wait("tensor", rTM, A["rTM"] + (h - 1) * 8 * 16)
            for u in (urange if urange is not None else range(NT // 2)):
                if u % 4 == 0:
                    for sq, vv_ in dS_vt[h][u // 4]:
                        P.wait("tensor", sq, vv_)
                vv = vtpp[u % 2][:, 0:128]
                P.wait("tensor", vSB, vSB.n)
                pVT.n += 1
                P.on("tensor", lambda t, u=u, vv=vv: t.transpose(
                    out=vv, in_=vtb[:, (u % 4) * 128:(u % 4) * 128 + 128],
                    identity=idb_sb[:]).then_inc(pVT.h, 1))
                P.wait("vector", pVT, pVT.n)
                for par in range(2):
                    t_ = 2 * u + par
                    vSB.n += 1
                    P.on("vector", lambda v, t_=t_, vv=vv, par=par: v.tensor_copy(
                        out=vsb[h % 2][:, t_ * 65: t_ * 65 + 64],
                        in_=vv[:, par * 64:par * 64 + 64]).then_inc(vSB.h, 1))
            if urange is None or urange[-1] == NT // 2 - 1:
                vsb_done[h] = vSB.n

        EFN_REQ = {ch: 2 * (ch // 2) - 2 for ch in range(16)}

        def emit_av(i, ch):
            h, qc = divmod(i, QCN)
            if A3CUT >= 5:
                P.wait("tensor", eFN, A["eFN"] + i * 16 + EFN_REQ[ch])
            if A3CUT >= 3:
                P.wait("tensor", vSB, vsb_done[h])
            for hh in range(2):
                idx = i * 32 + ch * 2 + hh
                slot = idx % 3
                P.wait("tensor", eXP, A["eXP"] + idx + 1)
                for tl in range(32):
                    t_ = hh * 32 + tl
                    if tl == 31:
                        eAQ.n += 1
                    P.on("tensor", lambda t, h=h, ch=ch, t_=t_, tl=tl, slot=slot: (
                        t.matmul(out=avp[(ch // 2) % 2][0:65, (ch % 2) * 256:(ch % 2) * 256 + 256],
                                 lhsT=vsb[h % 2][:, t_ * 65:(t_ + 1) * 65],
                                 rhs=pbuf[:, slot * 8192 + tl * 256: slot * 8192 + tl * 256 + 256],
                                 start=(t_ == 0), stop=(t_ == 63)).then_inc(eAQ.h, 1)
                        if tl == 31 else
                        t.matmul(out=avp[(ch // 2) % 2][0:65, (ch % 2) * 256:(ch % 2) * 256 + 256],
                                 lhsT=vsb[h % 2][:, t_ * 65:(t_ + 1) * 65],
                                 rhs=pbuf[:, slot * 8192 + tl * 256: slot * 8192 + tl * 256 + 256],
                                 start=(t_ == 0), stop=(t_ == 63))))


        def emit_rbc(i, p):
            g = i * 8 + p
            P.wait("tensor", rRQ, A["rRQ"] + g + 1)
            P.wait("tensor", rTM, A["rTM"] + 2 * g)
            rBC.n += 1
            P.on("tensor", lambda t: t.matmul(
                out=rbcp[0:64, 0:512], lhsT=ones_f[0:1, 0:64], rhs=sccol[0:1, 512:1024],
                start=True, stop=True).then_inc(rBC.h, 1))

        def emit_finish(i, p):
            che, cho = 2 * p, 2 * p + 1
            pt = avp[p % 2]
            g = i * 8 + p
            if p == 0 and i > 0:
                P.wait("vector", DA[0], acc_dma[i - 1])
            P.wait("vector", eAQ, A["eAQ"] + i * 32 + cho * 2 + 2)
            P.wait("vector", rBC, A["rBC"] + g)
            for chl, col in ((che, 0), (cho, 256)):
                c2.n += 1
                P.on("vector", lambda v, pt=pt, col=col: v.reciprocal(
                    out=sccol[0:1, 512 + col:768 + col],
                    in_=pt[64:65, col:col + 256]).then_inc(c2.h, 1))
                P.wait("vector", c2, c2.n)
                if chl == cho:
                    rRQ.n += 1
                P.on("vector", lambda v, chl=chl, col=col, last=(chl == cho): (
                    v.tensor_scalar(out=sccol[0:1, 512 + col:768 + col],
                                    in0=sccol[0:1, 512 + col:768 + col],
                                    scalar1=iwd[0:1, chl:chl + 1], scalar2=None,
                                    op0=OP.mult).then_inc(rRQ.h, 1)
                    if last else
                    v.tensor_scalar(out=sccol[0:1, 512 + col:768 + col],
                                    in0=sccol[0:1, 512 + col:768 + col],
                                    scalar1=iwd[0:1, chl:chl + 1], scalar2=None,
                                    op0=OP.mult)))
            P.wait("vector", rBC, A["rBC"] + g + 1)
            for chl, col in ((che, 0), (cho, 256)):
                c2.n += 1
                P.on("vector", lambda v, pt=pt, col=col: v.tensor_copy(
                    out=vsb1[64:128, NT * 65:NT * 65 + NQ], in_=pt[0:64, col:col + 256]).then_inc(c2.h, 1))
                P.wait("vector", c2, c2.n)
                rTM.n += 1
                P.on("vector", lambda v, col=col: v.tensor_tensor(
                    out=vsb1[64:128, NT * 65 + NQ:NT * 65 + 2 * NQ],
                    in0=vsb1[64:128, NT * 65:NT * 65 + NQ], in1=rbcp[0:64, col:col + 256],
                    op=OP.mult).then_inc(rTM.h, 1))
                P.wait("vector", rTM, rTM.n)
                eFN.n += 1
                if chl == 0:
                    P.on("vector", lambda v, i=i: v.tensor_copy(
                        out=sccol[64:128, 512:768],
                        in_=vsb1[64:128, NT * 65 + NQ:NT * 65 + 2 * NQ]).then_inc(eFN.h, 1))
                else:
                    P.on("vector", lambda v, i=i: v.tensor_tensor(
                        out=sccol[64:128, 512:768], in0=sccol[64:128, 512:768],
                        in1=vsb1[64:128, NT * 65 + NQ:NT * 65 + 2 * NQ],
                        op=OP.add).then_inc(eFN.h, 1))

        NITEM = NH_RUN * QCN
        qld = {}

        def emit_qload(j):
            h_, qc_ = divmod(j, QCN)
            qld[j] = _dma(P, "gpsimd", vsb1[0:64, NT * 65:NT * 65 + NQ],
                          qts[h_][:, qc_ * NQ:(qc_ + 1) * NQ], q=DQ[0])

        if NITEM:
            emit_qload(0)
        if NITEM:
            if A3CUT >= 3:
                emit_vbuild(0)
            emit_fills(0)
        for i in range(NITEM):
            h, qc = divmod(i, QCN)
            if i + 1 < NITEM:
                P.wait("gpsimd", eST, A["eST"] + (i + 1) * 16)
                emit_qload(i + 1)
            for ch in range(16) if A3CUT >= 2 else []:
                for hh in range(2):
                    idx = i * 32 + ch * 2 + hh
                    slot = idx % 3
                    if ch == 0:
                        P.wait("scalar", eEV, A["eEV"] + i * 16 + (hh + 1) * 8)
                    if idx >= 3 and A3CUT >= 4:
                        P.wait("scalar", eAQ, A["eAQ"] + idx - 2)
                    eXP.n += 1
                    hc = h * 16 + ch
                    sg = (i * 2 + hh) % 4
                    P.on("scalar", lambda s, sg=sg, slot=slot, hc=hc: s.activation(
                        out=pbuf[:, slot * 8192:(slot + 1) * 8192],
                        in_=spr[:, sg * 8192:(sg + 1) * 8192], func=AF.Exp,
                        scale=sccol[:, hc:hc + 1],
                        bias=sccol[:, 256 + hc:256 + hc + 1]).then_inc(eXP.h, 1))
            if A3CUT >= 4:
                for ch in range(16):
                    emit_av(i, ch)
                    if i + 1 < NITEM:
                        emit_fills(i + 1, [ch])
                    if ch % 2 == 1 and A3CUT >= 5:
                        emit_rbc(i, ch // 2)
                        emit_finish(i, ch // 2)
                if i + 1 < NITEM and qc == QCN - 1 and A3CUT >= 3:
                    emit_vbuild(h + 1)
            else:
                if i + 1 < NITEM:
                    emit_fills(i + 1)
                    if qc == QCN - 1 and A3CUT >= 3:
                        emit_vbuild(h + 1)
            if A3CUT < 5:
                continue
            P.wait("gpsimd", eFN, eFN.n)
            acc_dma[i] = _dma(P, "gpsimd",
                              accs[h * 64:(h + 1) * 64, qc * NQ:(qc + 1) * NQ],
                              sccol[64:128, 512:768], q=DA[0])
        barrier(P)
        P.emit()

    stackA.close()
    xres = outer.enter_context(nc.sbuf_tensor("xres", [128, NRT * E], F32))
    idf_sb = outer.enter_context(nc.sbuf_tensor("idf_sb", [128, 128], F32))
    fb1c = outer.enter_context(nc.sbuf_tensor("fb1c", [128, 32], F32))
    fb2c = outer.enter_context(nc.sbuf_tensor("fb2c", [128, 8], F32))
    bkvc = outer.enter_context(nc.sbuf_tensor("bkvc", [128, 5 * 8], F32))

    a4_on = nphase >= 3 and NH_RUN == H
    if a4_on:
     with ExitStack() as st:
        en = st.enter_context
        acct = en(nc.sbuf_tensor("acct", [128, 8 * ROWS], BF16))
        wos = en(nc.sbuf_tensor("wosA", [128, 8 * E], BF16))
        otT = en(nc.sbuf_tensor("otT", [128, 8 * ROWS], F32))
        xst = [en(nc.sbuf_tensor(f"xstA{i}", [128, E], F32)) for i in range(2)]
        ppj = [en(nc.psum_tensor(f"ppjA{i}", [128, 512], F32)) for i in range(2)]

        P = Prog(nc)
        c0, c2, c3, c4 = (sm[x] for x in ("c0", "c2", "c3", "c4"))
        eST, eEV, pTR, c5 = sm["eST"], sm["eEV"], sm["pTR"], sm["c5"]
        A4 = {x: sm[x].n for x in ("eST", "eEV", "pTR", "c5", "c0")}

        _dma(P, "gpsimd", wos[:].rearrange("p (c n) -> p c n", n=E), wo_a.rearrange("(c p) n -> p c n", p=128))
        ldI = _dma(P, "sync", idf_sb[:], idf, q=DK[0])
        _dma(P, "sync", fb1c[:], fb1.rearrange("(c p) -> p c", p=128), q=DK[0], slow=True)
        for i_, src_ in enumerate([bq_b, bk_b0, bk_b1, bv_b0, bv_b1]):
            _dma(P, "sync", bkvc[:, i_ * 8:(i_ + 1) * 8],
                 src_.rearrange("(c p) -> p c", p=128), q=DK[0], slow=True)
        ldI = DK[0].n
        P.wait("tensor", DK[0], ldI)
        P.wait("vector", DK[0], ldI)
        c5.n += 1
        P.on("vector", lambda v: v.tensor_scalar(
            out=fb2c[:], in0=bcols[:, BC["fb2h"]:BC["fb2h"] + 8],
            scalar1=0.5, scalar2=None, op0=OP.mult).then_inc(c5.h, 1))
        _dma(P, "sync", acct[:].rearrange("p (c n) -> p c n", n=ROWS), accs.rearrange("(c p) n -> p c n", p=128))
        ldT = dS.n
        P.wait("tensor", dS, ldT)
        P.wait("tensor", dG, dG.n)
        nj = 0
        for dp in range(8):
            for rc in range(4):
                pj = ppj[nj % 2]
                if nj >= 2:
                    P.wait("tensor", eEV, A4["eEV"] + nj - 1)
                for fc in range(8):
                    if fc == 7:
                        eST.n += 1
                    P.on("tensor", lambda t, fc=fc, dp=dp, rc=rc, pj=pj: (
                        t.matmul(out=pj[:],
                                 lhsT=wos[:, fc * E + dp * 128: fc * E + dp * 128 + 128],
                                 rhs=acct[:, fc * ROWS + rc * 512: fc * ROWS + rc * 512 + 512],
                                 start=(fc == 0), stop=(fc == 7)).then_inc(eST.h, 1)
                        if fc == 7 else
                        t.matmul(out=pj[:],
                                 lhsT=wos[:, fc * E + dp * 128: fc * E + dp * 128 + 128],
                                 rhs=acct[:, fc * ROWS + rc * 512: fc * ROWS + rc * 512 + 512],
                                 start=(fc == 0), stop=(fc == 7))))
                P.wait("scalar", eST, eST.n)
                eEV.n += 1
                P.on("scalar", lambda s, pj=pj, dp=dp, rc=rc: s.activation(
                    out=otT[:, dp * ROWS + rc * 512: dp * ROWS + rc * 512 + 512],
                    in_=pj[:], func=AF.Identity,
                    bias=bcols[:, BC["bo_a"] + dp: BC["bo_a"] + dp + 1]).then_inc(eEV.h, 1))
                nj += 1
        for rt in range(NRT):
            if rt >= 2:
                P.wait("sync", c0, A4["c0"] + (rt - 1) * 8)
            dtot = _dma(P, "sync", xst[rt % 2][:], xb[rt * 128:(rt + 1) * 128, :],
                        q=DX[rt % 2])
            P.wait("tensor", eEV, eEV.n)
            P.wait("vector", DX[rt % 2], dtot)
            for fc in range(8):
                gt = rt * 8 + fc
                pj = ppj[gt % 2]
                if gt >= 2:
                    P.wait("tensor", c0, A4["c0"] + gt - 1)
                pTR.n += 1
                P.on("tensor", lambda t, fc=fc, rt=rt, pj=pj: t.transpose(
                    out=pj[:, 0:128], in_=otT[:, fc * ROWS + rt * 128: fc * ROWS + rt * 128 + 128],
                    identity=idf_sb[:]).then_inc(pTR.h, 1))
                P.wait("vector", pTR, pTR.n)
                c0.n += 1
                P.on("vector", lambda v, fc=fc, rt=rt, pj=pj: v.tensor_tensor(
                    out=xres[:, rt * E + fc * 128: rt * E + fc * 128 + 128],
                    in0=xst[rt % 2][:, fc * 128:(fc + 1) * 128], in1=pj[:, 0:128],
                    op=OP.add).then_inc(c0.h, 1))
        barrier(P)
        P.emit()

    def emit_dequant():
        with ExitStack() as st:
            en = st.enter_context
            qw = en(nc.sbuf_tensor("qw", [128, FF], I32))
            srow = en(nc.sbuf_tensor("srow", [1, FF], F32))
            scf = en(nc.sbuf_tensor("scf", [128, FF], F32))
            tqf = en(nc.sbuf_tensor("tqf", [128, FF], F32))
            wbb = en(nc.sbuf_tensor("wbb", [128, FF], BF16))
            pbq = [en(nc.psum_tensor(f"pbq{i}", [128, 512], F32)) for i in range(2)]

            P = Prog(nc)
            c2, eST = sm["c2"], sm["eST"]
            for w_src, s_src, wd, nkc, width, nw in [
                    (w1q, s1, w1ds, 8, FF, 32), (w2q, s2, w2ds, 32, E, 8)]:
                for kc in range(nkc):
                    P.wait("sync", dG, dG.n)
                    _dma(P, "sync", qw[:, 0:width], w_src[kc * 128:(kc + 1) * 128, :])
                    _dma(P, "sync", srow[0:1, 0:width], s_src[kc:kc + 1, :])
                    dtot = dS.n
                    P.wait("tensor", dS, dtot)
                    P.wait("tensor", c2, c2.n)
                    base_c2 = c2.n
                    for j in range(width // 512):
                        pj = pbq[j % 2]
                        if j >= 2:
                            P.wait("tensor", c2, base_c2 + j - 1)
                        eST.n += 1
                        P.on("tensor", lambda t, pj=pj, j=j: t.matmul(
                            out=pj[:], lhsT=ones_f[:], rhs=srow[0:1, j * 512:(j + 1) * 512],
                            start=True, stop=True).then_inc(eST.h, 1))
                        P.wait("vector", eST, eST.n)
                        c2.n += 1
                        P.on("vector", lambda v, pj=pj, j=j: v.tensor_copy(
                            out=scf[:, j * 512:(j + 1) * 512], in_=pj[:]).then_inc(c2.h, 1))
                    P.wait("vector", dS, dtot)
                    c2.n += 1
                    P.on("vector", lambda v, width=width: v.tensor_scalar(
                        out=tqf[:, 0:width], in0=qw[:, 0:width], scalar1=-128.0,
                        scalar2=None, op0=OP.add).then_inc(c2.h, 1))
                    P.wait("vector", c2, c2.n)
                    c2.n += 1
                    P.on("vector", lambda v, width=width: v.tensor_tensor(
                        out=wbb[:, 0:width], in0=tqf[:, 0:width], in1=scf[:, 0:width],
                        op=OP.mult).then_inc(c2.h, 1))
                    P.wait("gpsimd", c2, c2.n)
                    for oc in range(nw):
                        _dma(P, "gpsimd", wd[oc, :, kc, :], wbb[:, oc * 128:(oc + 1) * 128])
            barrier(P)
            P.emit()

    def emit_ffn(tag):
        with ExitStack() as st:
            en = st.enter_context
            ybuf = en(nc.sbuf_tensor(f"ybuf{tag}", [128, 4 * E], F32))
            cb = en(nc.sbuf_tensor(f"cb{tag}", [128, E], F32))
            sqb2 = en(nc.sbuf_tensor(f"sqb2{tag}", [128, E], F32))
            hrow = en(nc.sbuf_tensor(f"hrow{tag}", [128, E], BF16))
            hT = en(nc.sbuf_tensor(f"hT{tag}", [128, 8 * 512], BF16))
            y1T = en(nc.sbuf_tensor(f"y1T{tag}", [128, 32 * 512], BF16))
            zt = en(nc.sbuf_tensor(f"zt{tag}", [128, 8 * 512], F32))
            w1sb = [en(nc.sbuf_tensor(f"w1sb{tag}{i}", [128, 8 * 128], BF16)) for i in range(2)]
            w2sb = [en(nc.sbuf_tensor(f"w2sb{tag}{i}", [128, 32 * 128], BF16)) for i in range(2)]
            mcol = en(nc.sbuf_tensor(f"mcolF{tag}", [128, 4], F32))
            ppf = [en(nc.psum_tensor(f"ppf{tag}{i}", [128, 512], F32)) for i in range(2)]

            P = Prog(nc)
            c0, c2, c3, c4, c5 = (sm[x] for x in ("c0", "c2", "c3", "c4", "c5"))
            eST, eEV, pTR, pPJ = sm["eST"], sm["eEV"], sm["pTR"], sm["pPJ"]
            FB = {x: sm[x].n for x in ("eST", "eEV", "pTR", "c5", "c0", "pPJ")}
            w1dma = {}
            w2dma = {}
            hT_ptr = {}

            for rc in range(4):
                for r in range(4):
                    rt = rc * 4 + r
                    xsl = xres[:, rt * E:(rt + 1) * E]
                    ysl = ybuf[:, r * E:(r + 1) * E]
                    emit_ln(P, mcol, xsl, ysl, sqb2[:], c2, c3, c4)
                    c2.n += 1
                    P.on("vector", lambda v, ysl=ysl: v.tensor_scalar(
                        out=ysl, in0=ysl, scalar1=mcol[:, 3:4], scalar2=None,
                        op0=OP.mult).then_inc(c2.h, 1))
                    P.wait("vector", c2, c2.n)
                    c2.n += 1
                    P.on("vector", lambda v, xsl=xsl, ysl=ysl: v.tensor_tensor(
                        out=xsl, in0=xsl, in1=ysl, op=OP.add).then_inc(c2.h, 1))
                    P.wait("vector", c2, c2.n)
                    emit_ln(P, mcol, ysl, cb[:], sqb2[:], c2, c3, c4)
                    gr = rc * 4 + r
                    if gr >= 1:
                        P.wait("vector", pTR, hT_ptr[gr - 1])
                    c0.n += 1
                    P.on("vector", lambda v: v.tensor_scalar(
                        out=hrow[:], in0=cb[:], scalar1=mcol[:, 3:4], scalar2=None,
                        op0=OP.mult).then_inc(c0.h, 1))
                    P.wait("tensor", c0, c0.n)
                    if r == 0 and rc > 0:
                        P.wait("tensor", c0, FB["c0"] + rc * 36)
                    for fc in range(8):
                        gl = r * 8 + fc
                        pj = ppf[fc % 2]
                        if gl >= 2 or r > 0:
                            P.wait("tensor", c5, c5.n - 1)
                        pTR.n += 1
                        P.on("tensor", lambda t, fc=fc, pj=pj: t.transpose(
                            out=pj.bitcast(BF16)[:, 0:128], in_=hrow[:, fc * 128:(fc + 1) * 128],
                            identity=idb_sb[:]).then_inc(pTR.h, 1))
                        P.wait("scalar", pTR, pTR.n)
                        c5.n += 1
                        P.on("scalar", lambda s, pj=pj, fc=fc, r=r: s.activation(
                            out=hT[:, fc * 512 + r * 128: fc * 512 + r * 128 + 128],
                            in_=pj.bitcast(BF16)[:, 0:128], func=AF.Identity,
                            scale=bcols[:, BC["f_g"] + fc: BC["f_g"] + fc + 1],
                            bias=bcols[:, BC["f_b"] + fc: BC["f_b"] + fc + 1]).then_inc(c5.h, 1))
                        hT_ptr[gr] = pTR.n
                P.wait("tensor", c5, c5.n)
                for oc in range(32):
                    wb = w1sb[oc % 2]
                    g1 = rc * 32 + oc
                    if g1 >= 2:
                        rcp, ocp = divmod(g1 - 2, 32)
                        P.wait("sync", eEV, FB["eEV"] + rcp * 40 + ocp + 1)
                    w1dma[(rc, oc)] = _dma(P, "sync", wb[:],
                                           w1ds[oc].rearrange("p c n -> p (c n)"),
                                           q=DW[oc % 2])
                    pj = ppf[oc % 2]
                    P.wait("tensor", DW[oc % 2], w1dma[(rc, oc)])
                    if oc >= 2:
                        P.wait("tensor", eEV, FB["eEV"] + rc * 40 + oc - 1)
                    for kc in range(8):
                        if kc == 7:
                            eST.n += 1
                        P.on("tensor", lambda t, wb=wb, kc=kc, pj=pj: (
                            t.matmul(out=pj[:], lhsT=wb[:, kc * 128:(kc + 1) * 128],
                                     rhs=hT[:, kc * 512:(kc + 1) * 512],
                                     start=(kc == 0), stop=(kc == 7)).then_inc(eST.h, 1)
                            if kc == 7 else
                            t.matmul(out=pj[:], lhsT=wb[:, kc * 128:(kc + 1) * 128],
                                     rhs=hT[:, kc * 512:(kc + 1) * 512],
                                     start=(kc == 0), stop=(kc == 7))))
                    P.wait("scalar", eST, eST.n)
                    eEV.n += 1
                    P.on("scalar", lambda s, pj=pj, oc=oc: s.activation(
                        out=y1T[:, oc * 512:(oc + 1) * 512], in_=pj[:], func=AF.Silu,
                        bias=fb1c[:, oc:oc + 1]).then_inc(eEV.h, 1))
                for oc in range(8):
                    wb = w2sb[oc % 2]
                    g2 = rc * 8 + oc
                    if g2 >= 2:
                        rcp, ocp = divmod(g2 - 2, 8)
                        P.wait("sync", eEV, FB["eEV"] + rcp * 40 + 32 + ocp + 1)
                    w2dma[(rc, oc)] = _dma(P, "sync", wb[:],
                                           w2ds[oc].rearrange("p c n -> p (c n)"),
                                           q=DW[oc % 2])
                    pj = ppf[oc % 2]
                    P.wait("tensor", DW[oc % 2], w2dma[(rc, oc)])
                    if oc >= 2:
                        P.wait("tensor", eEV, FB["eEV"] + rc * 40 + 32 + oc - 1)
                    else:
                        P.wait("tensor", eEV, FB["eEV"] + rc * 40 + 32)
                    for kc in range(32):
                        if kc == 31:
                            eST.n += 1
                        P.on("tensor", lambda t, wb=wb, kc=kc, pj=pj: (
                            t.matmul(out=pj[:], lhsT=wb[:, kc * 128:(kc + 1) * 128],
                                     rhs=y1T[:, kc * 512:(kc + 1) * 512],
                                     start=(kc == 0), stop=(kc == 31)).then_inc(eST.h, 1)
                            if kc == 31 else
                            t.matmul(out=pj[:], lhsT=wb[:, kc * 128:(kc + 1) * 128],
                                     rhs=y1T[:, kc * 512:(kc + 1) * 512],
                                     start=(kc == 0), stop=(kc == 31))))
                    P.wait("scalar", eST, eST.n)
                    eEV.n += 1
                    P.on("scalar", lambda s, pj=pj, oc=oc: s.activation(
                        out=zt[:, oc * 512:(oc + 1) * 512], in_=pj[:], func=AF.Identity,
                        scale=0.5, bias=fb2c[:, oc:oc + 1]).then_inc(eEV.h, 1))
                P.wait("tensor", eEV, eEV.n)
                for r in range(4):
                    rt = rc * 4 + r
                    for fc in range(8):
                        pj = ppf[fc % 2]
                        P.wait("tensor", c0, c0.n - 1)
                        pTR.n += 1
                        P.on("tensor", lambda t, fc=fc, r=r, pj=pj: t.transpose(
                            out=pj[:, 0:128], in_=zt[:, fc * 512 + r * 128: fc * 512 + r * 128 + 128],
                            identity=idf_sb[:]).then_inc(pTR.h, 1))
                        P.wait("vector", pTR, pTR.n)
                        c0.n += 1
                        P.on("vector", lambda v, fc=fc, rt=rt, pj=pj: v.tensor_tensor(
                            out=xres[:, rt * E + fc * 128: rt * E + fc * 128 + 128],
                            in0=xres[:, rt * E + fc * 128: rt * E + fc * 128 + 128],
                            in1=pj[:, 0:128], op=OP.add).then_inc(c0.h, 1))
            barrier(P)
            P.emit()

    def emit_attnb():
        c_outer = ExitStack()
        enO = c_outer.enter_context
        c12 = ExitStack()
        yT = c12.enter_context(nc.sbuf_tensor("yTC", [128, 8 * ROWS], BF16))

        with ExitStack() as st:
            en = st.enter_context
            ybf = en(nc.sbuf_tensor("ybfC", [128, E], F32))
            sqbC = en(nc.sbuf_tensor("sqbC", [128, E], F32))
            ylb = [en(nc.sbuf_tensor(f"ylbC{i}", [128, E], BF16)) for i in range(2)]
            mcol = en(nc.sbuf_tensor("mcolC", [128, 4], F32))
            ptc = [en(nc.psum_tensor(f"ptcC{i}", [128, 512], F32)) for i in range(2)]

            P = Prog(nc)
            c0, c2, c3, c4, c5 = (sm[x] for x in ("c0", "c2", "c3", "c4", "c5"))
            pTR = sm["pTR"]
            C1 = {x: sm[x].n for x in ("pTR", "c5")}
            for rt in range(NRT):
                xsl = xres[:, rt * E:(rt + 1) * E]
                yl = ylb[rt % 2]
                if rt >= 2:
                    P.wait("vector", pTR, C1["pTR"] + (rt - 1) * 8)
                emit_ln(P, mcol, xsl, ybf[:], sqbC[:], c2, c3, c4)
                c0.n += 1
                P.on("vector", lambda v, yl=yl: v.tensor_scalar(
                    out=yl[:], in0=ybf[:], scalar1=mcol[:, 3:4], scalar2=None,
                    op0=OP.mult).then_inc(c0.h, 1))
                P.wait("tensor", c0, c0.n)
                for fc in range(8):
                    gt = rt * 8 + fc
                    pj = ptc[gt % 2]
                    if gt >= 2:
                        P.wait("tensor", c5, C1["c5"] + gt - 1)
                    pTR.n += 1
                    P.on("tensor", lambda t, yl=yl, fc=fc, pj=pj: t.transpose(
                        out=pj.bitcast(BF16)[:, 0:128], in_=yl[:, fc * 128:(fc + 1) * 128],
                        identity=idb_sb[:]).then_inc(pTR.h, 1))
                    P.wait("vector", pTR, pTR.n)
                    c5.n += 1
                    P.on("vector", lambda v, pj=pj, fc=fc, rt=rt: v.tensor_copy(
                        out=yT[:, fc * ROWS + rt * 128: fc * ROWS + rt * 128 + 128],
                        in_=pj.bitcast(BF16)[:, 0:128]).then_inc(c5.h, 1))
            barrier(P)
            P.emit()

        with ExitStack() as st:
            en = st.enter_context
            rowb = en(nc.sbuf_tensor("rowbC", [128, NRT * E], BF16))
            wcb = en(nc.sbuf_tensor("wcbC", [128, 8 * E], BF16))
            stgc = [en(nc.sbuf_tensor(f"stgC{i}", [128, 512], BF16)) for i in range(2)]
            res2 = [en(nc.sbuf_tensor(f"resC{i}", [128, 512], BF16)) for i in range(2)]
            ppc = [en(nc.psum_tensor(f"ppcC{i}", [128, 512], F32)) for i in range(2)]
            ptc2 = [en(nc.psum_tensor(f"ptc2C{i}", [128, 512], F32)) for i in range(2)]

            P = Prog(nc)
            c0, c2, c5 = sm["c0"], sm["c2"], sm["c5"]
            eST, eEV, pTR, pPJ = sm["eST"], sm["eEV"], sm["pTR"], sm["pPJ"]
            C2 = {x: sm[x].n for x in ("eST", "eEV", "pTR", "c0", "c2", "c5", "pPJ")}
            WLIST = [("q", wq_b, 0, None), ("k0", wk_b0, 1, ksr), ("k1", wk_b1, 2, ksr),
                     ("v0", wv_b0, 3, vsr), ("v1", wv_b1, 4, vsr)]
            nev = 0
            ntr = 0
            res2_cons = {}
            for wi, (wn, wsrc, bidx, dstt) in enumerate(WLIST):
                if wi > 0:
                    P.wait("gpsimd", eST, eST.n)
                _dma(P, "gpsimd", wcb[:], wsrc.rearrange("(c p) n -> p c n", p=128))
                P.wait("tensor", dG, dG.n)
                for dp in range(8):
                    for rc in range(4):
                        jj = nev
                        pj = ppc[jj % 2]
                        sgt = stgc[jj % 2]
                        rs2 = res2[jj % 2]
                        if jj >= 2:
                            P.wait("tensor", eEV, C2["eEV"] + jj - 1)
                        for fc in range(8):
                            if fc == 7:
                                eST.n += 1
                            P.on("tensor", lambda t, fc=fc, dp=dp, rc=rc, pj=pj: (
                                t.matmul(out=pj[:],
                                         lhsT=wcb[:, fc * E + dp * 128: fc * E + dp * 128 + 128],
                                         rhs=yT[:, fc * ROWS + rc * 512: fc * ROWS + rc * 512 + 512],
                                         start=(fc == 0), stop=(fc == 7)).then_inc(eST.h, 1)
                                if fc == 7 else
                                t.matmul(out=pj[:],
                                         lhsT=wcb[:, fc * E + dp * 128: fc * E + dp * 128 + 128],
                                         rhs=yT[:, fc * ROWS + rc * 512: fc * ROWS + rc * 512 + 512],
                                         start=(fc == 0), stop=(fc == 7))))
                        P.wait("scalar", eST, eST.n)
                        if jj >= 2:
                            P.wait("scalar", c2, C2["c2"] + jj - 1)
                        eEV.n += 1
                        P.on("scalar", lambda s, pj=pj, sgt=sgt, bidx=bidx, dp=dp: s.activation(
                            out=sgt[:], in_=pj[:], func=AF.Identity,
                            bias=bkvc[:, bidx * 8 + dp: bidx * 8 + dp + 1]).then_inc(eEV.h, 1))
                        P.wait("vector", eEV, eEV.n)
                        if jj - 2 in res2_cons:
                            kind, cnt = res2_cons[jj - 2]
                            if kind == "ptr":
                                P.wait("vector", pTR, cnt)
                            else:
                                P.wait("vector", DS4[0], cnt)
                        c2.n += 1
                        P.on("vector", lambda v, sgt=sgt, rs2=rs2, dp=dp, rc=rc: v.tensor_tensor(
                            out=rs2[:], in0=sgt[:],
                            in1=yT[:, dp * ROWS + rc * 512: dp * ROWS + rc * 512 + 512],
                            op=OP.add).then_inc(c2.h, 1))
                        if wn == "q":
                            P.wait("sync", c2, c2.n)
                            _dma(P, "sync", qtb[2 * dp, :, rc * 512:(rc + 1) * 512], rs2[0:64, :],
                                 q=DS4[0])
                            res2_cons[jj] = ("dma", _dma(
                                P, "sync", qtb[2 * dp + 1, :, rc * 512:(rc + 1) * 512],
                                rs2[64:128, :], q=DS4[0]))
                        else:
                            P.wait("tensor", c2, c2.n)
                            for tt in range(4):
                                gt = ntr
                                pj2 = ptc2[gt % 2]
                                if gt >= 2:
                                    P.wait("tensor", c5, C2["c5"] + gt - 1)
                                pTR.n += 1
                                P.on("tensor", lambda t, rs2=rs2, tt=tt, pj2=pj2: t.transpose(
                                    out=pj2.bitcast(BF16)[:, 0:128],
                                    in_=rs2[:, tt * 128:(tt + 1) * 128],
                                    identity=idb_sb[:]).then_inc(pTR.h, 1))
                                P.wait("vector", pTR, pTR.n)
                                c5.n += 1
                                P.on("vector", lambda v, pj2=pj2, rc=rc, tt=tt, dp=dp: v.tensor_copy(
                                    out=rowb[:, (rc * 4 + tt) * E + dp * 128: (rc * 4 + tt) * E + dp * 128 + 128],
                                    in_=pj2.bitcast(BF16)[:, 0:128]).then_inc(c5.h, 1))
                                ntr += 1
                            res2_cons[jj] = ("ptr", pTR.n)
                        nev += 1
                if wn != "q":
                    k = int(wn[1])
                    P.wait("sync", c5, c5.n)
                    nrb = _dma(P, "sync", dstt[k].rearrange("(t p) d -> p t d", p=128),
                               rowb[:].rearrange("p (t d) -> p t d", d=E), q=DRB)
                    P.wait("vector", DRB, nrb)
            barrier(P)
            P.emit()

        c12.close()

        outT = enO(nc.sbuf_tensor("outTC", [128, 8 * ROWS], F32))
        with ExitStack() as st:
            en = st.enter_context
            krb1 = en(nc.sbuf_tensor("krbC", [128, 16 * E], BF16))
            vrb1 = en(nc.sbuf_tensor("vrbC", [128, 16 * E], BF16))
            krb = [krb1, krb1]
            vrb = [vrb1, vrb1]
            s01 = en(nc.sbuf_tensor("s01C", [128, E], F32))
            s1tmp = en(nc.sbuf_tensor("s1tmpC", [64, E], F32))
            stb = s1tmp[:].bitcast(BF16)[:, 0:E]
            qth1 = en(nc.sbuf_tensor("qthC0", [64, ROWS], BF16))
            qth = [qth1, qth1]
            stk = [en(nc.psum_tensor(f"stkC{i}", [64, 1024], F32)) for i in range(2)]
            pqc = [en(nc.psum_tensor(f"pqcC{i}", [64, 512], F32)) for i in range(2)]

            P = Prog(nc)
            c0, c2, c5 = sm["c0"], sm["c2"], sm["c5"]
            eST, eEV = sm["eST"], sm["eEV"]
            cc = sm["cAC"]
            C3 = {x: sm[x].n for x in ("eST", "c2", "c0")}
            for k in range(2):
                if k == 1:
                    P.wait("sync", eST, eST.n)
                _dma(P, "sync", krb[k][:].rearrange("p (t d) -> p t d", d=E),
                     ksr[k].rearrange("(t p) d -> p t d", p=128), q=DN[k])
                dtot = _dma(P, "sync", vrb[k][:].rearrange("p (t d) -> p t d", d=E),
                            vsr[k].rearrange("(t p) d -> p t d", p=128), q=DN[k])
                P.wait("tensor", DN[k], dtot)
                for h in range(H):
                    for ns in range(16):
                        if ns == 15:
                            eST.n += 1
                        P.on("tensor", lambda t, k=k, h=h, ns=ns: (
                            t.matmul(out=stk[k][:, h * 64:(h + 1) * 64],
                                     lhsT=krb[k][:, ns * E + h * 64: ns * E + h * 64 + 64],
                                     rhs=vrb[k][:, ns * E + h * 64: ns * E + h * 64 + 64],
                                     start=(ns == 0), stop=(ns == 15)).then_inc(eST.h, 1)
                            if ns == 15 else
                            t.matmul(out=stk[k][:, h * 64:(h + 1) * 64],
                                     lhsT=krb[k][:, ns * E + h * 64: ns * E + h * 64 + 64],
                                     rhs=vrb[k][:, ns * E + h * 64: ns * E + h * 64 + 64],
                                     start=(ns == 0), stop=(ns == 15))))
            P.wait("vector", eST, eST.n)
            c2.n += 2
            P.on("vector", lambda v: v.tensor_copy(out=s01[0:64, :], in_=stk[0][:]).then_inc(sm["c2"].h, 1))
            P.on("vector", lambda v: v.tensor_copy(out=s01[64:128, :], in_=stk[1][:]).then_inc(sm["c2"].h, 1))
            P.wait("gpsimd", c2, c2.n)
            _dma(P, "gpsimd", cin_t[:, :], s01[:])
            P.wait("gpsimd", dG, dG.n)
            cc.n += 1
            P.on("gpsimd", lambda g: g.collective_compute(
                "AllReduce", OP.add, replica_groups=[[0, 1, 2, 3], [4, 5, 6, 7]],
                ins=[cin_t.ap().opt()], outs=[cout_t.ap().opt()]).then_inc(cc.h, 1))
            P.wait("gpsimd", cc, cc.n)
            _dma(P, "gpsimd", s01[:], cout_t[:, :])
            P.wait("vector", dG, dG.n)
            c2.n += 1
            P.on("vector", lambda v: v.tensor_copy(out=s1tmp[:], in_=s01[64:128, :]
                                                   ).then_inc(c2.h, 1))
            P.wait("vector", c2, c2.n)
            c2.n += 1
            P.on("vector", lambda v: v.tensor_tensor(out=s01[0:64, :], in0=s01[0:64, :],
                                                     in1=s1tmp[:], op=OP.mult
                                                     ).then_inc(c2.h, 1))
            P.wait("vector", c2, c2.n)
            c2.n += 1
            P.on("vector", lambda v: v.tensor_scalar(out=stb, in0=s01[0:64, :],
                                                     scalar1=0.5 ** 24, scalar2=None,
                                                     op0=OP.mult).then_inc(c2.h, 1))
            stb_done = c2.n
            for h in range(H):
                if h >= 1:
                    P.wait("sync", c0, c0.n)
                dtot = _dma(P, "sync", qth[h % 2][:], qtb[h], q=DQ[h % 2])
                P.wait("tensor", DQ[h % 2], dtot)
                P.wait("tensor", c2, stb_done)
                for rc in range(4):
                    pj = pqc[rc % 2]
                    if h * 4 + rc >= 2:
                        P.wait("tensor", c0, C3["c0"] + h * 4 + rc - 1)
                    eST.n += 1
                    P.on("tensor", lambda t, h=h, rc=rc, pj=pj: t.matmul(
                        out=pj[:], lhsT=stb[:, h * 64:(h + 1) * 64],
                        rhs=qth[h % 2][:, rc * 512:(rc + 1) * 512],
                        start=True, stop=True).then_inc(eST.h, 1))
                    P.wait("vector", eST, eST.n)
                    c0.n += 1
                    P.on("vector", lambda v, h=h, rc=rc, pj=pj: v.tensor_copy(
                        out=outT[(h % 2) * 64:(h % 2) * 64 + 64,
                                 (h // 2) * ROWS + rc * 512:(h // 2) * ROWS + rc * 512 + 512],
                        in_=pj[:]).then_inc(c0.h, 1))
            barrier(P)
            P.emit()

        with ExitStack() as st:
            en = st.enter_context
            orow = en(nc.sbuf_tensor("orowC", [128, E], F32))
            sqbC2 = en(nc.sbuf_tensor("sqbC2", [128, E], F32))
            xh = [en(nc.sbuf_tensor(f"xhC{i}", [128, E], BF16)) for i in range(2)]
            wob = en(nc.sbuf_tensor("wobC", [128, 8 * E], BF16))
            ostg = [en(nc.sbuf_tensor(f"ostgC{i}", [128, 512], F32)) for i in range(2)]
            mcol = en(nc.sbuf_tensor("mcolC7", [128, 4], F32))
            xhT = en(nc.sbuf_tensor("xhTC", [128, 8 * ROWS], BF16))
            pt7 = [en(nc.psum_tensor(f"pt7C{i}", [128, 512], F32)) for i in range(2)]
            px7 = [en(nc.psum_tensor(f"px7C{i}", [128, 512], F32)) for i in range(2)]
            pp7 = [en(nc.psum_tensor(f"pp7C{i}", [128, 512], F32)) for i in range(2)]

            P = Prog(nc)
            c0, c2, c3, c4, c5 = (sm[x] for x in ("c0", "c2", "c3", "c4", "c5"))
            eST, eEV, pTR = sm["eST"], sm["eEV"], sm["pTR"]
            C7 = {x: sm[x].n for x in ("eST", "eEV", "pTR", "c0", "c2", "c5")}
            _dma(P, "gpsimd", wob[:].rearrange("p (c n) -> p c n", n=E), wo_b.rearrange("(c p) n -> p c n", p=128))
            ntr = 0
            evac_c2 = []
            nor = 0
            for rt in range(NRT):
                for fc in range(8):
                    pj = pt7[nor % 2]
                    if nor >= 2:
                        P.wait("tensor", c2, evac_c2[nor - 2])
                    pTR.n += 1
                    P.on("tensor", lambda t, fc=fc, rt=rt, pj=pj: t.transpose(
                        out=pj[:, 0:128],
                        in_=outT[:, fc * ROWS + rt * 128: fc * ROWS + rt * 128 + 128],
                        identity=idf_sb[:]).then_inc(pTR.h, 1))
                    P.wait("vector", pTR, pTR.n)
                    c2.n += 1
                    evac_c2.append(c2.n)
                    P.on("vector", lambda v, fc=fc, pj=pj: v.tensor_copy(
                        out=orow[:, fc * 128:(fc + 1) * 128], in_=pj[:, 0:128]).then_inc(c2.h, 1))
                    nor += 1
                xhl = xh[rt % 2]
                if rt >= 2:
                    P.wait("vector", pTR, C7["pTR"] + (rt - 1) * 16)
                emit_ln(P, mcol, orow[:], orow[:], sqbC2[:], c2, c3, c4)
                c0.n += 1
                P.on("vector", lambda v, xhl=xhl: v.tensor_scalar(
                    out=xhl[:], in0=orow[:], scalar1=mcol[:, 3:4], scalar2=None,
                    op0=OP.mult).then_inc(c0.h, 1))
                P.wait("tensor", c0, c0.n)
                for fc in range(8):
                    gx = rt * 8 + fc
                    pj = px7[gx % 2]
                    if gx >= 2:
                        P.wait("tensor", c5, C7["c5"] + gx - 1)
                    pTR.n += 1
                    P.on("tensor", lambda t, xhl=xhl, fc=fc, pj=pj: t.transpose(
                        out=pj.bitcast(BF16)[:, 0:128], in_=xhl[:, fc * 128:(fc + 1) * 128],
                        identity=idb_sb[:]).then_inc(pTR.h, 1))
                    P.wait("scalar", pTR, pTR.n)
                    c5.n += 1
                    P.on("scalar", lambda s, pj=pj, fc=fc, rt=rt: s.activation(
                        out=xhT[:, fc * ROWS + rt * 128: fc * ROWS + rt * 128 + 128],
                        in_=pj.bitcast(BF16)[:, 0:128], func=AF.Identity,
                        scale=bcols[:, BC["lng"] + fc: BC["lng"] + fc + 1],
                        bias=bcols[:, BC["lnb"] + fc: BC["lnb"] + fc + 1]).then_inc(c5.h, 1))
                    ntr += 1
            P.wait("tensor", c5, c5.n)
            P.wait("tensor", c2, c2.n)
            P.wait("tensor", dG, dG.n)
            nj = 0
            for dp in range(8):
                for rc in range(4):
                    pj = pp7[nj % 2]
                    if nj >= 2:
                        P.wait("tensor", eEV, C7["eEV"] + nj - 1)
                    for fc in range(8):
                        if fc == 7:
                            eST.n += 1
                        P.on("tensor", lambda t, fc=fc, dp=dp, rc=rc, pj=pj: (
                            t.matmul(out=pj[:],
                                     lhsT=wob[:, fc * E + dp * 128: fc * E + dp * 128 + 128],
                                     rhs=xhT[:, fc * ROWS + rc * 512: fc * ROWS + rc * 512 + 512],
                                     start=(fc == 0), stop=(fc == 7)).then_inc(eST.h, 1)
                            if fc == 7 else
                            t.matmul(out=pj[:],
                                     lhsT=wob[:, fc * E + dp * 128: fc * E + dp * 128 + 128],
                                     rhs=xhT[:, fc * ROWS + rc * 512: fc * ROWS + rc * 512 + 512],
                                     start=(fc == 0), stop=(fc == 7))))
                    P.wait("scalar", eST, eST.n)
                    if nj >= 2:
                        P.wait("scalar", c0, C7["c0"] + 16 + (nj - 1) * 4)
                    eEV.n += 1
                    osg = ostg[nj % 2]
                    P.on("scalar", lambda s, pj=pj, dp=dp, osg=osg: s.activation(
                        out=osg[:], in_=pj[:], func=AF.Identity,
                        bias=bcols[:, BC["bo_b"] + dp: BC["bo_b"] + dp + 1]).then_inc(eEV.h, 1))
                    P.wait("tensor", eEV, eEV.n)
                    for tt in range(4):
                        pj2 = pp7[(nj + 1) % 2] if False else pt7[tt % 2]
                        P.wait("tensor", c0, c0.n - 1)
                        pTR.n += 1
                        P.on("tensor", lambda t, osg=osg, tt=tt, pj2=pj2: t.transpose(
                            out=pj2[:, 0:128], in_=osg[:, tt * 128:(tt + 1) * 128],
                            identity=idf_sb[:]).then_inc(pTR.h, 1))
                        P.wait("vector", pTR, pTR.n)
                        rt = rc * 4 + tt
                        c0.n += 1
                        P.on("vector", lambda v, rt=rt, dp=dp, pj2=pj2: v.tensor_tensor(
                            out=xres[:, rt * E + dp * 128: rt * E + dp * 128 + 128],
                            in0=xres[:, rt * E + dp * 128: rt * E + dp * 128 + 128],
                            in1=pj2[:, 0:128], op=OP.add).then_inc(c0.h, 1))
                    nj += 1
            barrier(P)
            P.emit()
        c_outer.close()

    if nphase >= 4:
        emit_dequant()
        emit_ffn("F1")
    if nphase >= 5:
        emit_attnb()
    if nphase >= 6:
        emit_ffn("F2")

    P = Prog(nc)
    if not a4_on:
        c0 = sm["c0"]
        c0.n += 1
        P.on("vector", lambda v: v.memset(xres[:], 0.0).then_inc(c0.h, 1))
        P.wait("sync", c0, c0.n)
    ov = out.rearrange("(t p) d -> p t d", p=128)
    xv = xres[:].rearrange("p (t d) -> p t d", d=E)
    for j in range(4):
        _dma(P, "sync", ov[:, j * 4:(j + 1) * 4, :], xv[:, j * 4:(j + 1) * 4, :])
    dbgsel = os.environ.get("MK_DBG", "")
    if dbgsel:
        dsrc = {
            "ksr0": ksr[0], "ksr1": ksr[1], "vsr0": vsr[0], "vsr1": vsr[1],
            "qtb": qtb.rearrange("a d (m n) -> (a d m) n", n=E),
            "cout": cout_t.ap(),
            "accs": accs,
        }[dbgsel]
        shp = dsrc.shape
        P.wait("gpsimd", dS, dS.n)
        dn = _dma(P, "gpsimd", out[0:shp[0], 0:shp[1]], dsrc)
        P.wait("sync", dG, dn)
    P.wait("sync", dS, dS.n)
    P.emit()

    outer.close()
    return nc


SCALE = 1.0 / math.sqrt(D)


_NC_CACHE = {}
LAST_RESULTS = None


def kernel(L0_attn_a_x, L0_attn_a_Wq, L0_attn_a_bq, L0_attn_a_Wk, L0_attn_a_bk,
           L0_attn_a_Wv, L0_attn_a_bv, L0_attn_a_Wo, L0_attn_a_bo,
           L0_attn_a_phase_modulators, L0_attn_a_interference_weights,
           L0_attn_b_Wq, L0_attn_b_bq, L0_attn_b_Wk, L0_attn_b_bk,
           L0_attn_b_Wv, L0_attn_b_bv, L0_attn_b_Wo, L0_attn_b_bo,
           L0_attn_b_ln_g, L0_attn_b_ln_b,
           L0_ffn_ln_gamma, L0_ffn_ln_beta, L0_ffn_fc1_q, L0_ffn_fc1_scales,
           L0_ffn_fc1_bias, L0_ffn_fc2_q, L0_ffn_fc2_scales, L0_ffn_fc2_bias):
    nphase = int(os.environ.get("MK_NPHASE", "6"))
    x = np.ascontiguousarray(np.asarray(L0_attn_a_x, dtype=np.float32))
    f32 = lambda a: np.ascontiguousarray(np.asarray(a, dtype=np.float32))
    i32 = lambda a: np.ascontiguousarray(np.asarray(a, dtype=np.int32))

    common = dict(
        wq_a=f32(L0_attn_a_Wq), wk_a=f32(L0_attn_a_Wk), wv_a=f32(L0_attn_a_Wv),
        wo_a=f32(L0_attn_a_Wo), bq_a=f32(L0_attn_a_bq), bk_a=f32(L0_attn_a_bk),
        bv_a=f32(L0_attn_a_bv), bo_a=f32(L0_attn_a_bo),
        pmt=np.ascontiguousarray(f32(L0_attn_a_phase_modulators).T),
        iw=f32(L0_attn_a_interference_weights),
        f_g=f32(L0_ffn_ln_gamma), f_b=f32(L0_ffn_ln_beta),
        w1q=i32(L0_ffn_fc1_q), s1=f32(L0_ffn_fc1_scales), fb1=f32(L0_ffn_fc1_bias),
        w2q=i32(L0_ffn_fc2_q), s2=f32(L0_ffn_fc2_scales), fb2=f32(L0_ffn_fc2_bias),
        wq_b=f32(L0_attn_b_Wq), wk_b0=f32(L0_attn_b_Wk[0]), wk_b1=f32(L0_attn_b_Wk[1]),
        wv_b0=f32(L0_attn_b_Wv[0]), wv_b1=f32(L0_attn_b_Wv[1]), wo_b=f32(L0_attn_b_Wo),
        bq_b=f32(L0_attn_b_bq), bk_b0=f32(L0_attn_b_bk[0]), bk_b1=f32(L0_attn_b_bk[1]),
        bv_b0=f32(L0_attn_b_bv[0]), bv_b1=f32(L0_attn_b_bv[1]), bo_b=f32(L0_attn_b_bo),
        lng=f32(L0_attn_b_ln_g), lnb=f32(L0_attn_b_ln_b),
        idf=np.eye(128, dtype=np.float32),
    )

    key = nphase
    if key not in _NC_CACHE:
        _NC_CACHE[key] = build(nphase)
    nc = _NC_CACHE[key]

    in_maps = []
    for c in range(8):
        b = c // 4
        off = (c % 4) * ROWS
        xq = np.roll(x[b], -off, axis=0)
        m = dict(common)
        m["xb"] = np.ascontiguousarray(xq)
        in_maps.append(m)

    res = run_bass_kernel_spmd(nc, in_maps, core_ids=list(range(8)))
    global LAST_RESULTS
    LAST_RESULTS = res.results
    shards = [np.asarray(res.results[c]["out"]) for c in range(8)]
    full = np.concatenate(shards, axis=0).reshape(B, S, E)
    return full.astype(np.float32)
